# version 01 (final)
# baseline (speedup 1.0000x reference)
import contextlib
import numpy as np

import concourse.bass as bass
import concourse.mybir as mybir
from concourse.bass import AP
from concourse.bass_utils import run_bass_kernel_spmd

F32 = mybir.dt.float32
BF16 = mybir.dt.bfloat16
I32 = mybir.dt.int32
AF = mybir.ActivationFunctionType
ALU = mybir.AluOpType

B, T, D = 4, 8192, 1024
TOWN = 4096
H0, HD0 = 8, 128
H0C = 4
H1, HD1 = 16, 64
H1C = 8
NT128 = T // 128
NT512 = T // 512
NO128 = TOWN // 128
NO512 = TOWN // 512
NDEC = 64
LN8 = 2.0794415416798357
EPS = 1e-5


class KB:

    def __init__(self, upto=99):
        self.upto = upto
        self.nc = bass.Bass("TRN2", target_bir_lowering=False)
        self.ctx = contextlib.ExitStack()
        self.debug_dumps = []
        self._sem_n = 0
        self._bar = None
        self._bar_v = 0
        self._declare_io()
        self._declare_scratch()

    def sem(self, name):
        s = self.ctx.enter_context(self.nc.semaphore(f"{name}_{self._sem_n}"))
        self._sem_n += 1
        return s

    def sbuf(self, name, shape, dtype, ctx=None):
        return (ctx or self.ctx).enter_context(self.nc.sbuf_tensor(name, shape, dtype))

    def psum(self, name, shape, dtype=F32, ctx=None):
        return (ctx or self.ctx).enter_context(self.nc.psum_tensor(name, shape, dtype))

    def _declare_io(self):
        nc = self.nc
        def inp(name, shape, dtype=F32):
            return nc.dram_tensor(name, shape, dtype, kind="ExternalInput").ap()
        self.x_full = inp("x_full", [T, D])
        self.x_own = inp("x_own", [TOWN, D])
        self.w_q0 = inp("w_q0", [D, H0C * HD0])
        self.w_k0 = inp("w_k0", [D, H0C * HD0])
        self.w_v0 = inp("w_v0", [D, H0C * HD0])
        self.w_o0 = inp("w_o0", [D, D])
        self.b_o0 = inp("b_o0", [D])
        self.theta = inp("theta", [8])
        self.gn_gamma = inp("gn_gamma", [H0C * HD0])
        self.gn_beta = inp("gn_beta", [H0C * HD0])
        self.f_qw = inp("f_qw", [6, D, D], I32)
        self.f_sc = inp("f_sc", [6, D, 64])
        self.f_bi = inp("f_bi", [6, D])
        self.f_la = inp("f_la", [6, 32, D])
        self.f_lb = inp("f_lb", [6, D, 32])
        self.f_lg = inp("f_lg", [3, D])
        self.f_lbe = inp("f_lbe", [3, D])
        self.w_q1 = inp("w_q1", [H1C * HD1, D])
        self.w_k1 = inp("w_k1", [H1C * HD1, D])
        self.w_v1 = inp("w_v1", [H1C * HD1, D])
        self.w_p1 = inp("w_p1", [D, D])
        self.b_p1 = inp("b_p1", [D])
        self.m_gw = inp("m_gw", [8, D])
        self.m_w1 = inp("m_w1", [9, D, 2048])
        self.m_w2 = inp("m_w2", [9, 2048, D])
        self.m_w3 = inp("m_w3", [9, D, 2048])
        self.out = nc.dram_tensor("out", [TOWN, D], F32, kind="ExternalOutput").ap()

    def _declare_scratch(self):
        nc = self.nc
        def scr(name, shape, dtype):
            return nc.dram_tensor(name, shape, dtype).ap()
        self.xln1_buf = scr("xln1_buf", [T, D], BF16)
        self.q0_buf = scr("q0_buf", [H0C, HD0, T], BF16)
        self.k0_buf = scr("k0_buf", [H0C, HD0, T], BF16)
        self.v0_buf = scr("v0_buf", [T, H0C * HD0], BF16)
        self.dec_buf = scr("dec_buf", [NDEC, 128, 512], BF16)
        self.cc1_in = scr("cc1_in", [H0C * HD0, T], BF16)
        self.cc1_out = scr("cc1_out", [2 * H0C * HD0, T], BF16)
        self.x1_buf = scr("x1_buf", [TOWN, D], F32)
        self.xln2_buf = scr("xln2_buf", [TOWN, D], BF16)
        self.x2_buf = scr("x2_buf", [TOWN, D], F32)
        self.cc2_in = scr("cc2_in", [TOWN, D], BF16)
        self.cc2_out = scr("cc2_out", [T, D], BF16)
        self.xln3_buf = scr("xln3_buf", [T, D], BF16)
        self.qp_buf = scr("qp_buf", [H1C, HD1, T], BF16)
        self.kp_buf = scr("kp_buf", [H1C, HD1, T], BF16)
        self.kpt_buf = scr("kpt_buf", [H1C, T, HD1], BF16)
        self.v1_buf = scr("v1_buf", [T, H1C * HD1], BF16)
        self.cc3_in = scr("cc3_in", [H1C * HD1, T], BF16)
        self.cc3_out = scr("cc3_out", [2 * H1C * HD1, T], BF16)
        self.x3_buf = scr("x3_buf", [TOWN, D], F32)
        self.xln4_buf = scr("xln4_buf", [TOWN, D], BF16)
        self.ct_buf = scr("ct_buf", [8, TOWN], F32)
        self.br_buf = scr("br_buf", [D, TOWN], BF16)

    def dump(self, name, src_dram, shape=None, dtype=None):
        o = self.nc.dram_tensor(f"dbg_{name}", list(shape or src_dram.shape),
                                dtype or src_dram.dtype,
                                kind="ExternalOutput").ap()
        self.debug_dumps.append((name, o, src_dram))

    def run_ops(self, ops, tag):
        def opinc(op):
            if "inc" in op:
                return op["inc"]
            return 16 if (op["eng"] in ("sy", "gp") and op.get("dma", True)) else 1

        sems = {e: self.sem(f"{tag}_{e}") for e in ("t", "v", "s", "sy", "gp")}
        cnt = {e: 0 for e in sems}
        ev = {}
        for op in ops:
            cnt[op["eng"]] += opinc(op)
            op["_tgt"] = cnt[op["eng"]]
            if op.get("name"):
                assert op["name"] not in ev, op["name"]
                ev[op["name"]] = (op["eng"], cnt[op["eng"]])
        blk = self.block
        engmap = {"t": blk.tensor, "v": blk.vector, "s": blk.scalar,
                  "sy": blk.sync, "gp": blk.gpsimd}
        byeng = {}
        for op in ops:
            byeng.setdefault(op["eng"], []).append(op)
        pos = {e: 0 for e in byeng}
        done = {e: 0 for e in byeng}
        progress = True
        while progress:
            progress = False
            for e in byeng:
                while pos[e] < len(byeng[e]):
                    op = byeng[e][pos[e]]
                    ok = True
                    for w in op.get("waits", ()):
                        we, wv = ev[w]
                        if done.get(we, 0) < wv:
                            ok = False
                            break
                    if not ok:
                        break
                    done[e] += opinc(op)
                    pos[e] += 1
                    progress = True
        stuck = {e: (pos[e], len(byeng[e])) for e in byeng if pos[e] < len(byeng[e])}
        if stuck:
            detail = []
            for e in stuck:
                op = byeng[e][pos[e]]
                blocked = [w for w in op.get("waits", ())
                           if done.get(ev[w][0], 0) < ev[w][1]]
                detail.append(f"{e}@{pos[e]}/{len(byeng[e])} op={op.get('name')} "
                              f"blocked_on={blocked}")
            raise RuntimeError(f"run_ops({tag}): DEADLOCK: " + "; ".join(detail))
        for e, elist in byeng.items():
            def section(handle, elist=elist, e=e):
                for op in elist:
                    for w in op.get("waits", ()):
                        we, wv = ev[w]
                        handle.wait_ge(sems[we], wv)
                    op["emit"](handle).then_inc(sems[e], opinc(op))
            engmap[e](section)
        return sems, cnt, ev

    def barrier(self):
        if self._bar is None:
            self._bar = self.sem("bar")
        blk = self.block
        self._bar_v += 5
        tgt = self._bar_v
        bar = self._bar
        @blk.tensor
        def _(e):
            e.nop().then_inc(bar, 1)
            e.wait_ge(bar, tgt)
        @blk.vector
        def _(e):
            e.nop().then_inc(bar, 1)
            e.wait_ge(bar, tgt)
        @blk.scalar
        def _(e):
            e.nop().then_inc(bar, 1)
            e.wait_ge(bar, tgt)
        @blk.gpsimd
        def _(e):
            e.nop().then_inc(bar, 1)
            e.wait_ge(bar, tgt)
        @blk.sync
        def _(e):
            e.nop().then_inc(bar, 1)
            e.wait_ge(bar, tgt)

    def build(self):
        with self.nc.Block() as blk:
            self.block = blk
            self.p0_setup()
            if self.upto >= 1:
                self.p1_ln1()
            if self.upto >= 2:
                self.barrier()
                self.p2_proj0()
            if self.upto >= 3:
                self.barrier()
                self.p3_attn0()
            if self.upto >= 4:
                self.barrier()
                self.p4_wo_res()
            if self.upto >= 5:
                self.barrier()
                self.p5_ffn0()
            if self.upto >= 6:
                self.barrier()
                self.p6_proj1()
            if self.upto >= 7:
                self.barrier()
                self.p7_attn1()
            if self.upto >= 8:
                self.barrier()
                self.p8_wp_res()
            if self.upto >= 9:
                self.barrier()
                self.p9_moe()
            self.barrier()
            self._emit_dumps()
        return self.nc

    def _emit_dumps(self):
        if not self.debug_dumps:
            return
        blk = self.block
        dsem = self.sem("dump")
        dumps = self.debug_dumps
        ccs = [s for s in (getattr(self, "s_cc1", None), getattr(self, "s_cc2", None),
                           getattr(self, "s_cc3", None)) if s is not None]
        @blk.sync
        def _(sy):
            for s in ccs:
                sy.wait_ge(s, 1)
            n = 0
            for name, o, src in dumps:
                sy.dma_start(out=o[tuple(slice(None) for _ in o.shape)],
                             in_=src[tuple(slice(None) for _ in src.shape)]
                             ).then_inc(dsem, 16)
                n += 16
            sy.wait_ge(dsem, n)

    def p0_setup(self):
        k = self
        nc, blk = self.nc, self.block
        self.ones_bf = self.sbuf("ones_bf", [128, 128], BF16)
        self.ones_f = self.sbuf("ones_f", [128, 128], F32)
        self.ident_f = self.sbuf("ident_f", [128, 128], F32)
        self.onesf = self.sbuf("onesf", [1, 128], F32)
        self.ident = self.sbuf("ident", [128, 128], BF16)
        self.causal01 = self.sbuf("causal01", [128, 128], BF16)
        self.eps_t = self.sbuf("eps_t", [128, 1], F32)
        self.negln8 = self.sbuf("negln8", [128, 1], F32)
        self.th8 = self.sbuf("th8", [8, 1], F32)
        self.negth8 = self.sbuf("negth8", [8, 1], F32)
        self.thbc = self.sbuf("thbc", [128, 8], F32)
        self.negthbc = self.sbuf("negthbc", [128, 8], F32)
        self.am_sb = self.sbuf("am_sb", [128, 4 * 512], BF16)

        s_ld = self.sem("p0ld")
        s_gp = self.sem("p0gp")
        s_cm = self.sem("p0cm")
        s_mm = self.sem("p0mm")
        s_st = self.sem("p0st")
        s_th = self.sem("p0th")

        ctx = contextlib.ExitStack()
        d8 = self.sbuf("d8", [128, 8200], BF16, ctx)
        d8r = self.sbuf("d8r", [128, 8200], BF16, ctx)
        iot = self.sbuf("p0_iot", [8, 8200], F32, ctx)
        ntile = self.sbuf("p0_nt", [128, 512], F32, ctx)
        acc = self.sbuf("p0_acc", [128, 512], F32, ctx)
        eh = self.sbuf("p0_eh", [128, 512], F32, ctx)
        zt = self.sbuf("p0_zt", [128, 512], F32, ctx)
        dtile = self.sbuf("p0_dt", [128, 512], BF16, ctx)
        dtile2 = self.sbuf("p0_dt2", [128, 512], BF16, ctx)
        throw = self.sbuf("p0_throw", [1, 8], F32, ctx)
        pdec = self.psum("p0_pdec", [128, 512], F32, ctx)
        pth = self.psum("p0_pth", [128, 8], F32, ctx)

        @blk.sync
        def _(sy):
            sy.dma_start(out=k.th8[:], in_=k.theta.rearrange("(a b) -> a b", b=1)
                         ).then_inc(s_ld, 16)
            sy.dma_start(out=throw[:], in_=k.theta.rearrange("(a b) -> b a", b=1)
                         ).then_inc(s_ld, 16)

        @blk.gpsimd
        def _(gp):
            gp.memset(k.ones_bf[:], 1.0)
            gp.memset(k.ones_f[:], 1.0)
            gp.memset(k.onesf[:], 1.0)
            gp.memset(k.eps_t[:], EPS)
            gp.memset(k.negln8[:], -LN8)
            gp.memset(k.ident[:], 0.0)
            gp.affine_select(out=k.ident[:], in_=k.ident[:], compare_op=ALU.not_equal,
                             fill=1.0, base=0, pattern=[[-1, 128]], channel_multiplier=1)
            gp.memset(k.ident_f[:], 0.0)
            gp.affine_select(out=k.ident_f[:], in_=k.ident_f[:],
                             compare_op=ALU.not_equal, fill=1.0, base=0,
                             pattern=[[-1, 128]], channel_multiplier=1)
            gp.memset(k.causal01[:], 1.0)
            gp.affine_select(out=k.causal01[:], in_=k.causal01[:], compare_op=ALU.is_ge,
                             fill=0.0, base=0, pattern=[[1, 128]], channel_multiplier=-1)
            for i, c in enumerate([0, -128, -256, -384]):
                sl = k.am_sb[:, i * 512:(i + 1) * 512]
                gp.memset(sl, 0.0)
                gp.affine_select(out=sl, in_=sl, compare_op=ALU.is_ge, fill=-1e30,
                                 base=c, pattern=[[1, 512]], channel_multiplier=-1)
            gp.memset(d8[:], 0.0)
            gp.memset(d8r[:], 0.0)
            gp.iota(iot[0:8, :], pattern=[[1, 8200]], base=0, channel_multiplier=0,
                    allow_small_or_imprecise_dtypes=True).then_inc(s_gp, 1)

        @blk.scalar
        def _(sc):
            sc.wait_ge(s_ld, 32)
            sc.wait_ge(s_gp, 1)
            sc.mul(k.negth8[:], k.th8[:], -1.0)
            sc.activation(d8[0:8, :], iot[0:8, :], AF.Exp, bias=0.0,
                          scale=k.negth8[0:8, 0:1]).then_inc(s_cm, 1)

        @blk.gpsimd
        def _(gp):
            gp.wait_ge(s_cm, 1)
            gp.iota(iot[0:8, :], pattern=[[-1, 8200]], base=8199, channel_multiplier=0,
                    allow_small_or_imprecise_dtypes=True).then_inc(s_gp, 1)

        @blk.scalar
        def _(sc):
            sc.wait_ge(s_gp, 2)
            sc.activation(d8r[0:8, :], iot[0:8, :], AF.Exp, bias=k.negln8[0:8, 0:1],
                          scale=k.negth8[0:8, 0:1]).then_inc(s_cm, 1)

        @blk.tensor
        def _(tn):
            tn.wait_ge(s_ld, 32)
            tn.wait_ge(s_gp, 1)
            tn.matmul(pth[:], k.onesf[:], throw[:], start=True, stop=True
                      ).then_inc(s_mm, 1)

        @blk.vector
        def _(v):
            v.wait_ge(s_mm, 1)
            v.tensor_copy(k.thbc[:], pth[:])
            v.tensor_scalar_mul(k.negthbc[:], pth[:], -1.0).then_inc(s_th, 1)

        M0 = 8199
        offc = list(range(128, 7681, 128))

        @blk.tensor
        def _(tn):
            for i, c in enumerate(offc):
                tn.wait_ge(s_cm, 2 + i)
                tn.matmul(pdec[:], d8r[:, M0 - c:M0 - c + 128], d8[:, 0:512],
                          start=True, stop=True).then_inc(s_mm, 1)

        @blk.scalar
        def _(sc):
            for i in range(len(offc)):
                sc.wait_ge(s_mm, 2 + i)
                sc.wait_ge(s_st, i * 16)
                sc.copy(dtile[:], pdec[:]).then_inc(s_cm, 1)

        @blk.sync
        def _(sy):
            for i, c in enumerate(offc):
                sy.wait_ge(s_cm, 3 + i)
                sy.dma_start(out=k.dec_buf[(c + 384) // 128, :, :], in_=dtile[:]
                             ).then_inc(s_st, 16)

        chain = self.sem("p0ch")
        chn = [0]

        def cstep(instr, n=1):
            instr.then_inc(chain, n)
            chn[0] += n

        for i, c in enumerate([0, -128, -256, -384]):
            @blk.gpsimd
            def _(gp, c=c, n0=chn[0]):
                gp.wait_ge(chain, n0)
                cstep(gp.iota(ntile[:], pattern=[[1, 512]], base=c,
                              channel_multiplier=-1,
                              allow_small_or_imprecise_dtypes=True))
            for h in range(8):
                @blk.vector
                def _(v, h=h, n0=chn[0]):
                    v.wait_ge(chain, n0)
                    v.wait_ge(s_th, 1)
                    cstep(v.tensor_scalar(zt[:], ntile[:], k.negthbc[:, h:h + 1],
                                          0.0, ALU.mult, ALU.min))
                @blk.scalar
                def _(sc, n0=chn[0]):
                    sc.wait_ge(chain, n0)
                    cstep(sc.activation(eh[:], zt[:], AF.Exp,
                                        bias=k.negln8[:, 0:1], scale=1.0))
                @blk.vector
                def _(v, h=h, n0=chn[0]):
                    v.wait_ge(chain, n0)
                    if h == 0:
                        cstep(v.tensor_copy(acc[:], eh[:]))
                    else:
                        cstep(v.tensor_add(acc[:], acc[:], eh[:]))
            @blk.gpsimd
            def _(gp, c=c, n0=chn[0]):
                gp.wait_ge(chain, n0)
                cstep(gp.affine_select(out=acc[:], in_=acc[:], compare_op=ALU.is_ge,
                                       fill=0.0, base=c, pattern=[[1, 512]],
                                       channel_multiplier=-1))
            @blk.vector
            def _(v, n0=chn[0]):
                v.wait_ge(chain, n0)
                cstep(v.tensor_copy(dtile2[:], acc[:]))
            @blk.sync
            def _(sy, c=c, n0=chn[0]):
                sy.wait_ge(chain, n0)
                cstep(sy.dma_start(out=k.dec_buf[(c + 384) // 128, :, :],
                                   in_=dtile2[:]), 16)

        if getattr(self, "dbg_p0", False):
            dbg_d8 = nc.dram_tensor("dbg_d8_scr", [128, 8200], BF16).ap()
            dbg_d8r = nc.dram_tensor("dbg_d8r_scr", [128, 8200], BF16).ap()
            self._d8_scr, self._d8r_scr = dbg_d8, dbg_d8r
            @blk.sync
            def _(sy):
                sy.wait_ge(s_cm, 3)
                sy.dma_start(out=dbg_d8[:, :], in_=d8[:]).then_inc(s_st, 16)
                sy.dma_start(out=dbg_d8r[:, :], in_=d8r[:]).then_inc(s_st, 16)

        @blk.sync
        def _(sy, nst=len(offc) * 16 + (32 if getattr(self, "dbg_p0", False) else 0),
              nch=chn[0]):
            sy.wait_ge(s_st, nst)
            sy.wait_ge(chain, nch)
        ctx.close()

    def p1_ln1(self):
        self._ln_pass(self.x_full, T, out_bf=self.xln1_buf, tag="p1")

    def _ln_pass(self, x_src, nrows, out_bf, tag, br_src=None, x_out=None,
                 src_dt=F32):
        k = self
        blk = self.block
        ctx = contextlib.ExitStack()
        nt = nrows // 128
        xin = [self.sbuf(f"{tag}_xin{j}", [128, D], src_dt, ctx) for j in range(2)]
        xrs = [self.sbuf(f"{tag}_xrs{j}", [128, D], F32, ctx) for j in range(2)]
        brt = [self.sbuf(f"{tag}_brt{j}", [128, D], BF16, ctx) for j in range(2)]
        xsq = [self.sbuf(f"{tag}_xsq{j}", [128, D], BF16, ctx) for j in range(2)]
        xout = [self.sbuf(f"{tag}_xo{j}", [128, D], BF16, ctx) for j in range(2)]
        s1 = [self.sbuf(f"{tag}_s1{j}", [128, 1], F32, ctx) for j in range(2)]
        s2 = [self.sbuf(f"{tag}_s2{j}", [128, 1], F32, ctx) for j in range(2)]
        mu = [self.sbuf(f"{tag}_mu{j}", [128, 1], F32, ctx) for j in range(2)]
        t0_ = [self.sbuf(f"{tag}_t0{j}", [128, 1], F32, ctx) for j in range(2)]
        var = [self.sbuf(f"{tag}_va{j}", [128, 1], F32, ctx) for j in range(2)]
        lnv = [self.sbuf(f"{tag}_ln{j}", [128, 1], F32, ctx) for j in range(2)]
        rstd = [self.sbuf(f"{tag}_rs{j}", [128, 1], F32, ctx) for j in range(2)]
        s_ld = self.sem(f"{tag}ld")
        s_sq = self.sem(f"{tag}sq")
        s_dv = self.sem(f"{tag}dv")
        s_rs = self.sem(f"{tag}rs")
        s_ap = self.sem(f"{tag}ap")
        s_sd = self.sem(f"{tag}sd")
        s_ra = self.sem(f"{tag}ra")
        has_res = br_src is not None
        dma_per_tile = 2 if has_res else 1
        st_per_tile = (1 if out_bf is not None else 0) + (1 if x_out is not None else 0)

        @blk.sync
        def _(sy):
            for i in range(nt):
                if i >= 2:
                    sy.wait_ge(s_ap, i - 1)
                sy.dma_start(out=xin[i % 2][:], in_=x_src[i * 128:(i + 1) * 128, :]
                             ).then_inc(s_ld, 16)
                if has_res:
                    sy.dma_start(out=brt[i % 2][:],
                                 in_=br_src[:, i * 128:(i + 1) * 128],
                                 transpose=True).then_inc(s_ld, 16)
                if i > 0:
                    self._ln_store(sy, i - 1, out_bf, x_out, xout, xrs, s_ap, s_sd,
                                   st_per_tile)
            self._ln_store(sy, nt - 1, out_bf, x_out, xout, xrs, s_ap, s_sd,
                           st_per_tile)
            sy.wait_ge(s_sd, st_per_tile * 16 * nt)

        @blk.vector
        def _(v):
            for i in range(nt):
                j = i % 2
                v.wait_ge(s_ld, dma_per_tile * 16 * (i + 1))
                if i >= 2 and st_per_tile:
                    v.wait_ge(s_sd, st_per_tile * 16 * (i - 1))
                if has_res:
                    v.tensor_add(xrs[j][:], xin[j][:], brt[j][:]).then_inc(s_ra, 1)
                    src = xrs[j]
                else:
                    src = xin[j]
                if has_res:
                    v.wait_ge(s_ra, i + 1)
                v.tensor_reduce(s1[j][:], src[:], mybir.AxisListType.X, ALU.add
                                ).then_inc(s_dv, 1)
                v.wait_ge(s_sq, i + 1)
                v.tensor_reduce(s2[j][:], xsq[j][:], mybir.AxisListType.X, ALU.add)
                if i >= 2:
                    v.wait_ge(s_rs, 2 * i - 2)
                v.wait_ge(s_dv, 3 * i + 1)
                v.tensor_scalar(mu[j][:], s1[j][:], 1.0 / D, None, ALU.mult)
                v.tensor_scalar(t0_[j][:], s1[j][:], 1.0 / (1.0 * D * D), s1[j][:],
                                ALU.mult, ALU.mult).then_inc(s_dv, 1)
                v.wait_ge(s_dv, 3 * i + 2)
                v.scalar_tensor_tensor(var[j][:], s2[j][:], 1.0 / D, t0_[j][:],
                                       ALU.mult, ALU.subtract
                                       ).then_inc(s_dv, 1)
                v.wait_ge(s_rs, 2 * i + 2)
                v.tensor_scalar(xout[j][:], src[:], mu[j][:], rstd[j][:],
                                ALU.subtract, ALU.mult).then_inc(s_ap, 1)

        @blk.scalar
        def _(sc):
            for i in range(nt):
                j = i % 2
                if has_res:
                    sc.wait_ge(s_ra, i + 1)
                else:
                    sc.wait_ge(s_ld, dma_per_tile * 16 * (i + 1))
                if i >= 2:
                    sc.wait_ge(s_dv, 3 * (i - 2) + 2)
                src = xrs[j] if has_res else xin[j]
                sc.square(xsq[j][:], src[:]).then_inc(s_sq, 1)
                sc.wait_ge(s_dv, 3 * i + 3)
                if i >= 2:
                    sc.wait_ge(s_ap, i - 1)
                sc.activation(lnv[j][:], var[j][:], AF.Ln, bias=k.eps_t[:, 0:1],
                              scale=1.0).then_inc(s_rs, 1)
                sc.wait_ge(s_rs, 2 * i + 1)
                sc.activation(rstd[j][:], lnv[j][:], AF.Exp, bias=0.0,
                              scale=-0.5).then_inc(s_rs, 1)
        ctx.close()

    def _ln_store(self, sy, i, out_bf, x_out, xout, xrs, s_ap, s_sd, st_per_tile):
        sy.wait_ge(s_ap, i + 1)
        if out_bf is not None:
            sy.dma_start(out=out_bf[i * 128:(i + 1) * 128, :], in_=xout[i % 2][:]
                         ).then_inc(s_sd, 16)
        if x_out is not None:
            sy.dma_start(out=x_out[i * 128:(i + 1) * 128, :], in_=xrs[i % 2][:]
                         ).then_inc(s_sd, 16)

    def p0_setup(self):
        k = self
        nc, blk = self.nc, self.block
        self.ones_bf = self.sbuf("ones_bf", [128, 128], BF16)
        self.ones_f = self.sbuf("ones_f", [128, 128], F32)
        self.ident_f = self.sbuf("ident_f", [128, 128], F32)
        self.onesf = self.sbuf("onesf", [1, 128], F32)
        self.ident = self.sbuf("ident", [128, 128], BF16)
        self.causal01 = self.sbuf("causal01", [128, 128], BF16)
        self.eps_t = self.sbuf("eps_t", [128, 1], F32)
        self.negln8 = self.sbuf("negln8", [128, 1], F32)
        self.th8 = self.sbuf("th8", [8, 1], F32)
        self.negth8 = self.sbuf("negth8", [8, 1], F32)
        self.thbc = self.sbuf("thbc", [128, 8], F32)
        self.negthbc = self.sbuf("negthbc", [128, 8], F32)
        self.am_sb = self.sbuf("am_sb", [128, 4 * 512], BF16)

        s_ld = self.sem("p0ld")
        s_gp = self.sem("p0gp")
        s_cm = self.sem("p0cm")
        s_mm = self.sem("p0mm")
        s_st = self.sem("p0st")
        s_th = self.sem("p0th")

        ctx = contextlib.ExitStack()
        d8 = self.sbuf("d8", [128, 8200], BF16, ctx)
        d8r = self.sbuf("d8r", [128, 8200], BF16, ctx)
        iot = self.sbuf("p0_iot", [8, 8200], F32, ctx)
        ntile = self.sbuf("p0_nt", [128, 512], F32, ctx)
        acc = self.sbuf("p0_acc", [128, 512], F32, ctx)
        eh = self.sbuf("p0_eh", [128, 512], F32, ctx)
        zt = self.sbuf("p0_zt", [128, 512], F32, ctx)
        dtile = self.sbuf("p0_dt", [128, 512], BF16, ctx)
        dtile2 = self.sbuf("p0_dt2", [128, 512], BF16, ctx)
        throw = self.sbuf("p0_throw", [1, 8], F32, ctx)
        pdec = self.psum("p0_pdec", [128, 512], F32, ctx)
        pth = self.psum("p0_pth", [128, 8], F32, ctx)

        @blk.sync
        def _(sy):
            sy.dma_start(out=k.th8[:], in_=k.theta.rearrange("(a b) -> a b", b=1)
                         ).then_inc(s_ld, 16)
            sy.dma_start(out=throw[:], in_=k.theta.rearrange("(a b) -> b a", b=1)
                         ).then_inc(s_ld, 16)

        @blk.gpsimd
        def _(gp):
            gp.memset(k.ones_bf[:], 1.0)
            gp.memset(k.ones_f[:], 1.0)
            gp.memset(k.onesf[:], 1.0)
            gp.memset(k.eps_t[:], EPS)
            gp.memset(k.negln8[:], -LN8)
            gp.memset(k.ident[:], 0.0)
            gp.affine_select(out=k.ident[:], in_=k.ident[:], compare_op=ALU.not_equal,
                             fill=1.0, base=0, pattern=[[-1, 128]], channel_multiplier=1)
            gp.memset(k.ident_f[:], 0.0)
            gp.affine_select(out=k.ident_f[:], in_=k.ident_f[:],
                             compare_op=ALU.not_equal, fill=1.0, base=0,
                             pattern=[[-1, 128]], channel_multiplier=1)
            gp.memset(k.causal01[:], 1.0)
            gp.affine_select(out=k.causal01[:], in_=k.causal01[:], compare_op=ALU.is_ge,
                             fill=0.0, base=0, pattern=[[1, 128]], channel_multiplier=-1)
            for i, c in enumerate([0, -128, -256, -384]):
                sl = k.am_sb[:, i * 512:(i + 1) * 512]
                gp.memset(sl, 0.0)
                gp.affine_select(out=sl, in_=sl, compare_op=ALU.is_ge, fill=-1e30,
                                 base=c, pattern=[[1, 512]], channel_multiplier=-1)
            gp.memset(d8[:], 0.0)
            gp.memset(d8r[:], 0.0)
            gp.iota(iot[0:8, :], pattern=[[1, 8200]], base=0, channel_multiplier=0,
                    allow_small_or_imprecise_dtypes=True).then_inc(s_gp, 1)

        @blk.scalar
        def _(sc):
            sc.wait_ge(s_ld, 32)
            sc.wait_ge(s_gp, 1)
            sc.mul(k.negth8[:], k.th8[:], -1.0)
            sc.activation(d8[0:8, :], iot[0:8, :], AF.Exp, bias=0.0,
                          scale=k.negth8[0:8, 0:1]).then_inc(s_cm, 1)

        @blk.gpsimd
        def _(gp):
            gp.wait_ge(s_cm, 1)
            gp.iota(iot[0:8, :], pattern=[[-1, 8200]], base=8199, channel_multiplier=0,
                    allow_small_or_imprecise_dtypes=True).then_inc(s_gp, 1)

        @blk.scalar
        def _(sc):
            sc.wait_ge(s_gp, 2)
            sc.activation(d8r[0:8, :], iot[0:8, :], AF.Exp, bias=k.negln8[0:8, 0:1],
                          scale=k.negth8[0:8, 0:1]).then_inc(s_cm, 1)

        @blk.tensor
        def _(tn):
            tn.wait_ge(s_ld, 32)
            tn.wait_ge(s_gp, 1)
            tn.matmul(pth[:], k.onesf[:], throw[:], start=True, stop=True
                      ).then_inc(s_mm, 1)

        @blk.vector
        def _(v):
            v.wait_ge(s_mm, 1)
            v.tensor_copy(k.thbc[:], pth[:])
            v.tensor_scalar_mul(k.negthbc[:], pth[:], -1.0).then_inc(s_th, 1)

        M0 = 8199
        offc = list(range(128, 7681, 128))

        @blk.tensor
        def _(tn):
            for i, c in enumerate(offc):
                tn.wait_ge(s_cm, 2 + i)
                tn.matmul(pdec[:], d8r[:, M0 - c:M0 - c + 128], d8[:, 0:512],
                          start=True, stop=True).then_inc(s_mm, 1)

        @blk.scalar
        def _(sc):
            for i in range(len(offc)):
                sc.wait_ge(s_mm, 2 + i)
                sc.wait_ge(s_st, i * 16)
                sc.copy(dtile[:], pdec[:]).then_inc(s_cm, 1)

        @blk.sync
        def _(sy):
            for i, c in enumerate(offc):
                sy.wait_ge(s_cm, 3 + i)
                sy.dma_start(out=k.dec_buf[(c + 384) // 128, :, :], in_=dtile[:]
                             ).then_inc(s_st, 16)

        chain = self.sem("p0ch")
        chn = [0]

        def cstep(instr, n=1):
            instr.then_inc(chain, n)
            chn[0] += n

        for i, c in enumerate([0, -128, -256, -384]):
            @blk.gpsimd
            def _(gp, c=c, n0=chn[0]):
                gp.wait_ge(chain, n0)
                cstep(gp.iota(ntile[:], pattern=[[1, 512]], base=c,
                              channel_multiplier=-1,
                              allow_small_or_imprecise_dtypes=True))
            for h in range(8):
                @blk.vector
                def _(v, h=h, n0=chn[0]):
                    v.wait_ge(chain, n0)
                    v.wait_ge(s_th, 1)
                    cstep(v.tensor_scalar(zt[:], ntile[:], k.negthbc[:, h:h + 1],
                                          0.0, ALU.mult, ALU.min))
                @blk.scalar
                def _(sc, n0=chn[0]):
                    sc.wait_ge(chain, n0)
                    cstep(sc.activation(eh[:], zt[:], AF.Exp,
                                        bias=k.negln8[:, 0:1], scale=1.0))
                @blk.vector
                def _(v, h=h, n0=chn[0]):
                    v.wait_ge(chain, n0)
                    if h == 0:
                        cstep(v.tensor_copy(acc[:], eh[:]))
                    else:
                        cstep(v.tensor_add(acc[:], acc[:], eh[:]))
            @blk.gpsimd
            def _(gp, c=c, n0=chn[0]):
                gp.wait_ge(chain, n0)
                cstep(gp.affine_select(out=acc[:], in_=acc[:], compare_op=ALU.is_ge,
                                       fill=0.0, base=c, pattern=[[1, 512]],
                                       channel_multiplier=-1))
            @blk.vector
            def _(v, n0=chn[0]):
                v.wait_ge(chain, n0)
                cstep(v.tensor_copy(dtile2[:], acc[:]))
            @blk.sync
            def _(sy, c=c, n0=chn[0]):
                sy.wait_ge(chain, n0)
                cstep(sy.dma_start(out=k.dec_buf[(c + 384) // 128, :, :],
                                   in_=dtile2[:]), 16)

        if getattr(self, "dbg_p0", False):
            dbg_d8 = nc.dram_tensor("dbg_d8_scr", [128, 8200], BF16).ap()
            dbg_d8r = nc.dram_tensor("dbg_d8r_scr", [128, 8200], BF16).ap()
            self._d8_scr, self._d8r_scr = dbg_d8, dbg_d8r
            @blk.sync
            def _(sy):
                sy.wait_ge(s_cm, 3)
                sy.dma_start(out=dbg_d8[:, :], in_=d8[:]).then_inc(s_st, 16)
                sy.dma_start(out=dbg_d8r[:, :], in_=d8r[:]).then_inc(s_st, 16)

        @blk.sync
        def _(sy, nst=len(offc) * 16 + (32 if getattr(self, "dbg_p0", False) else 0),
              nch=chn[0]):
            sy.wait_ge(s_st, nst)
            sy.wait_ge(chain, nch)
        ctx.close()

    def p1_ln1(self):
        self._ln_pass(self.x_full, T, out_bf=self.xln1_buf, tag="p1")

    def _ln_pass(self, x_src, nrows, out_bf, tag, br_src=None, x_out=None,
                 src_dt=F32):
        k = self
        blk = self.block
        ctx = contextlib.ExitStack()
        nt = nrows // 128
        xin = [self.sbuf(f"{tag}_xin{j}", [128, D], src_dt, ctx) for j in range(2)]
        xrs = [self.sbuf(f"{tag}_xrs{j}", [128, D], F32, ctx) for j in range(2)]
        brt = [self.sbuf(f"{tag}_brt{j}", [128, D], BF16, ctx) for j in range(2)]
        xsq = [self.sbuf(f"{tag}_xsq{j}", [128, D], BF16, ctx) for j in range(2)]
        xout = [self.sbuf(f"{tag}_xo{j}", [128, D], BF16, ctx) for j in range(2)]
        s1 = [self.sbuf(f"{tag}_s1{j}", [128, 1], F32, ctx) for j in range(2)]
        s2 = [self.sbuf(f"{tag}_s2{j}", [128, 1], F32, ctx) for j in range(2)]
        mu = [self.sbuf(f"{tag}_mu{j}", [128, 1], F32, ctx) for j in range(2)]
        t0_ = [self.sbuf(f"{tag}_t0{j}", [128, 1], F32, ctx) for j in range(2)]
        var = [self.sbuf(f"{tag}_va{j}", [128, 1], F32, ctx) for j in range(2)]
        lnv = [self.sbuf(f"{tag}_ln{j}", [128, 1], F32, ctx) for j in range(2)]
        rstd = [self.sbuf(f"{tag}_rs{j}", [128, 1], F32, ctx) for j in range(2)]
        s_ld = self.sem(f"{tag}ld")
        s_sq = self.sem(f"{tag}sq")
        s_dv = self.sem(f"{tag}dv")
        s_rs = self.sem(f"{tag}rs")
        s_ap = self.sem(f"{tag}ap")
        s_sd = self.sem(f"{tag}sd")
        s_ra = self.sem(f"{tag}ra")
        has_res = br_src is not None
        dma_per_tile = 2 if has_res else 1
        st_per_tile = (1 if out_bf is not None else 0) + (1 if x_out is not None else 0)

        @blk.sync
        def _(sy):
            for i in range(nt):
                if i >= 2:
                    sy.wait_ge(s_ap, i - 1)
                sy.dma_start(out=xin[i % 2][:], in_=x_src[i * 128:(i + 1) * 128, :]
                             ).then_inc(s_ld, 16)
                if has_res:
                    sy.dma_start(out=brt[i % 2][:],
                                 in_=br_src[:, i * 128:(i + 1) * 128],
                                 transpose=True).then_inc(s_ld, 16)
                if i > 0:
                    self._ln_store(sy, i - 1, out_bf, x_out, xout, xrs, s_ap, s_sd,
                                   st_per_tile)
            self._ln_store(sy, nt - 1, out_bf, x_out, xout, xrs, s_ap, s_sd,
                           st_per_tile)
            sy.wait_ge(s_sd, st_per_tile * 16 * nt)

        @blk.vector
        def _(v):
            for i in range(nt):
                j = i % 2
                v.wait_ge(s_ld, dma_per_tile * 16 * (i + 1))
                if i >= 2 and st_per_tile:
                    v.wait_ge(s_sd, st_per_tile * 16 * (i - 1))
                if has_res:
                    v.tensor_add(xrs[j][:], xin[j][:], brt[j][:]).then_inc(s_ra, 1)
                    src = xrs[j]
                else:
                    src = xin[j]
                if has_res:
                    v.wait_ge(s_ra, i + 1)
                v.tensor_reduce(s1[j][:], src[:], mybir.AxisListType.X, ALU.add
                                ).then_inc(s_dv, 1)
                v.wait_ge(s_sq, i + 1)
                v.tensor_reduce(s2[j][:], xsq[j][:], mybir.AxisListType.X, ALU.add)
                if i >= 2:
                    v.wait_ge(s_rs, 2 * i - 2)
                v.wait_ge(s_dv, 3 * i + 1)
                v.tensor_scalar(mu[j][:], s1[j][:], 1.0 / D, None, ALU.mult)
                v.tensor_scalar(t0_[j][:], s1[j][:], 1.0 / (1.0 * D * D), s1[j][:],
                                ALU.mult, ALU.mult).then_inc(s_dv, 1)
                v.wait_ge(s_dv, 3 * i + 2)
                v.scalar_tensor_tensor(var[j][:], s2[j][:], 1.0 / D, t0_[j][:],
                                       ALU.mult, ALU.subtract
                                       ).then_inc(s_dv, 1)
                v.wait_ge(s_rs, 2 * i + 2)
                v.tensor_scalar(xout[j][:], src[:], mu[j][:], rstd[j][:],
                                ALU.subtract, ALU.mult).then_inc(s_ap, 1)

        @blk.scalar
        def _(sc):
            for i in range(nt):
                j = i % 2
                if has_res:
                    sc.wait_ge(s_ra, i + 1)
                else:
                    sc.wait_ge(s_ld, dma_per_tile * 16 * (i + 1))
                if i >= 2:
                    sc.wait_ge(s_dv, 3 * (i - 2) + 2)
                src = xrs[j] if has_res else xin[j]
                sc.square(xsq[j][:], src[:]).then_inc(s_sq, 1)
                sc.wait_ge(s_dv, 3 * i + 3)
                if i >= 2:
                    sc.wait_ge(s_ap, i - 1)
                sc.activation(lnv[j][:], var[j][:], AF.Ln, bias=k.eps_t[:, 0:1],
                              scale=1.0).then_inc(s_rs, 1)
                sc.wait_ge(s_rs, 2 * i + 1)
                sc.activation(rstd[j][:], lnv[j][:], AF.Exp, bias=0.0,
                              scale=-0.5).then_inc(s_rs, 1)
        ctx.close()

    def _ln_store(self, sy, i, out_bf, x_out, xout, xrs, s_ap, s_sd, has_res):
        sy.wait_ge(s_ap, i + 1)
        if out_bf is not None:
            sy.dma_start(out=out_bf[i * 128:(i + 1) * 128, :], in_=xout[i % 2][:]
                         ).then_inc(s_sd, 16)
        if x_out is not None:
            sy.dma_start(out=x_out[i * 128:(i + 1) * 128, :], in_=xrs[i % 2][:]
                         ).then_inc(s_sd, 16)

    def p2_proj0(self):
        k = self
        blk = self.block
        ctx = contextlib.ExitStack()
        wq = self.sbuf("p2_wq", [128, 8 * 512], BF16, ctx)
        wk = self.sbuf("p2_wk", [128, 8 * 512], BF16, ctx)
        wv = self.sbuf("p2_wv", [128, 8 * 512], BF16, ctx)
        xf = [[self.sbuf(f"p2_xf{j}_{dc}", [128, 512], BF16, ctx)
               for dc in range(8)] for j in range(2)]
        ot = [self.sbuf(f"p2_ot{j}", [128, 512], BF16, ctx) for j in range(4)]
        ps = [self.psum(f"p2_ps{j}", [128, 512], F32, ctx) for j in range(4)]
        s_w = self.sem("p2w")
        s_xf = self.sem("p2xf")
        s_mm = self.sem("p2mm")
        s_ev = self.sem("p2ev")
        s_st = self.sem("p2st")
        QSC = 0.08838834764831845

        @blk.gpsimd
        def _(gp):
            for wsb, wsrc in [(wq, k.w_q0), (wk, k.w_k0), (wv, k.w_v0)]:
                gp.dma_start(out=wsb.rearrange("p (dc o) -> p dc o", dc=8),
                             in_=wsrc.rearrange("(dc p) o -> p dc o", p=128)
                             ).then_inc(s_w, 16)

        @blk.sync
        def _(sy):
            for i in range(NT512):
                if i >= 2:
                    sy.wait_ge(s_mm, 12 * (i - 1))
                for dc in range(8):
                    sy.dma_start(out=xf[i % 2][dc][:],
                                 in_=k.xln1_buf[i * 512:(i + 1) * 512,
                                                dc * 128:(dc + 1) * 128],
                                 transpose=True).then_inc(s_xf, 16)
                for g in range(12):
                    n = i * 12 + g
                    sy.wait_ge(s_ev, n + 1)
                    if g < 4:
                        dst = k.q0_buf[g, :, i * 512:(i + 1) * 512]
                    elif g < 8:
                        dst = k.k0_buf[g - 4, :, i * 512:(i + 1) * 512]
                    else:
                        st = g - 8
                        dst = k.v0_buf[i * 512 + st * 128:i * 512 + (st + 1) * 128, :]
                    sy.dma_start(out=dst, in_=ot[n % 4][:]).then_inc(s_st, 16)
            sy.wait_ge(s_st, 16 * 12 * NT512)

        @blk.tensor
        def _(tn):
            tn.wait_ge(s_w, 48)
            for i in range(NT512):
                tn.wait_ge(s_xf, 16 * 8 * (i + 1))
                for g in range(12):
                    n = i * 12 + g
                    if n >= 4:
                        tn.wait_ge(s_ev, n - 3)
                    pb = ps[n % 4]
                    for dc in range(8):
                        if g < 4:
                            lhs = wq[:, dc * 512 + g * 128:dc * 512 + (g + 1) * 128]
                            rhs = xf[i % 2][dc][:]
                        elif g < 8:
                            h = g - 4
                            lhs = wk[:, dc * 512 + h * 128:dc * 512 + (h + 1) * 128]
                            rhs = xf[i % 2][dc][:]
                        else:
                            st = g - 8
                            lhs = xf[i % 2][dc][:, st * 128:(st + 1) * 128]
                            rhs = wv[:, dc * 512:(dc + 1) * 512]
                        mm = tn.matmul(pb[:], lhs, rhs, start=(dc == 0),
                                       stop=(dc == 7))
                    mm.then_inc(s_mm, 1)

        @blk.scalar
        def _(sc):
            for i in range(NT512):
                for g in range(12):
                    n = i * 12 + g
                    sc.wait_ge(s_mm, n + 1)
                    if n >= 4:
                        sc.wait_ge(s_st, 16 * (n - 3))
                    scale = QSC if g < 4 else 1.0
                    sc.activation(ot[n % 4][:], ps[n % 4][:], AF.Copy, bias=0.0,
                                  scale=scale).then_inc(s_ev, 1)
        ctx.close()

    def p3_attn0(self):
        k = self
        blk = self.block
        ctx = contextlib.ExitStack()
        qt = [[self.sbuf(f"p3_qt{j}{h}", [128, 512], BF16, ctx) for h in range(2)]
              for j in range(2)]
        kc = [[self.sbuf(f"p3_kc{j}{h}", [128, 1024], BF16, ctx) for h in range(2)]
              for j in range(2)]
        vc = [self.sbuf(f"p3_vc{j}", [128, 2048], BF16, ctx) for j in range(2)]
        dcc = [self.sbuf(f"p3_dc{j}", [128, 8 * 512], BF16, ctx) for j in range(2)]
        sstt = [self.sbuf(f"p3_stt{j}", [128, 512], F32, ctx) for j in range(2)]
        sd2 = [self.sbuf(f"p3_sd2{j}", [128, 512], F32, ctx) for j in range(2)]
        Pt = [self.sbuf(f"p3_pt{j}", [128, 512], BF16, ctx) for j in range(2)]
        rrow = [self.sbuf(f"p3_rr{h}", [1, 512], F32, ctx) for h in range(2)]
        rbs = [self.sbuf(f"p3_rbs{h}", [128, 512], F32, ctx) for h in range(2)]
        od = [self.sbuf(f"p3_od{h}", [128, 512], BF16, ctx) for h in range(2)]
        sq = [self.sbuf(f"p3_sq{h}", [128, 512], BF16, ctx) for h in range(2)]
        t1b = [self.sbuf(f"p3_t1{h}", [128, 512], F32, ctx) for h in range(2)]
        murow = [self.sbuf(f"p3_mur{h}", [1, 512], F32, ctx) for h in range(2)]
        mu2row = [self.sbuf(f"p3_mu2{h}", [1, 512], F32, ctx) for h in range(2)]
        varrow = [self.sbuf(f"p3_var{h}", [1, 512], F32, ctx) for h in range(2)]
        lnr = [self.sbuf(f"p3_lnr{h}", [1, 512], F32, ctx) for h in range(2)]
        rsr = [self.sbuf(f"p3_rsr{h}", [1, 512], F32, ctx) for h in range(2)]
        gn = [self.sbuf(f"p3_gn{j}", [128, 512], BF16, ctx) for j in range(2)]
        gg = self.sbuf("p3_gg", [128, 4], F32, ctx)
        gb = self.sbuf("p3_gb", [128, 4], F32, ctx)
        scP = [self.psum(f"p3_scP{j}", [128, 512], F32, ctx) for j in range(2)]
        avP = [self.psum(f"p3_avP{h}", [128, 512], F32, ctx) for h in range(2)]
        rP = [self.psum(f"p3_rP{h}", [128, 512], F32, ctx) for h in range(2)]
        bcP = [self.psum(f"p3_bcP{j}", [128, 512], F32, ctx) for j in range(2)]

        ops = []
        W = lambda *names: [w for w in names if w is not None]
        for hl in range(4):
            ops.append(dict(eng="sy", name=f"gg{hl}", emit=lambda sy, hl=hl:
                sy.dma_start(out=gg[:, hl:hl + 1],
                             in_=k.gn_gamma.rearrange("(a b) -> a b", b=1)[
                                 128 * hl:128 * (hl + 1), :])))
            ops.append(dict(eng="sy", name=f"gb{hl}", emit=lambda sy, hl=hl:
                sy.dma_start(out=gb[:, hl:hl + 1],
                             in_=k.gn_beta.rearrange("(a b) -> a b", b=1)[
                                 128 * hl:128 * (hl + 1), :])))

        n = 0
        gci = 0
        kidx = 0
        chunk_last = {}
        k_last = {}
        tail = {}
        gstore = {}
        for tb in range(16):
            t0 = 512 * tb
            S = 4 * tb + 4
            for hp in range(2):
                qwait = k_last.get(kidx - 2)
                for h in range(2):
                    ops.append(dict(eng="sy", name=f"q{kidx}_{h}",
                        waits=W(qwait),
                        emit=lambda sy, h=h, hp=hp, j=kidx % 2, t0=t0:
                        sy.dma_start(out=qt[j][h][:],
                                     in_=k.q0_buf[2 * hp + h, :, t0:t0 + 512])))
                gci0 = gci
                nch = (S + 7) // 8
                n0 = n
                for ch in range(nch):
                    si0 = ch * 8
                    nsi = min(8, S - si0)
                    cl = chunk_last.get(gci - 2, (None, None, None))
                    jc = gci % 2
                    for h in range(2):
                        ops.append(dict(eng="sy", name=f"k{gci}_{h}",
                            waits=W(cl[0] if h == 0 else None),
                            emit=lambda sy, h=h, hp=hp, jc=jc, s0=si0 * 128, w=nsi * 128:
                            sy.dma_start(out=kc[jc][h][:, 0:w],
                                         in_=k.k0_buf[2 * hp + h, :, s0:s0 + w])))
                    ops.append(dict(eng="sy", name=f"v{gci}",
                        waits=W(cl[2]),
                        emit=lambda sy, hp=hp, jc=jc, s0=si0 * 128, nsi=nsi:
                        sy.dma_start(out=vc[jc].rearrange("p (a b) -> p a b", a=8)[:, 0:nsi, :],
                                     in_=k.v0_buf[s0:s0 + nsi * 128, 256 * hp:256 * (hp + 1)]
                                     .rearrange("(a p) b -> p a b", p=128))))
                    for sl in range(nsi):
                        si = si0 + sl
                        c = t0 - 128 * si
                        ops.append(dict(eng="sy", name=f"d{gci}_{sl}",
                            waits=W(cl[1] if sl == 0 else None),
                            emit=lambda sy, jc=jc, sl=sl, idx=(c + 384) // 128:
                            sy.dma_start(out=dcc[jc][:, sl * 512:(sl + 1) * 512],
                                         in_=k.dec_buf[idx, :, :])))
                    for sl in range(nsi):
                        si = si0 + sl
                        c = t0 - 128 * si
                        diag = c <= 0
                        for h in range(2):
                            p2 = n % 2
                            scw = [f"stt{n - 2}"] if n >= 2 else []
                            if sl == 0 and h == 0:
                                scw += [f"k{gci}_0", f"k{gci}_1", f"q{kidx}_0",
                                        f"q{kidx}_1"]
                            ops.append(dict(eng="t", name=f"sc{n}", waits=scw,
                                emit=lambda tn, jc=jc, sl=sl, h=h, jq=kidx % 2, p2=p2:
                                tn.matmul(scP[p2][:], kc[jc][h][:, sl * 128:(sl + 1) * 128],
                                          qt[jq][h][:], start=True, stop=True)))
                            sttw = [f"sc{n}"]
                            if sl == 0 and h == 0:
                                sttw.append(f"d{gci}_{nsi - 1}")
                            if n >= 2:
                                sttw.append(f"exp{n - 2}")
                            ops.append(dict(eng="v", name=f"stt{n}", waits=sttw,
                                emit=lambda v, jc=jc, sl=sl, p2=p2:
                                v.scalar_tensor_tensor(sstt[p2][:], scP[p2][:], 30.0,
                                    dcc[jc][:, sl * 512:(sl + 1) * 512],
                                    ALU.min, ALU.mult)))
                            fname = f"stt{n}"
                            if diag:
                                am = (-c) // 128
                                ops.append(dict(eng="v", name=f"dg{n}",
                                    waits=[f"stt{n}"],
                                    emit=lambda v, p2=p2, am=am:
                                    v.tensor_add(sd2[p2][:], sstt[p2][:],
                                                 k.am_sb[:, am * 512:(am + 1) * 512])))
                                fname = f"dg{n}"
                            expw = [fname]
                            if n >= 2:
                                expw.append(f"av{n - 2}")
                            ops.append(dict(eng="s", name=f"exp{n}", waits=expw,
                                emit=lambda sc, p2=p2, diag=diag:
                                sc.activation(Pt[p2][:], (sd2 if diag else sstt)[p2][:],
                                              AF.Exp, bias=0.0, scale=1.0)))
                            rw = [f"exp{n}"]
                            if si == 0:
                                pk = tail.get((kidx - 1, h))
                                if pk:
                                    rw.append(pk["varrow"])
                            ops.append(dict(eng="t", name=f"r{n}", waits=rw,
                                emit=lambda tn, p2=p2, h=h, st=(si == 0), sp=(si == S - 1):
                                tn.matmul(rP[h][:], k.ones_bf[:], Pt[p2][:],
                                          start=st, stop=sp, skip_group_check=True)))
                            avw = []
                            if sl == 0 and h == 0:
                                avw.append(f"v{gci}")
                            if si == 0:
                                pk = tail.get((kidx - 1, h))
                                if pk:
                                    avw.append(pk["murow"])
                            ops.append(dict(eng="t", name=f"av{n}", waits=avw,
                                emit=lambda tn, jc=jc, sl=sl, h=h, p2=p2,
                                       st=(si == 0), sp=(si == S - 1):
                                tn.matmul(avP[h][:],
                                          vc[jc][:, sl * 256 + h * 128:sl * 256 + (h + 1) * 128],
                                          Pt[p2][:], start=st, stop=sp,
                                          skip_group_check=True)))
                            if h == 0 and sl == nsi - 1:
                                pass
                            n += 1
                    chunk_last[gci] = (f"sc{n - 1}", f"stt{n - 1}", f"av{n - 1}")
                    gci += 1
                k_last[kidx] = f"sc{n - 1}"
                for h in range(2):
                    kk = f"{kidx}_{h}"
                    ptl = tail.get((kidx, h - 1)) or tail.get((kidx - 1, 1))
                    pk = tail.get((kidx - 1, h))
                    gidx = kidx * 2 + h
                    t = {}
                    t["rinv"] = f"ri{kk}"
                    ops.append(dict(eng="v", name=f"ri{kk}",
                        waits=W(f"r{n - 2 + h}", pk["rb"] if pk else None),
                        emit=lambda v, h=h: v.reciprocal(rrow[h][:], rP[h][0:1, :])))
                    t["rb"] = f"rb{kk}"
                    ops.append(dict(eng="t", name=f"rb{kk}",
                        waits=W(f"ri{kk}", ptl["t2"] if ptl else None,
                                f"stt{n - 1}"),
                        emit=lambda tn, h=h: tn.matmul(bcP[0][:], k.onesf[:],
                                                       rrow[h][:], start=True,
                                                       stop=True)))
                    t["rbs"] = f"rv{kk}"
                    ops.append(dict(eng="s", name=f"rv{kk}",
                        waits=W(f"rb{kk}", pk["od"] if pk else None),
                        emit=lambda sc, h=h: sc.copy(rbs[h][:], bcP[0][:])))
                    t["od"] = f"od{kk}"
                    ops.append(dict(eng="v", name=f"od{kk}",
                        waits=W(f"rv{kk}", f"av{n - 2 + h}",
                                pk["t1"] if pk else None),
                        emit=lambda v, h=h: v.tensor_mul(od[h][:], avP[h][:],
                                                         rbs[h][:])))
                    if getattr(self, "dbg_p3", None) == (kidx, h):
                        dbgp3 = self.nc.dram_tensor("p3dbg", [4, 128, 512], F32).ap()
                        self._p3dbg = dbgp3
                        ops.append(dict(eng="sy", name="dbg_od", waits=[f"od{kk}"],
                            emit=lambda sy, h=h: sy.dma_start(
                                out=dbgp3[0, :, :].bitcast(BF16)[:, 0:512],
                                in_=od[h][:])))
                        ops.append(dict(eng="sy", name="dbg_rr", waits=[f"ri{kk}"],
                            emit=lambda sy, h=h: sy.dma_start(
                                out=dbgp3[1, 0:1, :], in_=rrow[h][:])))
                        ops.append(dict(eng="sy", name="dbg_pt", waits=[f"exp{n - 2 + h}"],
                            emit=lambda sy, p2=(n - 2 + h) % 2: sy.dma_start(
                                out=dbgp3[2, :, :].bitcast(BF16)[:, 0:512],
                                in_=Pt[p2][:])))

                    if getattr(self, "dbg_p3", None) == (kidx, h):
                        ops.append(dict(eng="sy", name="dbg_mu", waits=[f"mur{kk}"],
                            emit=lambda sy, h=h: sy.dma_start(
                                out=self._p3dbg[1, 1:2, :], in_=murow[h][:])))
                        ops.append(dict(eng="sy", name="dbg_va", waits=[f"var{kk}"],
                            emit=lambda sy, h=h: sy.dma_start(
                                out=self._p3dbg[1, 2:3, :], in_=varrow[h][:])))
                        ops.append(dict(eng="sy", name="dbg_rs", waits=[f"ex2{kk}"],
                            emit=lambda sy, h=h: sy.dma_start(
                                out=self._p3dbg[1, 3:4, :], in_=rsr[h][:])))
                        ops.append(dict(eng="sy", name="dbg_gn", waits=[f"gnf{kk}"],
                            emit=lambda sy, jg=(kidx * 2 + h) % 2: sy.dma_start(
                                out=self._p3dbg[3, :, 0:256].bitcast(BF16),
                                in_=gn[jg][:])))
                    t["sq"] = f"sqq{kk}"
                    ops.append(dict(eng="s", name=f"sqq{kk}",
                        waits=W(f"od{kk}", pk["st2"] if pk else None),
                        emit=lambda sc, h=h: sc.square(sq[h][:], od[h][:])))
                    t["st1"] = f"s1m{kk}"
                    ops.append(dict(eng="t", name=f"s1m{kk}", waits=[f"od{kk}"],
                        emit=lambda tn, h=h: tn.matmul(avP[h][:], k.ones_bf[:],
                                                       od[h][:], start=True,
                                                       stop=True)))
                    t["st2"] = f"s2m{kk}"
                    ops.append(dict(eng="t", name=f"s2m{kk}",
                        waits=[f"sqq{kk}", f"ri{kk}"],
                        emit=lambda tn, h=h: tn.matmul(rP[h][:], k.ones_bf[:],
                                                       sq[h][:], start=True,
                                                       stop=True)))
                    t["murow"] = f"mur{kk}"
                    ops.append(dict(eng="v", name=f"mur{kk}",
                        waits=W(f"s1m{kk}", pk["mubc"] if pk else None),
                        emit=lambda v, h=h: v.tensor_scalar(murow[h][:],
                            avP[h][0:1, :], 1.0 / 128, None, ALU.mult)))
                    ops.append(dict(eng="v", name=f"mu2{kk}", waits=[f"mur{kk}"],
                        emit=lambda v, h=h: v.tensor_mul(mu2row[h][:],
                            murow[h][:], murow[h][:])))
                    t["varrow"] = f"var{kk}"
                    ops.append(dict(eng="v", name=f"var{kk}",
                        waits=[f"s2m{kk}", f"mu2{kk}"],
                        emit=lambda v, h=h: v.scalar_tensor_tensor(varrow[h][:],
                            rP[h][0:1, :], 1.0 / 128, mu2row[h][:],
                            ALU.mult, ALU.subtract)))
                    ops.append(dict(eng="s", name=f"lnn{kk}", waits=[f"var{kk}"],
                        emit=lambda sc, h=h: sc.activation(lnr[h][:], varrow[h][:],
                            AF.Ln, bias=k.eps_t[0:1, 0:1], scale=1.0)))
                    ops.append(dict(eng="s", name=f"ex2{kk}", waits=[f"lnn{kk}"],
                        emit=lambda sc, h=h: sc.activation(rsr[h][:], lnr[h][:],
                            AF.Exp, bias=0.0, scale=-0.5)))
                    t["mubc"] = f"mbc{kk}"
                    ops.append(dict(eng="t", name=f"mbc{kk}",
                        waits=W(f"mur{kk}", ptl["t1"] if ptl else None),
                        emit=lambda tn, h=h: tn.matmul(bcP[1][:], k.onesf[:],
                            murow[h][:], start=True, stop=True)))
                    t["rsbc"] = f"rbc{kk}"
                    ops.append(dict(eng="t", name=f"rbc{kk}",
                        waits=[f"ex2{kk}", f"rv{kk}"],
                        emit=lambda tn, h=h: tn.matmul(bcP[0][:], k.onesf[:],
                            rsr[h][:], start=True, stop=True)))
                    t["t1"] = f"t1{kk}"
                    ops.append(dict(eng="v", name=f"t1{kk}", waits=[f"mbc{kk}"],
                        emit=lambda v, h=h: v.tensor_sub(t1b[h][:], od[h][:],
                                                         bcP[1][:])))
                    t["t2"] = f"t2{kk}"
                    gw = [f"t1{kk}", f"rbc{kk}"]
                    if gidx >= 2:
                        gw.append(gstore[gidx - 2])
                    ops.append(dict(eng="v", name=f"t2{kk}", waits=gw,
                        emit=lambda v, h=h, jg=gidx % 2: v.tensor_mul(
                            t1b[h][:], t1b[h][:], bcP[0][:])))
                    ops.append(dict(eng="v", name=f"gnf{kk}", waits=[f"t2{kk}"],
                        emit=lambda v, h=h, hl=2 * hp + h, jg=gidx % 2:
                        v.tensor_scalar(gn[jg][:], t1b[h][:], gg[:, hl:hl + 1],
                                        gb[:, hl:hl + 1], ALU.mult, ALU.add)))
                    gstore[gidx] = f"gst{gidx}"
                    ops.append(dict(eng="sy", name=f"gst{gidx}",
                        waits=[f"gnf{kk}"],
                        emit=lambda sy, hl=2 * hp + h, t0=t0, jg=gidx % 2:
                        sy.dma_start(out=k.cc1_in[128 * hl:128 * (hl + 1),
                                                  t0:t0 + 512], in_=gn[jg][:])))
                    tail[(kidx, h)] = t
                kidx += 1
        sems, cnt, ev = self.run_ops(ops, "p3")
        self.s_cc1 = self.sem("cc1")
        sy_total = cnt["sy"]
        s_cc1 = self.s_cc1

        @blk.gpsimd
        def _(gp):
            gp.wait_ge(sems["sy"], sy_total)
            gp.collective_compute(
                "AllGather", ALU.bypass,
                replica_groups=[[0, 1], [2, 3], [4, 5], [6, 7]],
                ins=[k.cc1_in.opt()], outs=[k.cc1_out.opt()],
            ).then_inc(s_cc1, 1)
        ctx.close()

    def p4_wo_res(self):
        self._proj_ln_res(self.w_o0, self.b_o0, self.cc1_out, self.s_cc1,
                          self.x_own, self.x1_buf, self.xln2_buf, "p4")

    def _proj_ln_res(self, w_src, b_src, g_src, cc_sem, x_src, x_out, xln_out,
                     tag):
        k = self
        blk = self.block
        ctx = contextlib.ExitStack()
        wo = self.sbuf(f"{tag}_wo", [128, 8 * 1024], BF16, ctx)
        bo = self.sbuf(f"{tag}_bo", [128, 8], F32, ctx)
        gt = [self.sbuf(f"{tag}_g{j}", [128, 8 * 512], BF16, ctx) for j in range(2)]
        ao = [self.sbuf(f"{tag}_ao{j}", [128, 8 * 512], F32, ctx) for j in range(2)]
        aob = [self.sbuf(f"{tag}_aob{j}", [128, 8 * 512], BF16, ctx) for j in range(2)]
        sqb = [self.sbuf(f"{tag}_sq{j}", [128, 8 * 512], BF16, ctx) for j in range(2)]
        t1 = self.sbuf(f"{tag}_t1", [128, 512], F32, ctx)
        brt = [self.sbuf(f"{tag}_br{j}", [128, 512], BF16, ctx) for j in range(2)]
        murow = self.sbuf(f"{tag}_mur", [1, 512], F32, ctx)
        mu2row = self.sbuf(f"{tag}_mu2", [1, 512], F32, ctx)
        varrow = self.sbuf(f"{tag}_var", [1, 512], F32, ctx)
        lnr = self.sbuf(f"{tag}_lnr", [1, 512], F32, ctx)
        rsr = self.sbuf(f"{tag}_rsr", [1, 512], F32, ctx)
        pj = [self.psum(f"{tag}_pj{j}", [128, 512], F32, ctx) for j in range(2)]
        stP = [self.psum(f"{tag}_st{j}", [128, 512], F32, ctx) for j in range(2)]
        bcP = [self.psum(f"{tag}_bc{j}", [128, 512], F32, ctx) for j in range(2)]

        ops = []
        W = lambda *names: [w for w in names if w is not None]
        box = {}
        ops.append(dict(eng="sy", name="ofs", dma=False, emit=lambda sy:
            (sy.wait_ge(cc_sem, 1),
             box.__setitem__("ofs", sy.compute_val((sy.partition_id() % 2) * TOWN)),
             sy.nop())[2]))
        ops.append(dict(eng="gp", name="wo", emit=lambda gp:
            gp.dma_start(out=wo.rearrange("p (dc o) -> p dc o", dc=8),
                         in_=w_src.rearrange("(dc p) o -> p dc o", p=128))))
        for oc in range(8):
            ops.append(dict(eng="sy", name=f"bo{oc}", emit=lambda sy, oc=oc:
                sy.dma_start(out=bo[:, oc:oc + 1],
                             in_=b_src.rearrange("(a b) -> a b", b=1)[
                                 128 * oc:128 * (oc + 1), :])))
        nmm = 0
        for i in range(NO512):
            j = i % 2
            ops.append(dict(eng="sy", name=f"g{i}",
                waits=W("ofs" if i == 0 else None,
                        f"mm{i - 2}_7_7" if i >= 2 else None),
                emit=lambda sy, i=i, j=j: sy.dma_start(
                    out=gt[j].rearrange("p (dc t) -> p dc t", dc=8),
                    in_=AP(g_src.tensor, box["ofs"] + i * 512,
                           [[T, 128], [128 * T, 8], [1, 512]]))))
            for oc in range(8):
                for dc in range(8):
                    w = []
                    if dc == 0:
                        if oc == 0:
                            w.append("wo")
                            w.append(f"g{i}")
                        if i >= 1 or oc >= 2:
                            pe = i * 8 + oc - 2
                            w.append(f"ev{pe // 8}_{pe % 8}")
                    ops.append(dict(eng="t", name=f"mm{i}_{oc}_{dc}", waits=w,
                        emit=lambda tn, j=j, oc=oc, dc=dc, b=(i * 8 + oc) % 2:
                        tn.matmul(pj[b][:],
                                  wo[:, dc * 1024 + oc * 128:dc * 1024 + (oc + 1) * 128],
                                  gt[j][:, dc * 512:(dc + 1) * 512],
                                  start=(dc == 0), stop=(dc == 7))))
                ops.append(dict(eng="s", name=f"ev{i}_{oc}",
                    waits=W(f"mm{i}_{oc}_7",
                            f"apa{i - 2}_{oc}" if i >= 2 else None),
                    emit=lambda sc, j=j, oc=oc, b=(i * 8 + oc) % 2:
                    sc.activation(ao[j][:, oc * 512:(oc + 1) * 512], pj[b][:],
                                  AF.Identity, bias=bo[:, oc:oc + 1], scale=1.0)))
                ops.append(dict(eng="v", name=f"cb{i}_{oc}", waits=[f"ev{i}_{oc}"],
                    emit=lambda v, j=j, oc=oc:
                    v.tensor_copy(aob[j][:, oc * 512:(oc + 1) * 512],
                                  ao[j][:, oc * 512:(oc + 1) * 512])))
                ops.append(dict(eng="s", name=f"sq{i}_{oc}", waits=[f"ev{i}_{oc}"],
                    emit=lambda sc, j=j, oc=oc:
                    sc.square(sqb[j][:, oc * 512:(oc + 1) * 512],
                              ao[j][:, oc * 512:(oc + 1) * 512])))
            for oc in range(8):
                ops.append(dict(eng="t", name=f"sm1{i}_{oc}",
                    waits=W(f"cb{i}_{oc}",
                            f"mu{i - 2}" if (oc == 0 and i >= 2) else None),
                    emit=lambda tn, j=j, oc=oc: tn.matmul(
                        stP[0][:], k.ones_bf[:], aob[j][:, oc * 512:(oc + 1) * 512],
                        start=(oc == 0), stop=(oc == 7), skip_group_check=True)))
                ops.append(dict(eng="t", name=f"sm2{i}_{oc}",
                    waits=W(f"sq{i}_{oc}",
                            f"va{i - 2}" if (oc == 0 and i >= 2) else None),
                    emit=lambda tn, j=j, oc=oc: tn.matmul(
                        stP[1][:], k.ones_bf[:], sqb[j][:, oc * 512:(oc + 1) * 512],
                        start=(oc == 0), stop=(oc == 7), skip_group_check=True)))
            ops.append(dict(eng="v", name=f"mu{i}", waits=[f"sm1{i}_7"],
                emit=lambda v: v.tensor_scalar(murow[:], stP[0][0:1, :], 1.0 / D,
                                               None, ALU.mult)))
            ops.append(dict(eng="v", name=f"mu2{i}", waits=[f"mu{i}"],
                emit=lambda v: v.tensor_mul(mu2row[:], murow[:], murow[:])))
            ops.append(dict(eng="v", name=f"va{i}", waits=[f"sm2{i}_7", f"mu2{i}"],
                emit=lambda v: v.scalar_tensor_tensor(varrow[:], stP[1][0:1, :],
                    1.0 / D, mu2row[:], ALU.mult, ALU.subtract)))
            ops.append(dict(eng="s", name=f"ln{i}", waits=[f"va{i}"],
                emit=lambda sc: sc.activation(lnr[:], varrow[:], AF.Ln,
                                              bias=k.eps_t[0:1, 0:1], scale=1.0)))
            ops.append(dict(eng="s", name=f"ex{i}", waits=[f"ln{i}"],
                emit=lambda sc: sc.activation(rsr[:], lnr[:], AF.Exp, bias=0.0,
                                              scale=-0.5)))
            ops.append(dict(eng="t", name=f"mb{i}",
                waits=W(f"mu{i}", f"ap{i - 1}_7" if i >= 1 else None),
                emit=lambda tn: tn.matmul(bcP[0][:], k.onesf[:], murow[:],
                                          start=True, stop=True)))
            ops.append(dict(eng="t", name=f"rb{i}", waits=[f"ex{i}"],
                emit=lambda tn: tn.matmul(bcP[1][:], k.onesf[:], rsr[:],
                                          start=True, stop=True)))
            for oc in range(8):
                g = i * 8 + oc
                ops.append(dict(eng="v", name=f"apa{i}_{oc}",
                    waits=W(f"mb{i}" if oc == 0 else None,
                            f"va{i}" if oc == 0 else None,
                            f"apb{i}_{oc - 1}" if oc >= 1 else
                            (f"apb{i - 1}_7" if i >= 1 else None)),
                    emit=lambda v, j=j, oc=oc: v.tensor_sub(
                        t1[:], ao[j][:, oc * 512:(oc + 1) * 512], bcP[0][:])))
                ops.append(dict(eng="v", name=f"apb{i}_{oc}",
                    waits=W(f"apa{i}_{oc}", f"rb{i}" if oc == 0 else None,
                            f"bst{g - 2}" if g >= 2 else None),
                    emit=lambda v, g=g: v.tensor_mul(brt[g % 2][:], t1[:],
                                                     bcP[1][:])))
                ops.append(dict(eng="v", name=f"ap{i}_{oc}", dma=False, inc=1,
                    waits=[], emit=lambda v: v.engine_nop()))
                ops.append(dict(eng="sy", name=f"bst{g}", waits=[f"apb{i}_{oc}"],
                    emit=lambda sy, i=i, oc=oc, g=g: sy.dma_start(
                        out=k.br_buf[oc * 128:(oc + 1) * 128,
                                     i * 512:(i + 1) * 512],
                        in_=brt[g % 2][:])))
        self.run_ops(ops, tag)
        ctx.close()
        self.barrier()
        self._ln_pass(x_src, TOWN, out_bf=xln_out, tag=f"{tag}b",
                      br_src=self.br_buf, x_out=x_out)

    def p5_ffn0(self):
        k = self
        blk = self.block
        ctx = contextlib.ExitStack()
        wT = self.sbuf("p5_wT", [128, 6 * 8 * 1024], BF16, ctx)
        lbT = self.sbuf("p5_lbT", [128, 6 * 1024], BF16, ctx)
        laT = self.sbuf("p5_laT", [128, 6 * 8 * 32], BF16, ctx)
        qb = self.sbuf("p5_qb", [128, 1024], BF16, ctx)
        scs = self.sbuf("p5_scs", [128, 8 * 64], F32, ctx)
        wtmp = self.sbuf("p5_wtmp", [128, 1024], F32, ctx)
        wod = self.sbuf("p5_wod", [128, 1024], BF16, ctx)
        lbsc = self.sbuf("p5_lbsc", [128, 8 * 32], BF16, ctx)
        bi_sb = self.sbuf("p5_bi", [128, 48], F32, ctx)
        lg_sb = self.sbuf("p5_lg", [128, 24], F32, ctx)
        lbe_sb = self.sbuf("p5_lbe", [128, 24], F32, ctx)
        hb = self.sbuf("p5_hb", [128, 8 * 512], BF16, ctx)
        hx = self.sbuf("p5_hx", [128, 8 * 512], BF16, ctx)
        hf = [self.sbuf(f"p5_hf{j}", [128, 8 * 512], F32, ctx) for j in range(2)]
        y1 = self.sbuf("p5_y1", [128, 8 * 512], F32, ctx)
        sqb = self.sbuf("p5_sqb", [128, 2 * 512], BF16, ctx)
        usb = self.sbuf("p5_usb", [128, 512], BF16, ctx)
        t1 = self.sbuf("p5_t1", [128, 512], F32, ctx)
        brt = self.sbuf("p5_brt", [128, 512], BF16, ctx)
        murow = self.sbuf("p5_mur", [1, 512], F32, ctx)
        rowA = self.sbuf("p5_rwa", [1, 512], F32, ctx)
        rowB = self.sbuf("p5_rwb", [1, 512], F32, ctx)
        rsr = self.sbuf("p5_rsr", [1, 512], F32, ctx)
        pA = [self.psum(f"p5_pA{j}", [128, 512], F32, ctx) for j in range(2)]
        pU = self.psum("p5_pU", [128, 512], F32, ctx)
        pT = self.psum("p5_pT", [128, 128], BF16, ctx)
        stP = [self.psum(f"p5_st{j}", [128, 512], F32, ctx) for j in range(2)]
        bcP = [self.psum(f"p5_bc{j}", [128, 512], F32, ctx) for j in range(2)]

        ops = []
        W = lambda *names: [w for w in names if w is not None]

        ops.append(dict(eng="gp", name="z0", dma=False,
                        emit=lambda gp: gp.memset(lbT[:], 0.0)))
        ops.append(dict(eng="gp", name="z1", dma=False,
                        emit=lambda gp: gp.memset(usb[:], 0.0)))
        for i in range(6):
            ops.append(dict(eng="gp", name=f"ws{i}",
                waits=W(f"dq{i - 1}_7" if i >= 1 else None),
                emit=lambda gp, i=i: gp.dma_start(
                    out=scs.rearrange("p (oc g) -> p oc g", oc=8),
                    in_=k.f_sc[i].rearrange("(oc p) g -> p oc g", p=128))))
            for oc in range(8):
                ops.append(dict(eng="gp", name=f"wq{i}_{oc}",
                    waits=W(f"dqa{i}_{oc - 1}" if oc >= 1 else
                            (f"dqa{i - 1}_7" if i >= 1 else None),
                            f"lap{i - 1}_7" if (i >= 1 and oc == 0) else None),
                    emit=lambda gp, i=i, oc=oc: gp.dma_start(
                        out=qb[:],
                        in_=k.f_qw[i].rearrange("(oc p) d -> oc p d", p=128)[oc])))
                ops.append(dict(eng="v", name=f"dqa{i}_{oc}",
                    waits=W(f"wq{i}_{oc}", f"ws{i}" if oc == 0 else None,
                            f"dq{i}_{oc - 1}" if oc >= 1 else
                            (f"dq{i - 1}_7" if i >= 1 else None)),
                    emit=lambda v: v.tensor_scalar(
                        wtmp[:], qb[:], 1.0 / 7.5, -1.0, ALU.mult, ALU.add)))
                ops.append(dict(eng="v", name=f"dq{i}_{oc}",
                    waits=W(f"dqa{i}_{oc}",
                            f"tp{i}_{oc - 1}_7" if oc >= 1 else
                            (f"tp{i - 1}_7_7" if i >= 1 else None)),
                    emit=lambda v, oc=oc: v.tensor_tensor(
                        wod[:], wtmp[:],
                        AP(scs, oc * 64, [[8 * 64, 128], [1, 64], [0, 16]]),
                        ALU.mult)))
                for dc in range(8):
                    ops.append(dict(eng="t", name=f"tp{i}_{oc}_{dc}",
                        waits=W(f"dq{i}_{oc}" if dc == 0 else None,
                                f"tr{i}_{oc}_{dc - 1}" if dc >= 1 else
                                (f"tr{i}_{oc - 1}_7" if oc >= 1 else
                                 (f"tr{i - 1}_7_7" if i >= 1 else None))),
                        emit=lambda tn, dc=dc: tn.transpose(
                            pT[:], wod[:, dc * 128:(dc + 1) * 128], k.ident[:])))
                    ops.append(dict(eng="v", name=f"tr{i}_{oc}_{dc}",
                        waits=[f"tp{i}_{oc}_{dc}"],
                        emit=lambda v, i=i, oc=oc, dc=dc: v.tensor_copy(
                            wT[:, i * 8192 + dc * 1024 + oc * 128:
                               i * 8192 + dc * 1024 + (oc + 1) * 128], pT[:])))
            ops.append(dict(eng="gp", name=f"wla{i}",
                waits=[f"dqa{i}_7"],
                emit=lambda gp, i=i: gp.dma_start(out=qb[0:32, :], in_=k.f_la[i])))
            for dc in range(8):
                ops.append(dict(eng="t", name=f"lap{i}_{dc}",
                    waits=W(f"wla{i}" if dc == 0 else None,
                            f"lat{i}_{dc - 1}" if dc >= 1 else
                            (f"lat{i - 1}_7" if i >= 1 else None),
                            f"tp{i}_7_7" if dc == 0 else None),
                    emit=lambda tn, dc=dc: tn.transpose(
                        pT[:, 0:32], qb[0:32, dc * 128:(dc + 1) * 128],
                        k.ident[0:32, 0:32])))
                ops.append(dict(eng="v", name=f"lat{i}_{dc}",
                    waits=[f"lap{i}_{dc}"],
                    emit=lambda v, i=i, dc=dc: v.tensor_copy(
                        laT[:, i * 256 + dc * 32:i * 256 + (dc + 1) * 32],
                        pT[:, 0:32])))
            ops.append(dict(eng="gp", name=f"wlb{i}",
                waits=W(f"lbt{i - 1}_7" if i >= 1 else None),
                emit=lambda gp, i=i: gp.dma_start(
                    out=lbsc.rearrange("p (oc r) -> p oc r", oc=8),
                    in_=k.f_lb[i].rearrange("(oc p) r -> p oc r", p=128))))
            for oc in range(8):
                ops.append(dict(eng="t", name=f"lbp{i}_{oc}",
                    waits=W(f"wlb{i}" if oc == 0 else None,
                            f"lbt{i}_{oc - 1}" if oc >= 1 else
                            (f"lbt{i - 1}_7" if i >= 1 else None),
                            f"lap{i}_7" if oc == 0 else None),
                    emit=lambda tn, oc=oc: tn.transpose(
                        pT[0:32, :], lbsc[:, oc * 32:(oc + 1) * 32], k.ident[:])))
                ops.append(dict(eng="v", name=f"lbt{i}_{oc}",
                    waits=W(f"lbp{i}_{oc}", "z0" if i == 0 and oc == 0 else None),
                    emit=lambda v, i=i, oc=oc: v.tensor_copy(
                        lbT[0:32, i * 1024 + oc * 128:i * 1024 + (oc + 1) * 128],
                        pT[0:32, 0:128])))
        for i6 in range(6):
            for oc in range(8):
                ops.append(dict(eng="sy", name=f"bi{i6}_{oc}",
                    emit=lambda sy, i6=i6, oc=oc: sy.dma_start(
                        out=bi_sb[:, i6 * 8 + oc:i6 * 8 + oc + 1],
                        in_=k.f_bi[i6].rearrange("(a b) -> a b", b=1)[
                            128 * oc:128 * (oc + 1), :])))
        for b3 in range(3):
            for oc in range(8):
                ops.append(dict(eng="sy", name=f"lg{b3}_{oc}",
                    emit=lambda sy, b3=b3, oc=oc: sy.dma_start(
                        out=lg_sb[:, b3 * 8 + oc:b3 * 8 + oc + 1],
                        in_=k.f_lg[b3].rearrange("(a b) -> a b", b=1)[
                            128 * oc:128 * (oc + 1), :])))
                ops.append(dict(eng="sy", name=f"lbe{b3}_{oc}",
                    emit=lambda sy, b3=b3, oc=oc: sy.dma_start(
                        out=lbe_sb[:, b3 * 8 + oc:b3 * 8 + oc + 1],
                        in_=k.f_lbe[b3].rearrange("(a b) -> a b", b=1)[
                            128 * oc:128 * (oc + 1), :])))

        def emit_qlora(i, b3, qi, src, dst, extra_in_wait, jj=None):
            if jj is None:
                jj = i % 2
            tag = f"i{i}b{b3}q{qi % 2}"
            for dc in range(8):
                ops.append(dict(eng="t", name=f"u{tag}_{dc}",
                    waits=W(extra_in_wait if dc == 0 else None,
                            f"ue{tag}_prevfree" if False else None),
                    emit=lambda tn, qi=qi, dc=dc, src=src: tn.matmul(
                        pU[0:32, :],
                        laT[:, qi * 256 + dc * 32:qi * 256 + (dc + 1) * 32],
                        src[:, dc * 512:(dc + 1) * 512],
                        start=(dc == 0), stop=(dc == 7))))
            ops.append(dict(eng="v", name=f"ue{tag}",
                waits=W(f"u{tag}_7", "z1" if (i == 0 and b3 == 0 and qi == 0)
                        else None),
                emit=lambda v: v.tensor_copy(usb[0:32, :], pU[0:32, :])))
            for oc in range(8):
                g = oc % 2
                for dc in range(8):
                    ops.append(dict(eng="t", name=f"m{tag}_{oc}_{dc}",
                        waits=W((f"ev{tag}_{oc - 2}" if oc >= 2 else None)
                                if dc == 0 else None),
                        emit=lambda tn, qi=qi, oc=oc, dc=dc, g=g, src=src:
                        tn.matmul(pA[g][:],
                                  wT[:, qi * 8192 + dc * 1024 + oc * 128:
                                     qi * 8192 + dc * 1024 + (oc + 1) * 128],
                                  src[:, dc * 512:(dc + 1) * 512],
                                  start=(dc == 0), stop=False)))
                ops.append(dict(eng="t", name=f"ml{tag}_{oc}",
                    waits=W(f"ue{tag}" if oc == 0 else None),
                    emit=lambda tn, qi=qi, oc=oc, g=g: tn.matmul(
                        pA[g][:],
                        lbT[:, qi * 1024 + oc * 128:qi * 1024 + (oc + 1) * 128],
                        usb[:], start=False, stop=True)))
                if dst == "y1":
                    ops.append(dict(eng="s", name=f"ev{tag}_{oc}",
                        waits=[f"ml{tag}_{oc}"],
                        emit=lambda sc, qi=qi, oc=oc, g=g: sc.activation(
                            y1[:, oc * 512:(oc + 1) * 512], pA[g][:], AF.Identity,
                            bias=bi_sb[:, qi * 8 + oc:qi * 8 + oc + 1], scale=1.0)))
                else:
                    ops.append(dict(eng="v", name=f"ev{tag}_{oc}",
                        waits=[f"ml{tag}_{oc}"],
                        emit=lambda v, qi=qi, oc=oc, g=g, jj=jj:
                        v.scalar_tensor_tensor(
                            hf[1 - jj][:, oc * 512:(oc + 1) * 512], pA[g][:],
                            bi_sb[:, qi * 8 + oc:qi * 8 + oc + 1],
                            hf[jj][:, oc * 512:(oc + 1) * 512], ALU.add, ALU.add)))
            return tag

        for i in range(NO512):
            jj = i % 2
            for dc in range(8):
                ops.append(dict(eng="sy", name=f"x{i}_{dc}",
                    waits=W((f"xmld{i - 1}" if i >= 1 else "tr5_7_7")
                            if dc == 0 else None),
                    emit=lambda sy, i=i, dc=dc: sy.dma_start(
                        out=hb[:, dc * 512:(dc + 1) * 512],
                        in_=k.xln2_buf[i * 512:(i + 1) * 512,
                                       dc * 128:(dc + 1) * 128], transpose=True)))
            ops.append(dict(eng="v", name=f"hcast{i}",
                waits=W(f"x{i}_7", f"evi{i - 1}b2done" if i >= 1 else None,
                        f"pbb{i - 1}_7" if i >= 1 else None,
                        f"psq{i - 1}_7" if i >= 1 else None,
                        f"psm1{i - 1}_7" if i >= 1 else None),
                emit=lambda v, jj=jj: v.tensor_copy(hf[jj][:], hb[:])))
            for b3 in range(3):
                qi0, qi1 = 2 * b3, 2 * b3 + 1
                tagp = f"i{i}b{b3}"
                src_a = hb if b3 == 0 else hx
                in_wait = f"hcast{i}" if b3 == 0 else f"hbc{i}_{b3 - 1}"
                tA = emit_qlora(i, b3, qi0, src_a, "y1", in_wait)
                for oc in range(8):
                    ops.append(dict(eng="s", name=f"sq{tagp}_{oc}",
                        waits=W(f"ev{tA}_{oc}",
                                f"sm2{tagp}_{oc - 2}" if oc >= 2 else None),
                        emit=lambda sc, oc=oc: sc.square(
                            sqb[:, (oc % 2) * 512:(oc % 2 + 1) * 512],
                            y1[:, oc * 512:(oc + 1) * 512])))
                    ops.append(dict(eng="t", name=f"sm1{tagp}_{oc}",
                        waits=[f"ev{tA}_{oc}"],
                        emit=lambda tn, oc=oc: tn.matmul(
                            stP[0][:], k.ones_f[:],
                            y1[:, oc * 512:(oc + 1) * 512],
                            start=(oc == 0), stop=(oc == 7),
                            skip_group_check=True)))
                    ops.append(dict(eng="t", name=f"sm2{tagp}_{oc}",
                        waits=[f"sq{tagp}_{oc}"],
                        emit=lambda tn, oc=oc: tn.matmul(
                            stP[1][:], k.ones_bf[:],
                            sqb[:, (oc % 2) * 512:(oc % 2 + 1) * 512],
                            start=(oc == 0), stop=(oc == 7),
                            skip_group_check=True)))
                if getattr(self, "dbg_p5", False) and i == 0 and b3 == 0:
                    dbg5 = self.nc.dram_tensor("p5dbg", [3, 128, 4096], F32).ap()
                    self._p5dbg = dbg5
                    ops.append(dict(eng="sy", name="dbgy1", waits=[f"ev{tA}_7"],
                        emit=lambda sy: sy.dma_start(out=dbg5[0], in_=y1[:])))
                    ops.append(dict(eng="sy", name="dbgu",
                        waits=[f"ue{tA}" if False else f"uei0b0q0"],
                        emit=lambda sy: sy.dma_start(
                            out=dbg5[1, :, 0:256].bitcast(BF16),
                            in_=usb[:])))
                    ops.append(dict(eng="sy", name="dbghb", waits=[f"hcast0"],
                        emit=lambda sy: sy.dma_start(
                            out=dbg5[2, :, 0:2048].bitcast(BF16), in_=hb[:])))
                    dbgw = self.nc.dram_tensor("p5dbgw", [128, 8192], BF16).ap()
                    self._p5dbgw = dbgw
                    ops.append(dict(eng="sy", name="dbgwt3", waits=["tr3_7_7"],
                        emit=lambda sy: sy.dma_start(
                            out=dbg5c[1, :, 0:4096].bitcast(BF16),
                            in_=wT[:, 3 * 8192:3 * 8192 + 8192])))
                    ops.append(dict(eng="sy", name="dbgwt", waits=["tr0_7_7"],
                        emit=lambda sy: sy.dma_start(
                            out=dbgw[:, :], in_=wT[:, 0:8192])))
                    dbg5b = self.nc.dram_tensor("p5dbg2", [2, 128, 4096], F32).ap()
                    self._p5dbg2 = dbg5b
                    ops.append(dict(eng="sy", name="dbggel", waits=["gelxi0b0"],
                        emit=lambda sy: sy.dma_start(
                            out=dbg5b[0, :, 0:2048].bitcast(BF16), in_=hb[:])))
                    ops.append(dict(eng="sy", name="dbghf", waits=["evi0b0done"],
                        emit=lambda sy: sy.dma_start(
                            out=dbg5b[1, :, :], in_=hf[1][:])))
                    dbg5c = self.nc.dram_tensor("p5dbg3", [2, 128, 4096], F32).ap()
                    self._p5dbg3 = dbg5c
                    ops.append(dict(eng="gp", name="zzc", dma=False,
                        emit=lambda gp: gp.memset(hx[0:1, 0:1], 0.0)))
                    dbg5d = self.nc.dram_tensor("p5dbg4", [2, 128, 4096], F32).ap()
                    self._p5dbg4 = dbg5d
                    ops.append(dict(eng="sy", name="dbghx", waits=["hbc0_0"],
                        emit=lambda sy: sy.dma_start(
                            out=dbg5d[0, :, 0:2048].bitcast(BF16), in_=hx[:])))
                    ops.append(dict(eng="sy", name="dbgy1b1", waits=["evi0b1q0_7"],
                        emit=lambda sy: sy.dma_start(
                            out=dbg5d[1, :, :], in_=y1[:])))
                    ops.append(dict(eng="sy", name="dbghf1b", waits=["evi0b1done"],
                        emit=lambda sy: sy.dma_start(
                            out=dbg5c[0, :, :], in_=hf[0][:])))
                ops.append(dict(eng="v", name=f"mu{tagp}", waits=[f"sm1{tagp}_7"],
                    emit=lambda v: v.tensor_scalar(murow[:], stP[0][0:1, :],
                                                   1.0 / D, None, ALU.mult)))
                ops.append(dict(eng="v", name=f"mu2{tagp}", waits=[f"mu{tagp}"],
                    emit=lambda v: v.tensor_mul(rowA[:], murow[:], murow[:])))
                ops.append(dict(eng="v", name=f"va{tagp}",
                    waits=[f"sm2{tagp}_7", f"mu2{tagp}"],
                    emit=lambda v: v.scalar_tensor_tensor(rowB[:], stP[1][0:1, :],
                        1.0 / D, rowA[:], ALU.mult, ALU.subtract)))
                ops.append(dict(eng="s", name=f"lnv{tagp}", waits=[f"va{tagp}"],
                    emit=lambda sc: sc.activation(rowA[:], rowB[:], AF.Ln,
                                                  bias=k.eps_t[0:1, 0:1],
                                                  scale=1.0)))
                ops.append(dict(eng="s", name=f"exv{tagp}", waits=[f"lnv{tagp}"],
                    emit=lambda sc: sc.activation(rsr[:], rowA[:], AF.Exp,
                                                  bias=0.0, scale=-0.5)))
                ops.append(dict(eng="t", name=f"mb{tagp}", waits=[f"mu{tagp}"],
                    emit=lambda tn: tn.matmul(bcP[0][:], k.onesf[:], murow[:],
                                              start=True, stop=True)))
                ops.append(dict(eng="t", name=f"rb{tagp}", waits=[f"exv{tagp}"],
                    emit=lambda tn: tn.matmul(bcP[1][:], k.onesf[:], rsr[:],
                                              start=True, stop=True)))
                for oc in range(8):
                    ops.append(dict(eng="v", name=f"ga{tagp}_{oc}",
                        waits=W(f"mb{tagp}" if oc == 0 else None,
                                f"rb{tagp}" if oc == 0 else None,
                                f"gel{tagp}_{oc - 1}" if oc >= 1 else None),
                        emit=lambda v, oc=oc: v.tensor_sub(
                            t1[:], y1[:, oc * 512:(oc + 1) * 512], bcP[0][:])))
                    ops.append(dict(eng="v", name=f"gb{tagp}_{oc}",
                        waits=[f"ga{tagp}_{oc}"],
                        emit=lambda v: v.tensor_mul(t1[:], t1[:], bcP[1][:])))
                    ops.append(dict(eng="s", name=f"gel{tagp}_{oc}",
                        waits=W(f"gb{tagp}_{oc}",
                                f"m{tA}_7_7" if oc == 0 else None),
                        emit=lambda sc, b3=b3, oc=oc: sc.activation(
                            hb[:, oc * 512:(oc + 1) * 512], t1[:], AF.Gelu,
                            bias=lbe_sb[:, b3 * 8 + oc:b3 * 8 + oc + 1],
                            scale=lg_sb[:, b3 * 8 + oc:b3 * 8 + oc + 1])))
                ops.append(dict(eng="s", name=f"gelx{tagp}", inc=1,
                    waits=[f"gel{tagp}_7"], emit=lambda sc: sc.nop()))
                tB = emit_qlora(i, b3, qi1, hb, "res", f"gelx{tagp}", jj=jj)
                ops.append(dict(eng="v", name=f"evi{i}b{b3}done", inc=1, dma=False,
                    waits=[f"ev{tB}_7"], emit=lambda v: v.engine_nop()))
                jj = 1 - jj
                if b3 < 2:
                    dbgw2 = ["dbghx", "dbgy1b1", "dbghf1b"] if (
                        getattr(self, "dbg_p5", False) and i == 0 and b3 == 1) else []
                    if getattr(self, "dbg_p5", False) and i == 0 and b3 == 1:
                        pass
                    ops.append(dict(eng="v", name=f"hbc{i}_{b3}",
                        waits=W(f"evi{i}b{b3}done",
                                f"m{f'i{i}b{b3}q0'}_7_7" if b3 >= 1 else None,
                                f"u{f'i{i}b{b3}q0'}_7" if b3 >= 1 else None,
                                *dbgw2),
                        emit=lambda v, jj=jj: v.tensor_copy(hx[:], hf[jj][:])))
            ops.append(dict(eng="t", name=f"xmld{i}", inc=1, dma=False,
                waits=[f"evi{i}b2done"], emit=lambda tn: tn.nop()))
            for oc in range(8):
                ops.append(dict(eng="s", name=f"psq{i}_{oc}",
                    waits=W(f"evi{i}b2done",
                            f"psm2{i}_{oc - 2}" if oc >= 2 else None),
                    emit=lambda sc, jj=jj, oc=oc: sc.square(
                        sqb[:, (oc % 2) * 512:(oc % 2 + 1) * 512],
                        hf[jj][:, oc * 512:(oc + 1) * 512])))
                ops.append(dict(eng="t", name=f"psm1{i}_{oc}",
                    waits=W(f"evi{i}b2done" if oc == 0 else None),
                    emit=lambda tn, jj=jj, oc=oc: tn.matmul(
                        stP[0][:], k.ones_f[:],
                        hf[jj][:, oc * 512:(oc + 1) * 512],
                        start=(oc == 0), stop=(oc == 7), skip_group_check=True)))
                ops.append(dict(eng="t", name=f"psm2{i}_{oc}",
                    waits=[f"psq{i}_{oc}"],
                    emit=lambda tn, oc=oc: tn.matmul(
                        stP[1][:], k.ones_bf[:],
                        sqb[:, (oc % 2) * 512:(oc % 2 + 1) * 512],
                        start=(oc == 0), stop=(oc == 7), skip_group_check=True)))
            ops.append(dict(eng="v", name=f"pmu{i}", waits=[f"psm1{i}_7"],
                emit=lambda v: v.tensor_scalar(murow[:], stP[0][0:1, :], 1.0 / D,
                                               None, ALU.mult)))
            ops.append(dict(eng="v", name=f"pmu2{i}", waits=[f"pmu{i}"],
                emit=lambda v: v.tensor_mul(rowA[:], murow[:], murow[:])))
            ops.append(dict(eng="v", name=f"pva{i}", waits=[f"psm2{i}_7", f"pmu2{i}"],
                emit=lambda v: v.scalar_tensor_tensor(rowB[:], stP[1][0:1, :],
                    1.0 / D, rowA[:], ALU.mult, ALU.subtract)))
            ops.append(dict(eng="s", name=f"pln{i}", waits=[f"pva{i}"],
                emit=lambda sc: sc.activation(rowA[:], rowB[:], AF.Ln,
                                              bias=k.eps_t[0:1, 0:1], scale=1.0)))
            ops.append(dict(eng="s", name=f"pex{i}", waits=[f"pln{i}"],
                emit=lambda sc: sc.activation(rsr[:], rowA[:], AF.Exp, bias=0.0,
                                              scale=-0.5)))
            ops.append(dict(eng="t", name=f"pmb{i}", waits=[f"pmu{i}"],
                emit=lambda tn: tn.matmul(bcP[0][:], k.onesf[:], murow[:],
                                          start=True, stop=True)))
            ops.append(dict(eng="t", name=f"prb{i}", waits=[f"pex{i}"],
                emit=lambda tn: tn.matmul(bcP[1][:], k.onesf[:], rsr[:],
                                          start=True, stop=True)))
            for oc in range(8):
                no = i * 8 + oc
                ops.append(dict(eng="v", name=f"pba{i}_{oc}",
                    waits=W(f"pmb{i}" if oc == 0 else None,
                            f"prb{i}" if oc == 0 else None,
                            f"pbb{i}_{oc - 1}" if oc >= 1 else None),
                    emit=lambda v, jj=jj, oc=oc: v.tensor_sub(
                        t1[:], hf[jj][:, oc * 512:(oc + 1) * 512], bcP[0][:])))
                ops.append(dict(eng="v", name=f"pbb{i}_{oc}",
                    waits=W(f"pba{i}_{oc}",
                            f"bst{no - 1}" if no >= 1 else None),
                    emit=lambda v: v.tensor_mul(brt[:], t1[:], bcP[1][:])))
                ops.append(dict(eng="sy", name=f"bst{no}",
                    waits=[f"pbb{i}_{oc}"],
                    emit=lambda sy, i=i, oc=oc: sy.dma_start(
                        out=k.br_buf[oc * 128:(oc + 1) * 128,
                                     i * 512:(i + 1) * 512], in_=brt[:])))
        self.run_ops(ops, "p5")
        ctx.close()
        self.barrier()
        self._res_pass(self.x1_buf, self.br_buf, self.x2_buf, self.cc2_in, "p5b")
        self.s_cc2 = self.sem("cc2")
        s_cc2 = self.s_cc2
        k2 = self

        @blk.gpsimd
        def _(gp):
            gp.wait_ge(self._res_done_sem, self._res_done_tgt)
            gp.collective_compute(
                "AllGather", ALU.bypass,
                replica_groups=[[0, 1], [2, 3], [4, 5], [6, 7]],
                ins=[k2.cc2_in.opt()], outs=[k2.cc2_out.opt()],
            ).then_inc(s_cc2, 1)

    def _res_pass(self, x_src, br_src, x_out, xbf_out, tag):
        k = self
        blk = self.block
        ctx = contextlib.ExitStack()
        nt = TOWN // 128
        xin = [self.sbuf(f"{tag}_xin{j}", [128, D], F32, ctx) for j in range(2)]
        brt = [self.sbuf(f"{tag}_brt{j}", [128, D], BF16, ctx) for j in range(2)]
        xrs = [self.sbuf(f"{tag}_xrs{j}", [128, D], F32, ctx) for j in range(2)]
        xbf = [self.sbuf(f"{tag}_xbf{j}", [128, D], BF16, ctx) for j in range(2)]
        s_ld = self.sem(f"{tag}ld")
        s_ad = self.sem(f"{tag}ad")
        s_sd = self.sem(f"{tag}sd")

        has_bf = xbf_out is not None
        spt = 32 if has_bf else 16
        apt = 2 if has_bf else 1

        @blk.sync
        def _(sy):
            for i in range(nt):
                j = i % 2
                if i >= 2:
                    sy.wait_ge(s_ad, apt * (i - 1))
                sy.dma_start(out=xin[j][:], in_=x_src[i * 128:(i + 1) * 128, :]
                             ).then_inc(s_ld, 16)
                sy.dma_start(out=brt[j][:], in_=br_src[:, i * 128:(i + 1) * 128],
                             transpose=True).then_inc(s_ld, 16)
                if i > 0:
                    sy.wait_ge(s_ad, apt * i)
                    sy.dma_start(out=x_out[(i - 1) * 128:i * 128, :],
                                 in_=xrs[(i - 1) % 2][:]).then_inc(s_sd, 16)
                    if has_bf:
                        sy.dma_start(out=xbf_out[(i - 1) * 128:i * 128, :],
                                     in_=xbf[(i - 1) % 2][:]).then_inc(s_sd, 16)
            sy.wait_ge(s_ad, apt * nt)
            sy.dma_start(out=x_out[(nt - 1) * 128:nt * 128, :],
                         in_=xrs[(nt - 1) % 2][:]).then_inc(s_sd, 16)
            if has_bf:
                sy.dma_start(out=xbf_out[(nt - 1) * 128:nt * 128, :],
                             in_=xbf[(nt - 1) % 2][:]).then_inc(s_sd, 16)
            sy.wait_ge(s_sd, spt * nt)

        @blk.vector
        def _(v):
            for i in range(nt):
                j = i % 2
                v.wait_ge(s_ld, 32 * (i + 1))
                if i >= 2:
                    v.wait_ge(s_sd, spt * (i - 1))
                v.tensor_add(xrs[j][:], xin[j][:], brt[j][:]).then_inc(s_ad, 1)
                if has_bf:
                    v.wait_ge(s_ad, 2 * i + 1)
                    v.tensor_copy(xbf[j][:], xrs[j][:]).then_inc(s_ad, 1)
        self._res_done_sem = s_sd
        self._res_done_tgt = spt * nt
        ctx.close()

    def p6_proj1(self):
        k = self
        blk = self.block
        ctx0 = contextlib.ExitStack()
        self._p6_gate(ctx0)
        self._ln_pass(self.cc2_out, T, out_bf=self.xln3_buf, tag="p6a",
                      src_dt=BF16)
        self.barrier()
        ctx0.close()
        ctx = contextlib.ExitStack()
        wsrc = self.sbuf("p6_wsrc", [128, 4 * 1024], BF16, ctx)
        wqT = self.sbuf("p6_wqT", [128, 8 * 512], BF16, ctx)
        wkT = self.sbuf("p6_wkT", [128, 8 * 512], BF16, ctx)
        wvT = self.sbuf("p6_wvT", [128, 8 * 512], BF16, ctx)
        xf = self.sbuf("p6_xf", [128, 8 * 512], BF16, ctx)
        zmin = self.sbuf("p6_zmin", [128, 512], F32, ctx)
        ex = self.sbuf("p6_ex", [128, 512], F32, ctx)
        fmo = [self.sbuf(f"p6_fm{j}", [128, 512], BF16, ctx) for j in range(2)]
        kpl = self.sbuf("p6_kpl", [128, 512], BF16, ctx)
        ktr = [self.sbuf(f"p6_ktr{j}", [128, 128], BF16, ctx) for j in range(2)]
        pP = [self.psum(f"p6_p{j}", [128, 512], F32, ctx) for j in range(2)]
        pT = self.psum("p6_pT", [128, 128], BF16, ctx)
        ops = []
        W = lambda *names: [w for w in names if w is not None]
        for wi, (wsb, wnm) in enumerate([(wqT, "w_q1"), (wkT, "w_k1"),
                                         (wvT, "w_v1")]):
            ops.append(dict(eng="gp", name=f"wl{wi}",
                waits=W(f"wt{wi - 1}_3_7" if wi >= 1 else None),
                emit=lambda gp, wnm=wnm: gp.dma_start(
                    out=wsrc.rearrange("p (rc d) -> p rc d", rc=4),
                    in_=getattr(k, wnm).rearrange("(rc p) d -> p rc d", p=128))))
            for rc in range(4):
                for dc in range(8):
                    ops.append(dict(eng="t", name=f"wp{wi}_{rc}_{dc}",
                        waits=W(f"wl{wi}" if rc == 0 and dc == 0 else None,
                                f"wt{wi}_{rc}_{dc - 1}" if dc >= 1 else
                                (f"wt{wi}_{rc - 1}_7" if rc >= 1 else
                                 (f"wt{wi - 1}_3_7" if wi >= 1 else None))),
                        emit=lambda tn, rc=rc, dc=dc: tn.transpose(
                            pT[:], wsrc[:, rc * 1024 + dc * 128:
                                        rc * 1024 + (dc + 1) * 128], k.ident[:])))
                    ops.append(dict(eng="v", name=f"wt{wi}_{rc}_{dc}",
                        waits=[f"wp{wi}_{rc}_{dc}"],
                        emit=lambda v, wsb=wsb, rc=rc, dc=dc: v.tensor_copy(
                            wsb[:, dc * 512 + rc * 128:dc * 512 + (rc + 1) * 128],
                            pT[:])))
        nq = 0
        prev_kk = [None]
        prev_fe = [None]
        cons = {}
        for i in range(NT512):
            for dc in range(8):
                ops.append(dict(eng="sy", name=f"x{i}_{dc}",
                    waits=W((f"vst{i - 1}_3" if i >= 1 else "wt2_3_7")
                            if dc == 0 else None),
                    emit=lambda sy, i=i, dc=dc: sy.dma_start(
                        out=xf[:, dc * 512:(dc + 1) * 512],
                        in_=k.xln3_buf[i * 512:(i + 1) * 512,
                                       dc * 128:(dc + 1) * 128], transpose=True)))
            for which, wsb, buf in [("q", wqT, k.qp_buf), ("kk", wkT, k.kp_buf)]:
                for rc in range(4):
                    tgq = f"{which}{i}_{rc}"
                    g = nq % 2
                    for dc in range(8):
                        ops.append(dict(eng="t", name=f"mm{tgq}_{dc}",
                            waits=W((f"x{i}_7" if rc == 0 and which == "q" else None)
                                    if dc == 0 else None,
                                    (f"fm{nq - 2}" if nq >= 2 else None)
                                    if dc == 0 else None),
                            emit=lambda tn, wsb=wsb, rc=rc, dc=dc, g=g: tn.matmul(
                                pP[g][:],
                                wsb[:, dc * 512 + rc * 128:dc * 512 + (rc + 1) * 128],
                                xf[:, dc * 512:(dc + 1) * 512],
                                start=(dc == 0), stop=(dc == 7))))
                    ops.append(dict(eng="v", name=f"fa{nq}",
                        waits=W(f"mm{tgq}_7", prev_fe[0]),
                        emit=lambda v, g=g: v.tensor_scalar(
                            zmin[:], pP[g][:], 0.0, None, ALU.min)))
                    ops.append(dict(eng="s", name=f"fe{nq}", waits=[f"fa{nq}"],
                        emit=lambda sc: sc.activation(ex[:], zmin[:], AF.Exp,
                                                      bias=0.0, scale=1.0)))
                    prev_fe[0] = f"fe{nq}"
                    ops.append(dict(eng="v", name=f"fm{nq}",
                        waits=W(f"fe{nq}", cons.get(nq - 2)),
                        emit=lambda v, g=g, jf=nq % 2: v.scalar_tensor_tensor(
                            fmo[jf][:], pP[g][:], 0.0, ex[:], ALU.max, ALU.add)))
                    for hh in range(2):
                        ops.append(dict(eng="sy", name=f"fst{nq}_{hh}",
                            waits=W(f"fm{nq}" if hh == 0 else None),
                            emit=lambda sy, buf=buf, rc=rc, hh=hh, i=i, jf=nq % 2:
                            sy.dma_start(
                                out=buf[2 * rc + hh, :, i * 512:(i + 1) * 512],
                                in_=fmo[jf][64 * hh:64 * (hh + 1), :])))
                    cons[nq] = f"fst{nq}_1"
                    if which == "kk":
                        pk = prev_kk[0]
                        ops.append(dict(eng="v", name=f"kc{nq}",
                            waits=W(f"fm{nq}",
                                    f"ktp{pk}_3" if pk is not None else None),
                            emit=lambda v, jf=nq % 2: v.tensor_copy(
                                kpl[:], fmo[jf][:])))
                        for tt in range(4):
                            ops.append(dict(eng="t", name=f"ktp{nq}_{tt}",
                                waits=W(f"kc{nq}" if tt == 0 else None,
                                        f"kte{nq}_{tt - 1}" if tt >= 1 else
                                        (f"kte{pk}_3" if pk is not None else None)),
                                emit=lambda tn, tt=tt: tn.transpose(
                                    pT[:], kpl[:, tt * 128:(tt + 1) * 128],
                                    k.ident[:])))
                            ops.append(dict(eng="v", name=f"kte{nq}_{tt}",
                                waits=W(f"ktp{nq}_{tt}",
                                        f"kst{nq}_{tt - 1}_1" if tt >= 1 else
                                        (f"kst{pk}_3_1" if pk is not None else None)),
                                emit=lambda v, jt=tt % 2: v.tensor_copy(
                                    ktr[jt][:], pT[:])))
                            for hh in range(2):
                                ops.append(dict(eng="sy", name=f"kst{nq}_{tt}_{hh}",
                                    waits=W(f"kte{nq}_{tt}" if hh == 0 else None),
                                    emit=lambda sy, rc=rc, tt=tt, hh=hh, i=i,
                                           jt=tt % 2: sy.dma_start(
                                        out=k.kpt_buf[2 * rc + hh,
                                                      i * 512 + tt * 128:
                                                      i * 512 + (tt + 1) * 128, :],
                                        in_=ktr[jt][:, 64 * hh:64 * (hh + 1)])))
                        prev_kk[0] = nq
                    nq += 1
            for st in range(4):
                g = nq % 2
                for dc in range(8):
                    ops.append(dict(eng="t", name=f"vm{i}_{st}_{dc}",
                        waits=W((f"fm{nq - 2}" if nq >= 2 else None)
                                if dc == 0 else None),
                        emit=lambda tn, st=st, dc=dc, g=g: tn.matmul(
                            pP[g][:], xf[:, dc * 512 + st * 128:dc * 512 + (st + 1) * 128],
                            wvT[:, dc * 512:(dc + 1) * 512],
                            start=(dc == 0), stop=(dc == 7))))
                ops.append(dict(eng="v", name=f"fm{nq}",
                    waits=W(f"vm{i}_{st}_7", cons.get(nq - 2)),
                    emit=lambda v, g=g, jf=nq % 2: v.tensor_copy(fmo[jf][:],
                                                                 pP[g][:])))
                ops.append(dict(eng="sy", name=f"vst{i}_{st}",
                    waits=[f"fm{nq}"],
                    emit=lambda sy, i=i, st=st, jf=nq % 2: sy.dma_start(
                        out=k.v1_buf[i * 512 + st * 128:
                                     i * 512 + (st + 1) * 128, :],
                        in_=fmo[jf][:])))
                cons[nq] = f"vst{i}_{st}"
                nq += 1
        self.run_ops(ops, "p6")
        ctx.close()

    def _p6_gate(self, ctx):
        blk = self.block
        s_cc2 = self.s_cc2

        @blk.sync
        def _(sy):
            sy.wait_ge(s_cc2, 1)

    def p7_attn1(self):
        k = self
        blk = self.block
        ctx = contextlib.ExitStack()
        NCH = T // 128
        qpr = [self.sbuf(f"p7_qp{j}", [128, 8 * 128], BF16, ctx) for j in range(2)]
        kpr = [self.sbuf(f"p7_kp{j}", [128, 8 * 128], BF16, ctx) for j in range(2)]
        ktr = [self.sbuf(f"p7_kt{j}", [128, 8 * 128], BF16, ctx) for j in range(2)]
        vr = [self.sbuf(f"p7_vr{j}", [128, 8 * 128], BF16, ctx) for j in range(2)]
        Ssb = self.sbuf("p7_S", [128, 8 * 128], BF16, ctx)
        ksc = self.sbuf("p7_ksc", [128, 8], BF16, ctx)
        ksr = self.sbuf("p7_ksr", [128, 8 * 128], BF16, ctx)
        Pl = [self.sbuf(f"p7_pl{j}", [128, 128], BF16, ctx) for j in range(2)]
        dtmp = [self.sbuf(f"p7_dt{j}", [64, 128], F32, ctx) for j in range(2)]
        dinv = [self.sbuf(f"p7_di{j}", [64, 128], F32, ctx) for j in range(2)]
        ysb = [self.sbuf(f"p7_y{j}", [128, 4 * 128], BF16, ctx) for j in range(2)]
        scP = self.psum("p7_sc", [128, 2 * 128], F32, ctx)
        avP = self.psum("p7_av", [128, 8 * 128], F32, ctx)
        dnP = self.psum("p7_dn", [128, 8 * 128], F32, ctx)
        SP = self.psum("p7_SP", [128, 512], F32, ctx)
        kP = self.psum("p7_kP", [128, 8], F32, ctx)
        ops = []
        W = lambda *names: [w for w in names if w is not None]
        for j in range(2):
            for t_ in (qpr, kpr, ktr, vr):
                ops.append(dict(eng="gp", name=f"z{id(t_) % 997}_{j}", dma=False,
                    emit=lambda gp, t_=t_, j=j: gp.memset(t_[j][:], 0.0)))
        ops.append(dict(eng="gp", name="zS", dma=False,
                        emit=lambda gp: gp.memset(Ssb[:], 0.0)))
        ops.append(dict(eng="gp", name="zksr", dma=False,
                        emit=lambda gp: gp.memset(ksr[:], 0.0)))

        def loads(c, j):
            c0 = c * 128
            lw = W(f"ks{c - 2}_7" if c >= 2 else "zS",
                   "zksr" if c < 2 else None)
            ops.append(dict(eng="sy", name=f"lq{c}", waits=lw,
                emit=lambda sy, c0=c0, j=j: sy.dma_start(
                    out=AP(qpr[j], 0, [[1024, 64], [128, 8], [1, 128]]),
                    in_=AP(k.qp_buf.tensor, c0, [[T, 64], [64 * T, 8], [1, 128]]))))
            ops.append(dict(eng="sy", name=f"lk{c}", waits=[],
                emit=lambda sy, c0=c0, j=j: sy.dma_start(
                    out=AP(kpr[j], 0, [[1024, 64], [128, 8], [1, 128]]),
                    in_=AP(k.kp_buf.tensor, c0, [[T, 64], [64 * T, 8], [1, 128]]))))
            ops.append(dict(eng="sy", name=f"lt{c}", waits=[],
                emit=lambda sy, c0=c0, j=j: sy.dma_start(
                    out=AP(ktr[j], 0, [[1024, 128], [128, 8], [1, 64]]),
                    in_=AP(k.kpt_buf.tensor, c0 * 64,
                           [[64, 128], [T * 64, 8], [1, 64]]))))
            ops.append(dict(eng="sy", name=f"lv{c}", waits=[],
                emit=lambda sy, c0=c0, j=j: sy.dma_start(
                    out=AP(vr[j], 0, [[1024, 128], [128, 8], [1, 64]]),
                    in_=AP(k.v1_buf.tensor, c0 * 512,
                           [[512, 128], [64, 8], [1, 64]]))))

        loads(0, 0)
        loads(1, 1)
        for c in range(NCH):
            j = c % 2
            if c + 2 < NCH:
                loads(c + 2, j)
            for h in range(8):
                g = h % 2
                plprev = (f"pl{c}_{h - 2}" if h >= 2 else
                          (f"pl{c - 1}_{h + 6}" if c >= 1 else None))
                ops.append(dict(eng="t", name=f"sc{c}_{h}",
                    waits=W(f"lq{c}" if h == 0 else None,
                            f"lk{c}" if h == 0 else None, plprev),
                    emit=lambda tn, j=j, h=h, g=g: tn.matmul(
                        scP[:, g * 128:(g + 1) * 128],
                        kpr[j][:, h * 128:(h + 1) * 128],
                        qpr[j][:, h * 128:(h + 1) * 128],
                        start=True, stop=True, skip_group_check=True)))
                dnprev = (f"dn{c}_{h - 2}_0" if h >= 2 else
                          (f"dn{c - 1}_{h + 6}_0" if c >= 1 else None))
                ops.append(dict(eng="v", name=f"pl{c}_{h}",
                    waits=W(f"sc{c}_{h}", dnprev),
                    emit=lambda v, h=h, g=g, jp=h % 2: v.tensor_tensor(
                        Pl[jp][:], scP[:, g * 128:(g + 1) * 128],
                        k.causal01[:], ALU.mult)))
                ops.append(dict(eng="t", name=f"av{c}_{h}_0",
                    waits=W(f"pl{c}_{h}", f"lv{c}" if h == 0 else None,
                            f"yt{c - 1}_{h}" if c >= 1 else None),
                    emit=lambda tn, j=j, h=h, jp=h % 2: tn.matmul(
                        avP[:, h * 128:(h + 1) * 128],
                        vr[j][:, h * 128:(h + 1) * 128], Pl[jp][:],
                        start=True, stop=False, skip_group_check=True)))
                ops.append(dict(eng="t", name=f"av{c}_{h}_1",
                    waits=W(f"se{c - 1}" if c >= 1 else "zS" if h == 0 else None),
                    emit=lambda tn, j=j, h=h: tn.matmul(
                        avP[:, h * 128:(h + 1) * 128],
                        Ssb[:, h * 128:(h + 1) * 128],
                        qpr[j][:, h * 128:(h + 1) * 128],
                        start=False, stop=True, skip_group_check=True)))
                ops.append(dict(eng="t", name=f"dn{c}_{h}_0",
                    waits=W(f"pl{c}_{h}", f"dnv{c - 1}_{h}" if c >= 1 else None),
                    emit=lambda tn, h=h, jp=h % 2: tn.matmul(
                        dnP[:, h * 128:(h + 1) * 128], k.ones_bf[:], Pl[jp][:],
                        start=True, stop=False, skip_group_check=True)))
                ops.append(dict(eng="t", name=f"dn{c}_{h}_1",
                    waits=W(f"ke{c - 1}" if (c >= 1 and h == 0) else None,
                            f"kr{c - 1}" if (c >= 1 and h == 0) else None),
                    emit=lambda tn, j=j, h=h: tn.matmul(
                        dnP[:, h * 128:(h + 1) * 128],
                        ksr[:, h * 128:(h + 1) * 128],
                        qpr[j][:, h * 128:(h + 1) * 128],
                        start=False, stop=True, skip_group_check=True)))
                ops.append(dict(eng="t", name=f"st{c}_{h}",
                    waits=W(f"lt{c}" if h == 0 else None,
                            f"se{c - 1}" if (c >= 1 and h == 0) else None),
                    emit=lambda tn, j=j, h=h, st=(c == 0 and h == 0),
                           sp=(c == NCH - 1):
                    tn.matmul(
                        SP[:, h * 64:(h + 1) * 64],
                        ktr[j][:, h * 128:(h + 1) * 128],
                        vr[j][:, h * 128:h * 128 + 64],
                        start=st, stop=sp, skip_group_check=True)))
                ops.append(dict(eng="t", name=f"ks{c}_{h}",
                    waits=W(f"ke{c - 1}" if (c >= 1 and h == 0) else None),
                    emit=lambda tn, j=j, h=h, st=(c == 0 and h == 0),
                           sp=(c == NCH - 1):
                    tn.matmul(
                        kP[:, h:h + 1], ktr[j][:, h * 128:(h + 1) * 128],
                        k.ones_bf[:, 0:1],
                        start=st, stop=sp, skip_group_check=True)))
                drprev = (f"dnr{c}_{h - 2}" if h >= 2 else
                          (f"dnr{c - 1}_{h + 6}" if c >= 1 else None))
                ops.append(dict(eng="v", name=f"dnv{c}_{h}",
                    waits=W(f"dn{c}_{h}_1", drprev),
                    emit=lambda v, h=h, jp=h % 2: v.tensor_scalar(
                        dtmp[jp][:], dnP[0:64, h * 128:(h + 1) * 128], 1e-6,
                        None, ALU.add)))
                ytprev = (f"yt{c}_{h - 2}" if h >= 2 else
                          (f"yt{c - 1}_{h + 6}" if c >= 1 else None))
                ops.append(dict(eng="v", name=f"dnr{c}_{h}",
                    waits=W(f"dnv{c}_{h}", ytprev),
                    emit=lambda v, h=h, jp=h % 2: v.reciprocal(
                        dinv[jp][:], dtmp[jp][:])))
                ops.append(dict(eng="v", name=f"dni{c}_{h}", inc=1, dma=False,
                    waits=[f"dnr{c}_{h}"], emit=lambda v: v.engine_nop()))
                ops.append(dict(eng="v", name=f"yt{c}_{h}",
                    waits=W(f"dnr{c}_{h}", f"av{c}_{h}_1",
                            f"yst{c - 2}" if c >= 2 else None),
                    emit=lambda v, h=h, jp=h % 2, jy=c % 2: v.tensor_mul(
                        ysb[jy][64 * (h % 2):64 * (h % 2) + 64,
                                (h // 2) * 128:(h // 2 + 1) * 128],
                        avP[0:64, h * 128:(h + 1) * 128], dinv[jp][:])))
            if getattr(self, "dbg_p7", False) and c in (0, 1):
                if c == 0:
                    dbg7 = self.nc.dram_tensor("p7dbg", [6, 128, 1024], F32).ap()
                    self._p7dbg = dbg7
                else:
                    dbg7 = self._p7dbg
                base = 3 * c
                ops.append(dict(eng="sy", name=f"dbgpl{c}", waits=[f"pl{c}_0"],
                    emit=lambda sy, base=base: sy.dma_start(
                        out=dbg7[base + 0, :, 0:64].bitcast(BF16),
                        in_=Pl[0][:])))
                ops.append(dict(eng="sy", name=f"dbgdn{c}", waits=[f"dnr{c}_0"],
                    emit=lambda sy, base=base: sy.dma_start(
                        out=dbg7[base + 1, 0:64, 0:128], in_=dinv[0][:])))
                ops.append(dict(eng="sy", name=f"dbgy{c}",
                    waits=[f"yt{c}_7", f"yt{c}_6"],
                    emit=lambda sy, base=base, jy=c % 2: sy.dma_start(
                        out=dbg7[base + 2, :, 0:256].bitcast(BF16),
                        in_=ysb[jy][:])))
                ops.append(dict(eng="v", name=f"dbgg{c}", inc=1, dma=False,
                    waits=[f"dbgpl{c}", f"dbgdn{c}", f"dbgy{c}"],
                    emit=lambda v: v.engine_nop()))
            ops.append(dict(eng="v", name=f"se{c}",
                waits=W(f"st{c}_7", f"av{c}_7_1",
                        "dbgS" if (getattr(self, "dbg_p7", False) and c == 1)
                        else None,
                        "dbgS1" if (getattr(self, "dbg_p7", False) and c == 2)
                        else None,
                        "dbgK1" if (getattr(self, "dbg_p7", False) and c == 2)
                        else None),
                emit=lambda v: v.tensor_copy(
                    AP(Ssb, 0, [[1024, 128], [128, 8], [1, 64]]),
                    AP(SP, 0, [[512, 128], [64, 8], [1, 64]]))))
            if getattr(self, "dbg_p7", False) and c == 0:
                ops.append(dict(eng="sy", name="dbgS", waits=["se0"],
                    emit=lambda sy: sy.dma_start(
                        out=self._p7dbg[1, 64:128, 0:512].bitcast(BF16),
                        in_=Ssb[0:64, :])))
            if getattr(self, "dbg_p7", False) and c == 1:
                ops.append(dict(eng="sy", name="dbgS1", waits=["se1"],
                    emit=lambda sy: sy.dma_start(
                        out=self._p7dbg[4, 0:128, 0:512].bitcast(BF16),
                        in_=Ssb[:, :])))
                ops.append(dict(eng="sy", name="dbgK1", waits=["kr1"],
                    emit=lambda sy: sy.dma_start(
                        out=self._p7dbg[4, 0:128, 512:1024].bitcast(BF16)[:, 0:8],
                        in_=ksc[:, :])))
            ops.append(dict(eng="v", name=f"ke{c}",
                waits=W(f"ks{c}_7", f"dn{c}_7_1"),
                emit=lambda v: v.tensor_copy(ksc[:], kP[:])))
            ops.append(dict(eng="v", name=f"kr{c}", waits=[f"ke{c}"],
                emit=lambda v: v.tensor_copy(
                    AP(ksr, 0, [[1024, 128], [128, 8], [1, 128]]),
                    AP(ksc, 0, [[8, 128], [1, 8], [0, 128]]))))
            ops.append(dict(eng="sy", name=f"yst{c}",
                waits=W(f"yt{c}_7", f"yt{c}_6",
                        f"dbgg{c}" if (getattr(self, "dbg_p7", False) and
                                      c in (0, 1)) else None),
                emit=lambda sy, c=c, jy=c % 2: sy.dma_start(
                    out=k.cc3_in.rearrange("(rc p) t -> p rc t", p=128)[
                        :, :, c * 128:(c + 1) * 128],
                    in_=ysb[jy].rearrange("p (rc t) -> p rc t", rc=4))))
        sems7, cnt7, _ = self.run_ops(ops, "p7")
        self._p7_sy_sem = sems7["sy"]
        self._p7_sy_tgt = cnt7["sy"]
        self.s_cc3 = self.sem("cc3")
        s_cc3 = self.s_cc3
        k2 = self
        nst = NCH

        @blk.sync
        def _(sy):
            pass

        @blk.gpsimd
        def _(gp):
            gp.wait_ge(self._p7_sy_sem, self._p7_sy_tgt)
            gp.collective_compute(
                "AllGather", ALU.bypass,
                replica_groups=[[0, 1], [2, 3], [4, 5], [6, 7]],
                ins=[k2.cc3_in.opt()], outs=[k2.cc3_out.opt()],
            ).then_inc(s_cc3, 1)
        ctx.close()

    def p8_wp_res(self):
        self._proj_ln_res(self.w_p1, self.b_p1, self.cc3_out, self.s_cc3,
                          self.x2_buf, self.x3_buf, self.xln4_buf, "p8")

    def p9_moe(self):
        k = self
        blk = self.block
        ctx = contextlib.ExitStack()
        gwT = self.sbuf("p9_gwT", [128, 8 * 8], BF16, ctx)
        gws = self.sbuf("p9_gws", [8, D], BF16, ctx)
        xfg = [self.sbuf(f"p9_xfg{j}", [128, 8 * 128], BF16, ctx) for j in range(2)]
        lg = [self.sbuf(f"p9_lg{j}", [128, 8], F32, ctx) for j in range(2)]
        m1 = self.sbuf("p9_m1", [128, 1], F32, ctx)
        m2 = self.sbuf("p9_m2", [128, 1], F32, ctx)
        eqm = self.sbuf("p9_eqm", [128, 8], F32, ctx)
        l2 = self.sbuf("p9_l2", [128, 8], F32, ctx)
        keep = self.sbuf("p9_keep", [128, 8], F32, ctx)
        zz = self.sbuf("p9_zz", [128, 8], F32, ctx)
        pp = self.sbuf("p9_pp", [128, 8], F32, ctx)
        dnv = self.sbuf("p9_dnv", [128, 1], F32, ctx)
        cmb = self.sbuf("p9_cmb", [128, 8], F32, ctx)
        ctv = [self.sbuf(f"p9_ct{j}", [8, 128], F32, ctx) for j in range(2)]
        lgP = self.psum("p9_lgP", [128, 8], F32, ctx)
        trP = self.psum("p9_trP", [8, 128], F32, ctx)
        trB = self.psum("p9_trB", [128, 8], BF16, ctx)
        ops = []
        W = lambda *names: [w for w in names if w is not None]
        ops.append(dict(eng="gp", name="gw",
            emit=lambda gp: gp.dma_start(out=gws[:], in_=k.m_gw[:, :])))
        for dc in range(8):
            ops.append(dict(eng="t", name=f"gwp{dc}",
                waits=W("gw" if dc == 0 else None,
                        f"gwe{dc - 1}" if dc >= 1 else None),
                emit=lambda tn, dc=dc: tn.transpose(
                    trB[:], gws[0:8, dc * 128:(dc + 1) * 128],
                    k.ident[0:8, 0:8])))
            ops.append(dict(eng="v", name=f"gwe{dc}", waits=[f"gwp{dc}"],
                emit=lambda v, dc=dc: v.tensor_copy(
                    gwT[:, dc * 8:(dc + 1) * 8], trB[:])))
        for i in range(NO128):
            j = i % 2
            for dc in range(8):
                ops.append(dict(eng="sy", name=f"gx{i}_{dc}",
                    waits=W((f"glg{i - 2}" if i >= 2 else None)
                            if dc == 0 else None),
                    emit=lambda sy, i=i, dc=dc, j=j: sy.dma_start(
                        out=xfg[j][:, dc * 128:(dc + 1) * 128],
                        in_=k.xln4_buf[i * 128:(i + 1) * 128,
                                       dc * 128:(dc + 1) * 128], transpose=True)))
            for dc in range(8):
                ops.append(dict(eng="t", name=f"gm{i}_{dc}",
                    waits=W((f"gx{i}_7" if dc == 0 else None),
                            (f"glg{i - 1}" if i >= 1 and dc == 0 else "gwe7"
                             if i == 0 and dc == 0 else None)),
                    emit=lambda tn, j=j, dc=dc: tn.matmul(
                        lgP[:], xfg[j][:, dc * 128:(dc + 1) * 128],
                        gwT[:, dc * 8:(dc + 1) * 8],
                        start=(dc == 0), stop=(dc == 7))))
            ops.append(dict(eng="v", name=f"glg{i}",
                waits=W(f"gm{i}_7", f"gtr{i - 1}" if i >= 1 else None),
                emit=lambda v, j=j: v.tensor_copy(lg[j][:], lgP[:])))
            ops.append(dict(eng="v", name=f"ga{i}", waits=[f"glg{i}"],
                emit=lambda v, j=j: v.reduce_max(m1[:], lg[j][:],
                                                 mybir.AxisListType.X)))
            ops.append(dict(eng="v", name=f"gb{i}", waits=[f"ga{i}"],
                emit=lambda v, j=j: v.tensor_single_scalar(
                    eqm[:], lg[j][:], m1[:], ALU.is_equal)))
            ops.append(dict(eng="v", name=f"gc{i}", waits=[f"gb{i}"],
                emit=lambda v, j=j: v.scalar_tensor_tensor(
                    l2[:], eqm[:], -1e30, lg[j][:], ALU.mult, ALU.add)))
            ops.append(dict(eng="v", name=f"gd{i}", waits=[f"gc{i}"],
                emit=lambda v: v.reduce_max(m2[:], l2[:], mybir.AxisListType.X)))
            ops.append(dict(eng="v", name=f"ge{i}", waits=[f"gd{i}"],
                emit=lambda v, j=j: v.tensor_single_scalar(
                    keep[:], lg[j][:], m2[:], ALU.is_ge)))
            ops.append(dict(eng="v", name=f"gf{i}", waits=[f"ge{i}"],
                emit=lambda v, j=j: v.tensor_single_scalar(
                    zz[:], lg[j][:], m1[:], ALU.subtract)))
            ops.append(dict(eng="s", name=f"gg{i}", waits=[f"gf{i}"],
                emit=lambda sc: sc.activation(pp[:], zz[:], AF.Exp, bias=0.0,
                                              scale=1.0)))
            ops.append(dict(eng="v", name=f"gh{i}", waits=[f"gg{i}"],
                emit=lambda v: v.tensor_mul(pp[:], pp[:], keep[:])))
            ops.append(dict(eng="v", name=f"gi{i}", waits=[f"gh{i}"],
                emit=lambda v: v.reduce_sum(dnv[:], pp[:], mybir.AxisListType.X)))
            ops.append(dict(eng="v", name=f"gj{i}", waits=[f"gi{i}"],
                emit=lambda v: v.tensor_scalar(dnv[:], dnv[:], 1e-20, None,
                                               ALU.add)))
            ops.append(dict(eng="v", name=f"gk{i}", waits=[f"gj{i}"],
                emit=lambda v: v.reciprocal(dnv[:], dnv[:])))
            ops.append(dict(eng="v", name=f"gl{i}", waits=[f"gk{i}"],
                emit=lambda v: v.tensor_single_scalar(cmb[:], pp[:], dnv[:],
                                                      ALU.mult)))
            ops.append(dict(eng="t", name=f"gtr{i}",
                waits=W(f"gl{i}", f"gte{i - 1}" if i >= 1 else None),
                emit=lambda tn: tn.transpose(trP[:], cmb[:], k.ident_f[:])))
            ops.append(dict(eng="v", name=f"gte{i}",
                waits=W(f"gtr{i}", f"gst{i - 2}" if i >= 2 else None),
                emit=lambda v, j=j: v.tensor_copy(ctv[j][:], trP[:])))
            ops.append(dict(eng="sy", name=f"gst{i}", waits=[f"gte{i}"],
                emit=lambda sy, i=i, j=j: sy.dma_start(
                    out=k.ct_buf[:, i * 128:(i + 1) * 128], in_=ctv[j][:])))
        self.run_ops(ops, "p9g")
        ctx.close()
        self.barrier()

        ctx = contextlib.ExitStack()
        w1s = self.sbuf("p9_w1", [128, 8 * 2048], BF16, ctx)
        w3s = self.sbuf("p9_w3", [128, 8 * 2048], BF16, ctx)
        w2s = self.sbuf("p9_w2", [128, 16 * 1024], BF16, ctx)
        ysb = self.sbuf("p9_y", [128, 8 * 4096], BF16, ctx)
        xft = [self.sbuf(f"p9_xf{j}", [128, 8 * 512], BF16, ctx) for j in range(2)]
        silu = [self.sbuf("p9_silu0", [128, 512], F32, ctx)] * 2
        h3c = [self.sbuf("p9_h3c0", [128, 512], F32, ctx)] * 2
        act = self.sbuf("p9_act", [128, 16 * 512], BF16, ctx)
        crow = self.sbuf("p9_crow", [1, 512], F32, ctx)
        cbs = self.sbuf("p9_cbs", [128, 512], F32, ctx)
        pH = [self.psum(f"p9_pH{j}", [128, 512], F32, ctx) for j in range(4)]
        pY = [self.psum(f"p9_pY{j}", [128, 512], F32, ctx) for j in range(2)]
        pB = self.psum("p9_pB", [128, 512], F32, ctx)
        ops = []
        ops.append(dict(eng="gp", name="zy", dma=False,
                        emit=lambda gp: gp.memset(ysb[:], 0.0)))
        for e in range(9):
            lastuse = f"w2m{e * NO512 - 1}_7_15" if e >= 1 else None
            ops.append(dict(eng="gp", name=f"w1l{e}", waits=W(lastuse),
                emit=lambda gp, e=e: gp.dma_start(
                    out=w1s.rearrange("p (dc i) -> p dc i", dc=8),
                    in_=k.m_w1[e].rearrange("(dc p) i -> p dc i", p=128))))
            ops.append(dict(eng="gp", name=f"w3l{e}", waits=[],
                emit=lambda gp, e=e: gp.dma_start(
                    out=w3s.rearrange("p (dc i) -> p dc i", dc=8),
                    in_=k.m_w3[e].rearrange("(dc p) i -> p dc i", p=128))))
            ops.append(dict(eng="gp", name=f"w2l{e}", waits=[],
                emit=lambda gp, e=e: gp.dma_start(
                    out=w2s.rearrange("p (ic d) -> p ic d", ic=16),
                    in_=k.m_w2[e].rearrange("(ic p) d -> p ic d", p=128))))
            for i in range(NO512):
                j = i % 2
                ei = e * NO512 + i
                for dc in range(8):
                    ops.append(dict(eng="sy", name=f"mx{ei}_{dc}",
                        waits=W((f"h3m{ei - 2}_15_7" if ei >= 2 else None)
                                if dc == 0 else None),
                        emit=lambda sy, i=i, dc=dc, j=j: sy.dma_start(
                            out=xft[j][:, dc * 512:(dc + 1) * 512],
                            in_=k.xln4_buf[i * 512:(i + 1) * 512,
                                           dc * 128:(dc + 1) * 128],
                            transpose=True)))
                if e < 8:
                    ops.append(dict(eng="sy", name=f"mc{ei}",
                        waits=W(f"cbe{ei - 1}" if ei >= 1 else None),
                        emit=lambda sy, e=e, i=i: sy.dma_start(
                            out=crow[:], in_=k.ct_buf[e, i * 512:(i + 1) * 512]
                            .rearrange("(a b) -> a b", a=1))))
                    ops.append(dict(eng="t", name=f"cb{ei}",
                        waits=W(f"mc{ei}", f"cbe{ei - 1}" if ei >= 1 else None),
                        emit=lambda tn: tn.matmul(pB[:], k.onesf[:], crow[:],
                                                  start=True, stop=True)))
                    ops.append(dict(eng="s", name=f"cbe{ei}",
                        waits=W(f"cb{ei}",
                                f"h3e{ei - 1}_15" if ei >= 1 else None),
                        emit=lambda sc: sc.copy(cbs[:], pB[:])))
                for ic in range(16):
                    gA = 2 * (ic % 2)
                    gB = gA + 1
                    jc = ic % 2
                    pr1 = (f"sle{ei}_{ic - 2}" if ic >= 2 else
                           (f"sle{ei - 1}_14" if ei >= 1 else None))
                    for dc in range(8):
                        ops.append(dict(eng="t", name=f"h1m{ei}_{ic}_{dc}",
                            waits=W(*([f"w1l{e}" if (ic == 0 and i == 0) else None,
                                       f"mx{ei}_7" if ic == 0 else None,
                                       pr1] if dc == 0 else [])),
                            emit=lambda tn, j=j, ic=ic, dc=dc, gA=gA: tn.matmul(
                                pH[gA][:],
                                w1s[:, dc * 2048 + ic * 128:
                                    dc * 2048 + (ic + 1) * 128],
                                xft[j][:, dc * 512:(dc + 1) * 512],
                                start=(dc == 0), stop=(dc == 7))))
                    pr3 = (f"h3e{ei}_{ic - 2}" if ic >= 2 else
                           (f"h3e{ei - 1}_14" if ei >= 1 else None))
                    for dc in range(8):
                        ops.append(dict(eng="t", name=f"h3m{ei}_{ic}_{dc}",
                            waits=W(*([f"w3l{e}" if (ic == 0 and i == 0) else None,
                                       pr3] if dc == 0 else [])),
                            emit=lambda tn, j=j, ic=ic, dc=dc, gB=gB: tn.matmul(
                                pH[gB][:],
                                w3s[:, dc * 2048 + ic * 128:
                                    dc * 2048 + (ic + 1) * 128],
                                xft[j][:, dc * 512:(dc + 1) * 512],
                                start=(dc == 0), stop=(dc == 7))))
                    ops.append(dict(eng="s", name=f"sle{ei}_{ic}",
                        waits=W(f"h1m{ei}_{ic}_7",
                                f"acc{ei}_{ic - 1}" if ic >= 1 else
                                (f"acc{ei - 1}_15" if ei >= 1 else None)),
                        emit=lambda sc, gA=gA, jc=jc: sc.activation(
                            silu[jc][:], pH[gA][:], AF.Silu, bias=0.0,
                            scale=1.0)))
                    ops.append(dict(eng="v", name=f"h3e{ei}_{ic}",
                        waits=W(f"h3m{ei}_{ic}_7",
                                f"cbe{ei}" if (e < 8 and ic == 0) else None,
                                f"acc{ei}_{ic - 1}" if ic >= 1 else
                                (f"acc{ei - 1}_15" if ei >= 1 else None)),
                        emit=lambda v, gB=gB, jc=jc, e=e: (
                            v.tensor_mul(h3c[jc][:], pH[gB][:], cbs[:])
                            if e < 8 else v.tensor_copy(h3c[jc][:], pH[gB][:]))))
                    ops.append(dict(eng="v", name=f"acc{ei}_{ic}",
                        waits=W(f"h3e{ei}_{ic}", f"sle{ei}_{ic}",
                                f"w2m{ei - 1}_7_{ic}" if ei >= 1 else None),
                        emit=lambda v, ic=ic, jc=jc: v.tensor_mul(
                            act[:, ic * 512:(ic + 1) * 512], silu[jc][:],
                            h3c[jc][:])))
                for dc in range(8):
                    gY = dc % 2
                    for ic in range(16):
                        ops.append(dict(eng="t", name=f"w2m{ei}_{dc}_{ic}",
                            waits=W(*([f"acc{ei}_15",
                                       f"w2l{e}" if i == 0 else None,
                                       f"yad{ei - 1}_{dc}" if False else None,
                                       (f"yad{ei}_{dc - 2}" if dc >= 2 else
                                        (f"yad{ei - 1}_6" if ei >= 1 else None))]
                                      if ic == 0 else [])),
                            emit=lambda tn, dc=dc, ic=ic, gY=gY: tn.matmul(
                                pY[gY][:],
                                w2s[:, ic * 1024 + dc * 128:
                                    ic * 1024 + (dc + 1) * 128],
                                act[:, ic * 512:(ic + 1) * 512],
                                start=(ic == 0), stop=(ic == 15))))
                    ops.append(dict(eng="v", name=f"yad{ei}_{dc}",
                        waits=[f"w2m{ei}_{dc}_15"],
                        emit=lambda v, i=i, dc=dc, gY=gY: v.tensor_add(
                            ysb[:, dc * 4096 + i * 512:dc * 4096 + (i + 1) * 512],
                            ysb[:, dc * 4096 + i * 512:dc * 4096 + (i + 1) * 512],
                            pY[gY][:])))
                ops.append(dict(eng="v", name=f"w2m{ei}_7x", inc=1, dma=False,
                    waits=[f"yad{ei}_7"], emit=lambda v: v.engine_nop()))
        murow = silu[0][0:1, :]
        rowA = cbs[0:1, :]
        rowB = h3c[0][0:1, :]
        rsr = crow
        t1 = xft[0][:, 0:1024].bitcast(F32)
        brt = act[:, 0:512].bitcast(BF16) if False else act[:, 0:512]
        sqs = act[:, 512:1024]
        LAST = 9 * NO512 - 1
        for i in range(NO512):
            for dc in range(8):
                ysl = ysb[:, dc * 4096 + i * 512:dc * 4096 + (i + 1) * 512]
                ops.append(dict(eng="s", name=f"qsq{i}_{dc}",
                    waits=W(f"w2m{LAST}_7x" if i == 0 and dc == 0 else None,
                            f"qm2{i}_{dc - 2}" if dc >= 2 else
                            (f"qm2{i - 1}_6" if i >= 1 else None)),
                    emit=lambda sc, ysl=ysl, dc=dc: sc.square(
                        sqs[:, (dc % 2) * 256:(dc % 2) * 256 + 256].bitcast(BF16)
                        if False else sqs, ysl) if False else sc.square(
                        act[:, (512 + (dc % 2) * 512):(1024 + (dc % 2) * 512)],
                        ysl)))
                ops.append(dict(eng="t", name=f"qm1{i}_{dc}",
                    waits=W(f"w2m{LAST}_7x" if i == 0 and dc == 0 else None,
                            f"qmu{i - 1}" if i >= 1 and dc == 0 else None),
                    emit=lambda tn, ysl=ysl: tn.matmul(
                        pH[0][:], k.ones_bf[:], ysl,
                        start=(dc == 0), stop=(dc == 7), skip_group_check=True)))
                ops.append(dict(eng="t", name=f"qm2{i}_{dc}",
                    waits=W(f"qsq{i}_{dc}",
                            f"qva{i - 1}" if i >= 1 and dc == 0 else None),
                    emit=lambda tn, dc=dc: tn.matmul(
                        pH[1][:], k.ones_bf[:],
                        act[:, (512 + (dc % 2) * 512):(1024 + (dc % 2) * 512)],
                        start=(dc == 0), stop=(dc == 7), skip_group_check=True)))
            ops.append(dict(eng="v", name=f"qmu{i}", waits=[f"qm1{i}_7"],
                emit=lambda v: v.tensor_scalar(murow, pH[0][0:1, :], 1.0 / D,
                                               None, ALU.mult)))
            ops.append(dict(eng="v", name=f"qmu2{i}", waits=[f"qmu{i}"],
                emit=lambda v: v.tensor_mul(rowA, murow, murow)))
            ops.append(dict(eng="v", name=f"qva{i}", waits=[f"qm2{i}_7", f"qmu2{i}"],
                emit=lambda v: v.scalar_tensor_tensor(rowB, pH[1][0:1, :],
                    1.0 / D, rowA, ALU.mult, ALU.subtract)))
            ops.append(dict(eng="s", name=f"qln{i}", waits=[f"qva{i}"],
                emit=lambda sc: sc.activation(rowA, rowB, AF.Ln,
                                              bias=k.eps_t[0:1, 0:1], scale=1.0)))
            ops.append(dict(eng="s", name=f"qex{i}", waits=[f"qln{i}"],
                emit=lambda sc: sc.activation(rsr[:], rowA, AF.Exp, bias=0.0,
                                              scale=-0.5)))
            ops.append(dict(eng="t", name=f"qmb{i}",
                waits=W(f"qmu{i}", f"qba{i - 1}_7" if i >= 1 else None),
                emit=lambda tn: tn.matmul(pH[2][:], k.onesf[:], murow,
                                          start=True, stop=True)))
            ops.append(dict(eng="t", name=f"qrb{i}",
                waits=W(f"qex{i}", f"qbb{i - 1}_7" if i >= 1 else None),
                emit=lambda tn: tn.matmul(pH[3][:], k.onesf[:], rsr[:],
                                          start=True, stop=True)))
            for dc in range(8):
                no = i * 8 + dc
                ysl = ysb[:, dc * 4096 + i * 512:dc * 4096 + (i + 1) * 512]
                ops.append(dict(eng="v", name=f"qba{i}_{dc}",
                    waits=W(f"qmb{i}" if dc == 0 else None,
                            f"qrb{i}" if dc == 0 else None,
                            f"qbb{i}_{dc - 1}" if dc >= 1 else None),
                    emit=lambda v, ysl=ysl: v.tensor_sub(t1[:], ysl, pH[2][:])))
                ops.append(dict(eng="v", name=f"qbb{i}_{dc}",
                    waits=W(f"qba{i}_{dc}",
                            f"qst{no - 1}" if no >= 1 else None),
                    emit=lambda v: v.tensor_mul(brt, t1[:], pH[3][:])))
                ops.append(dict(eng="sy", name=f"qst{no}",
                    waits=[f"qbb{i}_{dc}"],
                    emit=lambda sy, i=i, dc=dc: sy.dma_start(
                        out=k.br_buf[dc * 128:(dc + 1) * 128,
                                     i * 512:(i + 1) * 512], in_=brt)))
        self.run_ops(ops, "p9m")
        ctx.close()
        self.barrier()
        self._res_pass(self.x3_buf, self.br_buf, self.out, None, "p9r")

    def p1_ln1(self):
        self._ln_pass(self.x_full, T, out_bf=self.xln1_buf, tag="p1")

    def _ln_pass(self, x_src, nrows, out_bf, tag, br_src=None, x_out=None,
                 src_dt=F32):
        k = self
        blk = self.block
        ctx = contextlib.ExitStack()
        nt = nrows // 128
        xin = [self.sbuf(f"{tag}_xin{j}", [128, D], src_dt, ctx) for j in range(2)]
        xrs = [self.sbuf(f"{tag}_xrs{j}", [128, D], F32, ctx) for j in range(2)]
        brt = [self.sbuf(f"{tag}_brt{j}", [128, D], BF16, ctx) for j in range(2)]
        xsq = [self.sbuf(f"{tag}_xsq{j}", [128, D], BF16, ctx) for j in range(2)]
        xout = [self.sbuf(f"{tag}_xo{j}", [128, D], BF16, ctx) for j in range(2)]
        s1 = [self.sbuf(f"{tag}_s1{j}", [128, 1], F32, ctx) for j in range(2)]
        s2 = [self.sbuf(f"{tag}_s2{j}", [128, 1], F32, ctx) for j in range(2)]
        mu = [self.sbuf(f"{tag}_mu{j}", [128, 1], F32, ctx) for j in range(2)]
        t0_ = [self.sbuf(f"{tag}_t0{j}", [128, 1], F32, ctx) for j in range(2)]
        var = [self.sbuf(f"{tag}_va{j}", [128, 1], F32, ctx) for j in range(2)]
        lnv = [self.sbuf(f"{tag}_ln{j}", [128, 1], F32, ctx) for j in range(2)]
        rstd = [self.sbuf(f"{tag}_rs{j}", [128, 1], F32, ctx) for j in range(2)]
        s_ld = self.sem(f"{tag}ld")
        s_sq = self.sem(f"{tag}sq")
        s_dv = self.sem(f"{tag}dv")
        s_rs = self.sem(f"{tag}rs")
        s_ap = self.sem(f"{tag}ap")
        s_sd = self.sem(f"{tag}sd")
        s_ra = self.sem(f"{tag}ra")
        has_res = br_src is not None
        dma_per_tile = 2 if has_res else 1
        st_per_tile = (1 if out_bf is not None else 0) + (1 if x_out is not None else 0)

        @blk.sync
        def _(sy):
            for i in range(nt):
                if i >= 2:
                    sy.wait_ge(s_ap, i - 1)
                sy.dma_start(out=xin[i % 2][:], in_=x_src[i * 128:(i + 1) * 128, :]
                             ).then_inc(s_ld, 16)
                if has_res:
                    sy.dma_start(out=brt[i % 2][:],
                                 in_=br_src[:, i * 128:(i + 1) * 128],
                                 transpose=True).then_inc(s_ld, 16)
                if i > 0:
                    self._ln_store(sy, i - 1, out_bf, x_out, xout, xrs, s_ap, s_sd,
                                   st_per_tile)
            self._ln_store(sy, nt - 1, out_bf, x_out, xout, xrs, s_ap, s_sd,
                           st_per_tile)
            sy.wait_ge(s_sd, st_per_tile * 16 * nt)

        @blk.vector
        def _(v):
            for i in range(nt):
                j = i % 2
                v.wait_ge(s_ld, dma_per_tile * 16 * (i + 1))
                if i >= 2 and st_per_tile:
                    v.wait_ge(s_sd, st_per_tile * 16 * (i - 1))
                if has_res:
                    v.tensor_add(xrs[j][:], xin[j][:], brt[j][:]).then_inc(s_ra, 1)
                    src = xrs[j]
                else:
                    src = xin[j]
                if has_res:
                    v.wait_ge(s_ra, i + 1)
                v.tensor_reduce(s1[j][:], src[:], mybir.AxisListType.X, ALU.add
                                ).then_inc(s_dv, 1)
                v.wait_ge(s_sq, i + 1)
                v.tensor_reduce(s2[j][:], xsq[j][:], mybir.AxisListType.X, ALU.add)
                if i >= 2:
                    v.wait_ge(s_rs, 2 * i - 2)
                v.wait_ge(s_dv, 3 * i + 1)
                v.tensor_scalar(mu[j][:], s1[j][:], 1.0 / D, None, ALU.mult)
                v.tensor_scalar(t0_[j][:], s1[j][:], 1.0 / (1.0 * D * D), s1[j][:],
                                ALU.mult, ALU.mult).then_inc(s_dv, 1)
                v.wait_ge(s_dv, 3 * i + 2)
                v.scalar_tensor_tensor(var[j][:], s2[j][:], 1.0 / D, t0_[j][:],
                                       ALU.mult, ALU.subtract
                                       ).then_inc(s_dv, 1)
                v.wait_ge(s_rs, 2 * i + 2)
                v.tensor_scalar(xout[j][:], src[:], mu[j][:], rstd[j][:],
                                ALU.subtract, ALU.mult).then_inc(s_ap, 1)

        @blk.scalar
        def _(sc):
            for i in range(nt):
                j = i % 2
                if has_res:
                    sc.wait_ge(s_ra, i + 1)
                else:
                    sc.wait_ge(s_ld, dma_per_tile * 16 * (i + 1))
                if i >= 2:
                    sc.wait_ge(s_dv, 3 * (i - 2) + 2)
                src = xrs[j] if has_res else xin[j]
                sc.square(xsq[j][:], src[:]).then_inc(s_sq, 1)
                sc.wait_ge(s_dv, 3 * i + 3)
                if i >= 2:
                    sc.wait_ge(s_ap, i - 1)
                sc.activation(lnv[j][:], var[j][:], AF.Ln, bias=k.eps_t[:, 0:1],
                              scale=1.0).then_inc(s_rs, 1)
                sc.wait_ge(s_rs, 2 * i + 1)
                sc.activation(rstd[j][:], lnv[j][:], AF.Exp, bias=0.0,
                              scale=-0.5).then_inc(s_rs, 1)
        ctx.close()

    def _ln_store(self, sy, i, out_bf, x_out, xout, xrs, s_ap, s_sd, st_per_tile):
        sy.wait_ge(s_ap, i + 1)
        if out_bf is not None:
            sy.dma_start(out=out_bf[i * 128:(i + 1) * 128, :], in_=xout[i % 2][:]
                         ).then_inc(s_sd, 16)
        if x_out is not None:
            sy.dma_start(out=x_out[i * 128:(i + 1) * 128, :], in_=xrs[i % 2][:]
                         ).then_inc(s_sd, 16)

    def p0_setup(self):
        k = self
        nc, blk = self.nc, self.block
        self.ones_bf = self.sbuf("ones_bf", [128, 128], BF16)
        self.ones_f = self.sbuf("ones_f", [128, 128], F32)
        self.ident_f = self.sbuf("ident_f", [128, 128], F32)
        self.onesf = self.sbuf("onesf", [1, 128], F32)
        self.ident = self.sbuf("ident", [128, 128], BF16)
        self.causal01 = self.sbuf("causal01", [128, 128], BF16)
        self.eps_t = self.sbuf("eps_t", [128, 1], F32)
        self.negln8 = self.sbuf("negln8", [128, 1], F32)
        self.th8 = self.sbuf("th8", [8, 1], F32)
        self.negth8 = self.sbuf("negth8", [8, 1], F32)
        self.thbc = self.sbuf("thbc", [128, 8], F32)
        self.negthbc = self.sbuf("negthbc", [128, 8], F32)
        self.am_sb = self.sbuf("am_sb", [128, 4 * 512], BF16)

        s_ld = self.sem("p0ld")
        s_gp = self.sem("p0gp")
        s_cm = self.sem("p0cm")
        s_mm = self.sem("p0mm")
        s_st = self.sem("p0st")
        s_th = self.sem("p0th")

        ctx = contextlib.ExitStack()
        d8 = self.sbuf("d8", [128, 8200], BF16, ctx)
        d8r = self.sbuf("d8r", [128, 8200], BF16, ctx)
        iot = self.sbuf("p0_iot", [8, 8200], F32, ctx)
        ntile = self.sbuf("p0_nt", [128, 512], F32, ctx)
        acc = self.sbuf("p0_acc", [128, 512], F32, ctx)
        eh = self.sbuf("p0_eh", [128, 512], F32, ctx)
        zt = self.sbuf("p0_zt", [128, 512], F32, ctx)
        dtile = self.sbuf("p0_dt", [128, 512], BF16, ctx)
        dtile2 = self.sbuf("p0_dt2", [128, 512], BF16, ctx)
        throw = self.sbuf("p0_throw", [1, 8], F32, ctx)
        pdec = self.psum("p0_pdec", [128, 512], F32, ctx)
        pth = self.psum("p0_pth", [128, 8], F32, ctx)

        @blk.sync
        def _(sy):
            sy.dma_start(out=k.th8[:], in_=k.theta.rearrange("(a b) -> a b", b=1)
                         ).then_inc(s_ld, 16)
            sy.dma_start(out=throw[:], in_=k.theta.rearrange("(a b) -> b a", b=1)
                         ).then_inc(s_ld, 16)

        @blk.gpsimd
        def _(gp):
            gp.memset(k.ones_bf[:], 1.0)
            gp.memset(k.ones_f[:], 1.0)
            gp.memset(k.onesf[:], 1.0)
            gp.memset(k.eps_t[:], EPS)
            gp.memset(k.negln8[:], -LN8)
            gp.memset(k.ident[:], 0.0)
            gp.affine_select(out=k.ident[:], in_=k.ident[:], compare_op=ALU.not_equal,
                             fill=1.0, base=0, pattern=[[-1, 128]], channel_multiplier=1)
            gp.memset(k.ident_f[:], 0.0)
            gp.affine_select(out=k.ident_f[:], in_=k.ident_f[:],
                             compare_op=ALU.not_equal, fill=1.0, base=0,
                             pattern=[[-1, 128]], channel_multiplier=1)
            gp.memset(k.causal01[:], 1.0)
            gp.affine_select(out=k.causal01[:], in_=k.causal01[:], compare_op=ALU.is_ge,
                             fill=0.0, base=0, pattern=[[1, 128]], channel_multiplier=-1)
            for i, c in enumerate([0, -128, -256, -384]):
                sl = k.am_sb[:, i * 512:(i + 1) * 512]
                gp.memset(sl, 0.0)
                gp.affine_select(out=sl, in_=sl, compare_op=ALU.is_ge, fill=-1e30,
                                 base=c, pattern=[[1, 512]], channel_multiplier=-1)
            gp.memset(d8[:], 0.0)
            gp.memset(d8r[:], 0.0)
            gp.iota(iot[0:8, :], pattern=[[1, 8200]], base=0, channel_multiplier=0,
                    allow_small_or_imprecise_dtypes=True).then_inc(s_gp, 1)

        @blk.scalar
        def _(sc):
            sc.wait_ge(s_ld, 32)
            sc.wait_ge(s_gp, 1)
            sc.mul(k.negth8[:], k.th8[:], -1.0)
            sc.activation(d8[0:8, :], iot[0:8, :], AF.Exp, bias=0.0,
                          scale=k.negth8[0:8, 0:1]).then_inc(s_cm, 1)

        @blk.gpsimd
        def _(gp):
            gp.wait_ge(s_cm, 1)
            gp.iota(iot[0:8, :], pattern=[[-1, 8200]], base=8199, channel_multiplier=0,
                    allow_small_or_imprecise_dtypes=True).then_inc(s_gp, 1)

        @blk.scalar
        def _(sc):
            sc.wait_ge(s_gp, 2)
            sc.activation(d8r[0:8, :], iot[0:8, :], AF.Exp, bias=k.negln8[0:8, 0:1],
                          scale=k.negth8[0:8, 0:1]).then_inc(s_cm, 1)

        @blk.tensor
        def _(tn):
            tn.wait_ge(s_ld, 32)
            tn.wait_ge(s_gp, 1)
            tn.matmul(pth[:], k.onesf[:], throw[:], start=True, stop=True
                      ).then_inc(s_mm, 1)

        @blk.vector
        def _(v):
            v.wait_ge(s_mm, 1)
            v.tensor_copy(k.thbc[:], pth[:])
            v.tensor_scalar_mul(k.negthbc[:], pth[:], -1.0).then_inc(s_th, 1)

        M0 = 8199
        offc = list(range(128, 7681, 128))

        @blk.tensor
        def _(tn):
            for i, c in enumerate(offc):
                tn.wait_ge(s_cm, 2 + i)
                tn.matmul(pdec[:], d8r[:, M0 - c:M0 - c + 128], d8[:, 0:512],
                          start=True, stop=True).then_inc(s_mm, 1)

        @blk.scalar
        def _(sc):
            for i in range(len(offc)):
                sc.wait_ge(s_mm, 2 + i)
                sc.wait_ge(s_st, i * 16)
                sc.copy(dtile[:], pdec[:]).then_inc(s_cm, 1)

        @blk.sync
        def _(sy):
            for i, c in enumerate(offc):
                sy.wait_ge(s_cm, 3 + i)
                sy.dma_start(out=k.dec_buf[(c + 384) // 128, :, :], in_=dtile[:]
                             ).then_inc(s_st, 16)

        chain = self.sem("p0ch")
        chn = [0]

        def cstep(instr, n=1):
            instr.then_inc(chain, n)
            chn[0] += n

        for i, c in enumerate([0, -128, -256, -384]):
            @blk.gpsimd
            def _(gp, c=c, n0=chn[0]):
                gp.wait_ge(chain, n0)
                cstep(gp.iota(ntile[:], pattern=[[1, 512]], base=c,
                              channel_multiplier=-1,
                              allow_small_or_imprecise_dtypes=True))
            for h in range(8):
                @blk.vector
                def _(v, h=h, n0=chn[0]):
                    v.wait_ge(chain, n0)
                    v.wait_ge(s_th, 1)
                    cstep(v.tensor_scalar(zt[:], ntile[:], k.negthbc[:, h:h + 1],
                                          0.0, ALU.mult, ALU.min))
                @blk.scalar
                def _(sc, n0=chn[0]):
                    sc.wait_ge(chain, n0)
                    cstep(sc.activation(eh[:], zt[:], AF.Exp,
                                        bias=k.negln8[:, 0:1], scale=1.0))
                @blk.vector
                def _(v, h=h, n0=chn[0]):
                    v.wait_ge(chain, n0)
                    if h == 0:
                        cstep(v.tensor_copy(acc[:], eh[:]))
                    else:
                        cstep(v.tensor_add(acc[:], acc[:], eh[:]))
            @blk.gpsimd
            def _(gp, c=c, n0=chn[0]):
                gp.wait_ge(chain, n0)
                cstep(gp.affine_select(out=acc[:], in_=acc[:], compare_op=ALU.is_ge,
                                       fill=0.0, base=c, pattern=[[1, 512]],
                                       channel_multiplier=-1))
            @blk.vector
            def _(v, n0=chn[0]):
                v.wait_ge(chain, n0)
                cstep(v.tensor_copy(dtile2[:], acc[:]))
            @blk.sync
            def _(sy, c=c, n0=chn[0]):
                sy.wait_ge(chain, n0)
                cstep(sy.dma_start(out=k.dec_buf[(c + 384) // 128, :, :],
                                   in_=dtile2[:]), 16)

        if getattr(self, "dbg_p0", False):
            dbg_d8 = nc.dram_tensor("dbg_d8_scr", [128, 8200], BF16).ap()
            dbg_d8r = nc.dram_tensor("dbg_d8r_scr", [128, 8200], BF16).ap()
            self._d8_scr, self._d8r_scr = dbg_d8, dbg_d8r
            @blk.sync
            def _(sy):
                sy.wait_ge(s_cm, 3)
                sy.dma_start(out=dbg_d8[:, :], in_=d8[:]).then_inc(s_st, 16)
                sy.dma_start(out=dbg_d8r[:, :], in_=d8r[:]).then_inc(s_st, 16)

        @blk.sync
        def _(sy, nst=len(offc) * 16 + (32 if getattr(self, "dbg_p0", False) else 0),
              nch=chn[0]):
            sy.wait_ge(s_st, nst)
            sy.wait_ge(chain, nch)
        ctx.close()

    def p1_ln1(self):
        self._ln_pass(self.x_full, T, out_bf=self.xln1_buf, tag="p1")

    def _ln_pass(self, x_src, nrows, out_bf, tag, br_src=None, x_out=None,
                 src_dt=F32):
        k = self
        blk = self.block
        ctx = contextlib.ExitStack()
        nt = nrows // 128
        xin = [self.sbuf(f"{tag}_xin{j}", [128, D], src_dt, ctx) for j in range(2)]
        xrs = [self.sbuf(f"{tag}_xrs{j}", [128, D], F32, ctx) for j in range(2)]
        brt = [self.sbuf(f"{tag}_brt{j}", [128, D], BF16, ctx) for j in range(2)]
        xsq = [self.sbuf(f"{tag}_xsq{j}", [128, D], BF16, ctx) for j in range(2)]
        xout = [self.sbuf(f"{tag}_xo{j}", [128, D], BF16, ctx) for j in range(2)]
        s1 = [self.sbuf(f"{tag}_s1{j}", [128, 1], F32, ctx) for j in range(2)]
        s2 = [self.sbuf(f"{tag}_s2{j}", [128, 1], F32, ctx) for j in range(2)]
        mu = [self.sbuf(f"{tag}_mu{j}", [128, 1], F32, ctx) for j in range(2)]
        t0_ = [self.sbuf(f"{tag}_t0{j}", [128, 1], F32, ctx) for j in range(2)]
        var = [self.sbuf(f"{tag}_va{j}", [128, 1], F32, ctx) for j in range(2)]
        lnv = [self.sbuf(f"{tag}_ln{j}", [128, 1], F32, ctx) for j in range(2)]
        rstd = [self.sbuf(f"{tag}_rs{j}", [128, 1], F32, ctx) for j in range(2)]
        s_ld = self.sem(f"{tag}ld")
        s_sq = self.sem(f"{tag}sq")
        s_dv = self.sem(f"{tag}dv")
        s_rs = self.sem(f"{tag}rs")
        s_ap = self.sem(f"{tag}ap")
        s_sd = self.sem(f"{tag}sd")
        s_ra = self.sem(f"{tag}ra")
        has_res = br_src is not None
        dma_per_tile = 2 if has_res else 1
        st_per_tile = (1 if out_bf is not None else 0) + (1 if x_out is not None else 0)

        @blk.sync
        def _(sy):
            for i in range(nt):
                if i >= 2:
                    sy.wait_ge(s_ap, i - 1)
                sy.dma_start(out=xin[i % 2][:], in_=x_src[i * 128:(i + 1) * 128, :]
                             ).then_inc(s_ld, 16)
                if has_res:
                    sy.dma_start(out=brt[i % 2][:],
                                 in_=br_src[:, i * 128:(i + 1) * 128],
                                 transpose=True).then_inc(s_ld, 16)
                if i > 0:
                    self._ln_store(sy, i - 1, out_bf, x_out, xout, xrs, s_ap, s_sd,
                                   st_per_tile)
            self._ln_store(sy, nt - 1, out_bf, x_out, xout, xrs, s_ap, s_sd,
                           st_per_tile)
            sy.wait_ge(s_sd, st_per_tile * 16 * nt)

        @blk.vector
        def _(v):
            for i in range(nt):
                j = i % 2
                v.wait_ge(s_ld, dma_per_tile * 16 * (i + 1))
                if i >= 2 and st_per_tile:
                    v.wait_ge(s_sd, st_per_tile * 16 * (i - 1))
                if has_res:
                    v.tensor_add(xrs[j][:], xin[j][:], brt[j][:]).then_inc(s_ra, 1)
                    src = xrs[j]
                else:
                    src = xin[j]
                if has_res:
                    v.wait_ge(s_ra, i + 1)
                v.tensor_reduce(s1[j][:], src[:], mybir.AxisListType.X, ALU.add
                                ).then_inc(s_dv, 1)
                v.wait_ge(s_sq, i + 1)
                v.tensor_reduce(s2[j][:], xsq[j][:], mybir.AxisListType.X, ALU.add)
                if i >= 2:
                    v.wait_ge(s_rs, 2 * i - 2)
                v.wait_ge(s_dv, 3 * i + 1)
                v.tensor_scalar(mu[j][:], s1[j][:], 1.0 / D, None, ALU.mult)
                v.tensor_scalar(t0_[j][:], s1[j][:], 1.0 / (1.0 * D * D), s1[j][:],
                                ALU.mult, ALU.mult).then_inc(s_dv, 1)
                v.wait_ge(s_dv, 3 * i + 2)
                v.scalar_tensor_tensor(var[j][:], s2[j][:], 1.0 / D, t0_[j][:],
                                       ALU.mult, ALU.subtract
                                       ).then_inc(s_dv, 1)
                v.wait_ge(s_rs, 2 * i + 2)
                v.tensor_scalar(xout[j][:], src[:], mu[j][:], rstd[j][:],
                                ALU.subtract, ALU.mult).then_inc(s_ap, 1)

        @blk.scalar
        def _(sc):
            for i in range(nt):
                j = i % 2
                if has_res:
                    sc.wait_ge(s_ra, i + 1)
                else:
                    sc.wait_ge(s_ld, dma_per_tile * 16 * (i + 1))
                if i >= 2:
                    sc.wait_ge(s_dv, 3 * (i - 2) + 2)
                src = xrs[j] if has_res else xin[j]
                sc.square(xsq[j][:], src[:]).then_inc(s_sq, 1)
                sc.wait_ge(s_dv, 3 * i + 3)
                if i >= 2:
                    sc.wait_ge(s_ap, i - 1)
                sc.activation(lnv[j][:], var[j][:], AF.Ln, bias=k.eps_t[:, 0:1],
                              scale=1.0).then_inc(s_rs, 1)
                sc.wait_ge(s_rs, 2 * i + 1)
                sc.activation(rstd[j][:], lnv[j][:], AF.Exp, bias=0.0,
                              scale=-0.5).then_inc(s_rs, 1)
        ctx.close()

    def _ln_store(self, sy, i, out_bf, x_out, xout, xrs, s_ap, s_sd, has_res):
        sy.wait_ge(s_ap, i + 1)
        if out_bf is not None:
            sy.dma_start(out=out_bf[i * 128:(i + 1) * 128, :], in_=xout[i % 2][:]
                         ).then_inc(s_sd, 16)
        if x_out is not None:
            sy.dma_start(out=x_out[i * 128:(i + 1) * 128, :], in_=xrs[i % 2][:]
                         ).then_inc(s_sd, 16)

    def p2_proj0(self):
        k = self
        blk = self.block
        ctx = contextlib.ExitStack()
        wq = self.sbuf("p2_wq", [128, 8 * 512], BF16, ctx)
        wk = self.sbuf("p2_wk", [128, 8 * 512], BF16, ctx)
        wv = self.sbuf("p2_wv", [128, 8 * 512], BF16, ctx)
        xf = [[self.sbuf(f"p2_xf{j}_{dc}", [128, 512], BF16, ctx)
               for dc in range(8)] for j in range(2)]
        ot = [self.sbuf(f"p2_ot{j}", [128, 512], BF16, ctx) for j in range(4)]
        ps = [self.psum(f"p2_ps{j}", [128, 512], F32, ctx) for j in range(4)]
        s_w = self.sem("p2w")
        s_xf = self.sem("p2xf")
        s_mm = self.sem("p2mm")
        s_ev = self.sem("p2ev")
        s_st = self.sem("p2st")
        QSC = 0.08838834764831845

        @blk.gpsimd
        def _(gp):
            for wsb, wsrc in [(wq, k.w_q0), (wk, k.w_k0), (wv, k.w_v0)]:
                gp.dma_start(out=wsb.rearrange("p (dc o) -> p dc o", dc=8),
                             in_=wsrc.rearrange("(dc p) o -> p dc o", p=128)
                             ).then_inc(s_w, 16)

        @blk.sync
        def _(sy):
            for i in range(NT512):
                if i >= 2:
                    sy.wait_ge(s_mm, 12 * (i - 1))
                for dc in range(8):
                    sy.dma_start(out=xf[i % 2][dc][:],
                                 in_=k.xln1_buf[i * 512:(i + 1) * 512,
                                                dc * 128:(dc + 1) * 128],
                                 transpose=True).then_inc(s_xf, 16)
                for g in range(12):
                    n = i * 12 + g
                    sy.wait_ge(s_ev, n + 1)
                    if g < 4:
                        dst = k.q0_buf[g, :, i * 512:(i + 1) * 512]
                    elif g < 8:
                        dst = k.k0_buf[g - 4, :, i * 512:(i + 1) * 512]
                    else:
                        st = g - 8
                        dst = k.v0_buf[i * 512 + st * 128:i * 512 + (st + 1) * 128, :]
                    sy.dma_start(out=dst, in_=ot[n % 4][:]).then_inc(s_st, 16)
            sy.wait_ge(s_st, 16 * 12 * NT512)

        @blk.tensor
        def _(tn):
            tn.wait_ge(s_w, 48)
            for i in range(NT512):
                tn.wait_ge(s_xf, 16 * 8 * (i + 1))
                for g in range(12):
                    n = i * 12 + g
                    if n >= 4:
                        tn.wait_ge(s_ev, n - 3)
                    pb = ps[n % 4]
                    for dc in range(8):
                        if g < 4:
                            lhs = wq[:, dc * 512 + g * 128:dc * 512 + (g + 1) * 128]
                            rhs = xf[i % 2][dc][:]
                        elif g < 8:
                            h = g - 4
                            lhs = wk[:, dc * 512 + h * 128:dc * 512 + (h + 1) * 128]
                            rhs = xf[i % 2][dc][:]
                        else:
                            st = g - 8
                            lhs = xf[i % 2][dc][:, st * 128:(st + 1) * 128]
                            rhs = wv[:, dc * 512:(dc + 1) * 512]
                        mm = tn.matmul(pb[:], lhs, rhs, start=(dc == 0),
                                       stop=(dc == 7))
                    mm.then_inc(s_mm, 1)

        @blk.scalar
        def _(sc):
            for i in range(NT512):
                for g in range(12):
                    n = i * 12 + g
                    sc.wait_ge(s_mm, n + 1)
                    if n >= 4:
                        sc.wait_ge(s_st, 16 * (n - 3))
                    scale = QSC if g < 4 else 1.0
                    sc.activation(ot[n % 4][:], ps[n % 4][:], AF.Copy, bias=0.0,
                                  scale=scale).then_inc(s_ev, 1)
        ctx.close()

    def p3_attn0(self):
        k = self
        blk = self.block
        ctx = contextlib.ExitStack()
        qt = [[self.sbuf(f"p3_qt{j}{h}", [128, 512], BF16, ctx) for h in range(2)]
              for j in range(2)]
        kc = [[self.sbuf(f"p3_kc{j}{h}", [128, 1024], BF16, ctx) for h in range(2)]
              for j in range(2)]
        vc = [self.sbuf(f"p3_vc{j}", [128, 2048], BF16, ctx) for j in range(2)]
        dcc = [self.sbuf(f"p3_dc{j}", [128, 8 * 512], BF16, ctx) for j in range(2)]
        sstt = [self.sbuf(f"p3_stt{j}", [128, 512], F32, ctx) for j in range(2)]
        sd2 = [self.sbuf(f"p3_sd2{j}", [128, 512], F32, ctx) for j in range(2)]
        Pt = [self.sbuf(f"p3_pt{j}", [128, 512], BF16, ctx) for j in range(2)]
        rrow = [self.sbuf(f"p3_rr{h}", [1, 512], F32, ctx) for h in range(2)]
        rbs = [self.sbuf(f"p3_rbs{h}", [128, 512], F32, ctx) for h in range(2)]
        od = [self.sbuf(f"p3_od{h}", [128, 512], BF16, ctx) for h in range(2)]
        sq = [self.sbuf(f"p3_sq{h}", [128, 512], BF16, ctx) for h in range(2)]
        t1b = [self.sbuf(f"p3_t1{h}", [128, 512], F32, ctx) for h in range(2)]
        murow = [self.sbuf(f"p3_mur{h}", [1, 512], F32, ctx) for h in range(2)]
        mu2row = [self.sbuf(f"p3_mu2{h}", [1, 512], F32, ctx) for h in range(2)]
        varrow = [self.sbuf(f"p3_var{h}", [1, 512], F32, ctx) for h in range(2)]
        lnr = [self.sbuf(f"p3_lnr{h}", [1, 512], F32, ctx) for h in range(2)]
        rsr = [self.sbuf(f"p3_rsr{h}", [1, 512], F32, ctx) for h in range(2)]
        gn = [self.sbuf(f"p3_gn{j}", [128, 512], BF16, ctx) for j in range(2)]
        gg = self.sbuf("p3_gg", [128, 4], F32, ctx)
        gb = self.sbuf("p3_gb", [128, 4], F32, ctx)
        scP = [self.psum(f"p3_scP{j}", [128, 512], F32, ctx) for j in range(2)]
        avP = [self.psum(f"p3_avP{h}", [128, 512], F32, ctx) for h in range(2)]
        rP = [self.psum(f"p3_rP{h}", [128, 512], F32, ctx) for h in range(2)]
        bcP = [self.psum(f"p3_bcP{j}", [128, 512], F32, ctx) for j in range(2)]

        ops = []
        W = lambda *names: [w for w in names if w is not None]
        for hl in range(4):
            ops.append(dict(eng="sy", name=f"gg{hl}", emit=lambda sy, hl=hl:
                sy.dma_start(out=gg[:, hl:hl + 1],
                             in_=k.gn_gamma.rearrange("(a b) -> a b", b=1)[
                                 128 * hl:128 * (hl + 1), :])))
            ops.append(dict(eng="sy", name=f"gb{hl}", emit=lambda sy, hl=hl:
                sy.dma_start(out=gb[:, hl:hl + 1],
                             in_=k.gn_beta.rearrange("(a b) -> a b", b=1)[
                                 128 * hl:128 * (hl + 1), :])))

        n = 0
        gci = 0
        kidx = 0
        chunk_last = {}
        k_last = {}
        tail = {}
        gstore = {}
        for tb in range(16):
            t0 = 512 * tb
            S = 4 * tb + 4
            for hp in range(2):
                qwait = k_last.get(kidx - 2)
                for h in range(2):
                    ops.append(dict(eng="sy", name=f"q{kidx}_{h}",
                        waits=W(qwait),
                        emit=lambda sy, h=h, hp=hp, j=kidx % 2, t0=t0:
                        sy.dma_start(out=qt[j][h][:],
                                     in_=k.q0_buf[2 * hp + h, :, t0:t0 + 512])))
                gci0 = gci
                nch = (S + 7) // 8
                n0 = n
                for ch in range(nch):
                    si0 = ch * 8
                    nsi = min(8, S - si0)
                    cl = chunk_last.get(gci - 2, (None, None, None))
                    jc = gci % 2
                    for h in range(2):
                        ops.append(dict(eng="sy", name=f"k{gci}_{h}",
                            waits=W(cl[0] if h == 0 else None),
                            emit=lambda sy, h=h, hp=hp, jc=jc, s0=si0 * 128, w=nsi * 128:
                            sy.dma_start(out=kc[jc][h][:, 0:w],
                                         in_=k.k0_buf[2 * hp + h, :, s0:s0 + w])))
                    ops.append(dict(eng="sy", name=f"v{gci}",
                        waits=W(cl[2]),
                        emit=lambda sy, hp=hp, jc=jc, s0=si0 * 128, nsi=nsi:
                        sy.dma_start(out=vc[jc].rearrange("p (a b) -> p a b", a=8)[:, 0:nsi, :],
                                     in_=k.v0_buf[s0:s0 + nsi * 128, 256 * hp:256 * (hp + 1)]
                                     .rearrange("(a p) b -> p a b", p=128))))
                    for sl in range(nsi):
                        si = si0 + sl
                        c = t0 - 128 * si
                        ops.append(dict(eng="sy", name=f"d{gci}_{sl}",
                            waits=W(cl[1] if sl == 0 else None),
                            emit=lambda sy, jc=jc, sl=sl, idx=(c + 384) // 128:
                            sy.dma_start(out=dcc[jc][:, sl * 512:(sl + 1) * 512],
                                         in_=k.dec_buf[idx, :, :])))
                    for sl in range(nsi):
                        si = si0 + sl
                        c = t0 - 128 * si
                        diag = c <= 0
                        for h in range(2):
                            p2 = n % 2
                            scw = [f"stt{n - 2}"] if n >= 2 else []
                            if sl == 0 and h == 0:
                                scw += [f"k{gci}_0", f"k{gci}_1", f"q{kidx}_0",
                                        f"q{kidx}_1"]
                            ops.append(dict(eng="t", name=f"sc{n}", waits=scw,
                                emit=lambda tn, jc=jc, sl=sl, h=h, jq=kidx % 2, p2=p2:
                                tn.matmul(scP[p2][:], kc[jc][h][:, sl * 128:(sl + 1) * 128],
                                          qt[jq][h][:], start=True, stop=True)))
                            sttw = [f"sc{n}"]
                            if sl == 0 and h == 0:
                                sttw.append(f"d{gci}_{nsi - 1}")
                            if n >= 2:
                                sttw.append(f"exp{n - 2}")
                            ops.append(dict(eng="v", name=f"stt{n}", waits=sttw,
                                emit=lambda v, jc=jc, sl=sl, p2=p2:
                                v.scalar_tensor_tensor(sstt[p2][:], scP[p2][:], 30.0,
                                    dcc[jc][:, sl * 512:(sl + 1) * 512],
                                    ALU.min, ALU.mult)))
                            fname = f"stt{n}"
                            if diag:
                                am = (-c) // 128
                                ops.append(dict(eng="v", name=f"dg{n}",
                                    waits=[f"stt{n}"],
                                    emit=lambda v, p2=p2, am=am:
                                    v.tensor_add(sd2[p2][:], sstt[p2][:],
                                                 k.am_sb[:, am * 512:(am + 1) * 512])))
                                fname = f"dg{n}"
                            expw = [fname]
                            if n >= 2:
                                expw.append(f"av{n - 2}")
                            ops.append(dict(eng="s", name=f"exp{n}", waits=expw,
                                emit=lambda sc, p2=p2, diag=diag:
                                sc.activation(Pt[p2][:], (sd2 if diag else sstt)[p2][:],
                                              AF.Exp, bias=0.0, scale=1.0)))
                            rw = [f"exp{n}"]
                            if si == 0:
                                pk = tail.get((kidx - 1, h))
                                if pk:
                                    rw.append(pk["varrow"])
                            ops.append(dict(eng="t", name=f"r{n}", waits=rw,
                                emit=lambda tn, p2=p2, h=h, st=(si == 0), sp=(si == S - 1):
                                tn.matmul(rP[h][:], k.ones_bf[:], Pt[p2][:],
                                          start=st, stop=sp, skip_group_check=True)))
                            avw = []
                            if sl == 0 and h == 0:
                                avw.append(f"v{gci}")
                            if si == 0:
                                pk = tail.get((kidx - 1, h))
                                if pk:
                                    avw.append(pk["murow"])
                            ops.append(dict(eng="t", name=f"av{n}", waits=avw,
                                emit=lambda tn, jc=jc, sl=sl, h=h, p2=p2,
                                       st=(si == 0), sp=(si == S - 1):
                                tn.matmul(avP[h][:],
                                          vc[jc][:, sl * 256 + h * 128:sl * 256 + (h + 1) * 128],
                                          Pt[p2][:], start=st, stop=sp,
                                          skip_group_check=True)))
                            if h == 0 and sl == nsi - 1:
                                pass
                            n += 1
                    chunk_last[gci] = (f"sc{n - 1}", f"stt{n - 1}", f"av{n - 1}")
                    gci += 1
                k_last[kidx] = f"sc{n - 1}"
                for h in range(2):
                    kk = f"{kidx}_{h}"
                    ptl = tail.get((kidx, h - 1)) or tail.get((kidx - 1, 1))
                    pk = tail.get((kidx - 1, h))
                    gidx = kidx * 2 + h
                    t = {}
                    t["rinv"] = f"ri{kk}"
                    ops.append(dict(eng="v", name=f"ri{kk}",
                        waits=W(f"r{n - 2 + h}", pk["rb"] if pk else None),
                        emit=lambda v, h=h: v.reciprocal(rrow[h][:], rP[h][0:1, :])))
                    t["rb"] = f"rb{kk}"
                    ops.append(dict(eng="t", name=f"rb{kk}",
                        waits=W(f"ri{kk}", ptl["t2"] if ptl else None,
                                f"stt{n - 1}"),
                        emit=lambda tn, h=h: tn.matmul(bcP[0][:], k.onesf[:],
                                                       rrow[h][:], start=True,
                                                       stop=True)))
                    t["rbs"] = f"rv{kk}"
                    ops.append(dict(eng="s", name=f"rv{kk}",
                        waits=W(f"rb{kk}", pk["od"] if pk else None),
                        emit=lambda sc, h=h: sc.copy(rbs[h][:], bcP[0][:])))
                    t["od"] = f"od{kk}"
                    ops.append(dict(eng="v", name=f"od{kk}",
                        waits=W(f"rv{kk}", f"av{n - 2 + h}",
                                pk["t1"] if pk else None),
                        emit=lambda v, h=h: v.tensor_mul(od[h][:], avP[h][:],
                                                         rbs[h][:])))
                    if getattr(self, "dbg_p3", None) == (kidx, h):
                        dbgp3 = self.nc.dram_tensor("p3dbg", [4, 128, 512], F32).ap()
                        self._p3dbg = dbgp3
                        ops.append(dict(eng="sy", name="dbg_od", waits=[f"od{kk}"],
                            emit=lambda sy, h=h: sy.dma_start(
                                out=dbgp3[0, :, :].bitcast(BF16)[:, 0:512],
                                in_=od[h][:])))
                        ops.append(dict(eng="sy", name="dbg_rr", waits=[f"ri{kk}"],
                            emit=lambda sy, h=h: sy.dma_start(
                                out=dbgp3[1, 0:1, :], in_=rrow[h][:])))
                        ops.append(dict(eng="sy", name="dbg_pt", waits=[f"exp{n - 2 + h}"],
                            emit=lambda sy, p2=(n - 2 + h) % 2: sy.dma_start(
                                out=dbgp3[2, :, :].bitcast(BF16)[:, 0:512],
                                in_=Pt[p2][:])))

                    if getattr(self, "dbg_p3", None) == (kidx, h):
                        ops.append(dict(eng="sy", name="dbg_mu", waits=[f"mur{kk}"],
                            emit=lambda sy, h=h: sy.dma_start(
                                out=self._p3dbg[1, 1:2, :], in_=murow[h][:])))
                        ops.append(dict(eng="sy", name="dbg_va", waits=[f"var{kk}"],
                            emit=lambda sy, h=h: sy.dma_start(
                                out=self._p3dbg[1, 2:3, :], in_=varrow[h][:])))
                        ops.append(dict(eng="sy", name="dbg_rs", waits=[f"ex2{kk}"],
                            emit=lambda sy, h=h: sy.dma_start(
                                out=self._p3dbg[1, 3:4, :], in_=rsr[h][:])))
                        ops.append(dict(eng="sy", name="dbg_gn", waits=[f"gnf{kk}"],
                            emit=lambda sy, jg=(kidx * 2 + h) % 2: sy.dma_start(
                                out=self._p3dbg[3, :, 0:256].bitcast(BF16),
                                in_=gn[jg][:])))
                    t["sq"] = f"sqq{kk}"
                    ops.append(dict(eng="s", name=f"sqq{kk}",
                        waits=W(f"od{kk}", pk["st2"] if pk else None),
                        emit=lambda sc, h=h: sc.square(sq[h][:], od[h][:])))
                    t["st1"] = f"s1m{kk}"
                    ops.append(dict(eng="t", name=f"s1m{kk}", waits=[f"od{kk}"],
                        emit=lambda tn, h=h: tn.matmul(avP[h][:], k.ones_bf[:],
                                                       od[h][:], start=True,
                                                       stop=True)))
                    t["st2"] = f"s2m{kk}"
                    ops.append(dict(eng="t", name=f"s2m{kk}",
                        waits=[f"sqq{kk}", f"ri{kk}"],
                        emit=lambda tn, h=h: tn.matmul(rP[h][:], k.ones_bf[:],
                                                       sq[h][:], start=True,
                                                       stop=True)))
                    t["murow"] = f"mur{kk}"
                    ops.append(dict(eng="v", name=f"mur{kk}",
                        waits=W(f"s1m{kk}", pk["mubc"] if pk else None),
                        emit=lambda v, h=h: v.tensor_scalar(murow[h][:],
                            avP[h][0:1, :], 1.0 / 128, None, ALU.mult)))
                    ops.append(dict(eng="v", name=f"mu2{kk}", waits=[f"mur{kk}"],
                        emit=lambda v, h=h: v.tensor_mul(mu2row[h][:],
                            murow[h][:], murow[h][:])))
                    t["varrow"] = f"var{kk}"
                    ops.append(dict(eng="v", name=f"var{kk}",
                        waits=[f"s2m{kk}", f"mu2{kk}"],
                        emit=lambda v, h=h: v.scalar_tensor_tensor(varrow[h][:],
                            rP[h][0:1, :], 1.0 / 128, mu2row[h][:],
                            ALU.mult, ALU.subtract)))
                    ops.append(dict(eng="s", name=f"lnn{kk}", waits=[f"var{kk}"],
                        emit=lambda sc, h=h: sc.activation(lnr[h][:], varrow[h][:],
                            AF.Ln, bias=k.eps_t[0:1, 0:1], scale=1.0)))
                    ops.append(dict(eng="s", name=f"ex2{kk}", waits=[f"lnn{kk}"],
                        emit=lambda sc, h=h: sc.activation(rsr[h][:], lnr[h][:],
                            AF.Exp, bias=0.0, scale=-0.5)))
                    t["mubc"] = f"mbc{kk}"
                    ops.append(dict(eng="t", name=f"mbc{kk}",
                        waits=W(f"mur{kk}", ptl["t1"] if ptl else None),
                        emit=lambda tn, h=h: tn.matmul(bcP[1][:], k.onesf[:],
                            murow[h][:], start=True, stop=True)))
                    t["rsbc"] = f"rbc{kk}"
                    ops.append(dict(eng="t", name=f"rbc{kk}",
                        waits=[f"ex2{kk}", f"rv{kk}"],
                        emit=lambda tn, h=h: tn.matmul(bcP[0][:], k.onesf[:],
                            rsr[h][:], start=True, stop=True)))
                    t["t1"] = f"t1{kk}"
                    ops.append(dict(eng="v", name=f"t1{kk}", waits=[f"mbc{kk}"],
                        emit=lambda v, h=h: v.tensor_sub(t1b[h][:], od[h][:],
                                                         bcP[1][:])))
                    t["t2"] = f"t2{kk}"
                    gw = [f"t1{kk}", f"rbc{kk}"]
                    if gidx >= 2:
                        gw.append(gstore[gidx - 2])
                    ops.append(dict(eng="v", name=f"t2{kk}", waits=gw,
                        emit=lambda v, h=h, jg=gidx % 2: v.tensor_mul(
                            t1b[h][:], t1b[h][:], bcP[0][:])))
                    ops.append(dict(eng="v", name=f"gnf{kk}", waits=[f"t2{kk}"],
                        emit=lambda v, h=h, hl=2 * hp + h, jg=gidx % 2:
                        v.tensor_scalar(gn[jg][:], t1b[h][:], gg[:, hl:hl + 1],
                                        gb[:, hl:hl + 1], ALU.mult, ALU.add)))
                    gstore[gidx] = f"gst{gidx}"
                    ops.append(dict(eng="sy", name=f"gst{gidx}",
                        waits=[f"gnf{kk}"],
                        emit=lambda sy, hl=2 * hp + h, t0=t0, jg=gidx % 2:
                        sy.dma_start(out=k.cc1_in[128 * hl:128 * (hl + 1),
                                                  t0:t0 + 512], in_=gn[jg][:])))
                    tail[(kidx, h)] = t
                kidx += 1
        sems, cnt, ev = self.run_ops(ops, "p3")
        self.s_cc1 = self.sem("cc1")
        sy_total = cnt["sy"]
        s_cc1 = self.s_cc1

        @blk.gpsimd
        def _(gp):
            gp.wait_ge(sems["sy"], sy_total)
            gp.collective_compute(
                "AllGather", ALU.bypass,
                replica_groups=[[0, 1], [2, 3], [4, 5], [6, 7]],
                ins=[k.cc1_in.opt()], outs=[k.cc1_out.opt()],
            ).then_inc(s_cc1, 1)
        ctx.close()

    def p4_wo_res(self):
        self._proj_ln_res(self.w_o0, self.b_o0, self.cc1_out, self.s_cc1,
                          self.x_own, self.x1_buf, self.xln2_buf, "p4")

    def _proj_ln_res(self, w_src, b_src, g_src, cc_sem, x_src, x_out, xln_out,
                     tag):
        k = self
        blk = self.block
        ctx = contextlib.ExitStack()
        wo = self.sbuf(f"{tag}_wo", [128, 8 * 1024], BF16, ctx)
        bo = self.sbuf(f"{tag}_bo", [128, 8], F32, ctx)
        gt = [self.sbuf(f"{tag}_g{j}", [128, 8 * 512], BF16, ctx) for j in range(2)]
        ao = [self.sbuf(f"{tag}_ao{j}", [128, 8 * 512], F32, ctx) for j in range(2)]
        aob = [self.sbuf(f"{tag}_aob{j}", [128, 8 * 512], BF16, ctx) for j in range(2)]
        sqb = [self.sbuf(f"{tag}_sq{j}", [128, 8 * 512], BF16, ctx) for j in range(2)]
        t1 = self.sbuf(f"{tag}_t1", [128, 512], F32, ctx)
        brt = [self.sbuf(f"{tag}_br{j}", [128, 512], BF16, ctx) for j in range(2)]
        murow = self.sbuf(f"{tag}_mur", [1, 512], F32, ctx)
        mu2row = self.sbuf(f"{tag}_mu2", [1, 512], F32, ctx)
        varrow = self.sbuf(f"{tag}_var", [1, 512], F32, ctx)
        lnr = self.sbuf(f"{tag}_lnr", [1, 512], F32, ctx)
        rsr = self.sbuf(f"{tag}_rsr", [1, 512], F32, ctx)
        pj = [self.psum(f"{tag}_pj{j}", [128, 512], F32, ctx) for j in range(2)]
        stP = [self.psum(f"{tag}_st{j}", [128, 512], F32, ctx) for j in range(2)]
        bcP = [self.psum(f"{tag}_bc{j}", [128, 512], F32, ctx) for j in range(2)]

        ops = []
        W = lambda *names: [w for w in names if w is not None]
        box = {}
        ops.append(dict(eng="sy", name="ofs", dma=False, emit=lambda sy:
            (sy.wait_ge(cc_sem, 1),
             box.__setitem__("ofs", sy.compute_val((sy.partition_id() % 2) * TOWN)),
             sy.nop())[2]))
        ops.append(dict(eng="gp", name="wo", emit=lambda gp:
            gp.dma_start(out=wo.rearrange("p (dc o) -> p dc o", dc=8),
                         in_=w_src.rearrange("(dc p) o -> p dc o", p=128))))
        for oc in range(8):
            ops.append(dict(eng="sy", name=f"bo{oc}", emit=lambda sy, oc=oc:
                sy.dma_start(out=bo[:, oc:oc + 1],
                             in_=b_src.rearrange("(a b) -> a b", b=1)[
                                 128 * oc:128 * (oc + 1), :])))
        nmm = 0
        for i in range(NO512):
            j = i % 2
            ops.append(dict(eng="sy", name=f"g{i}",
                waits=W("ofs" if i == 0 else None,
                        f"mm{i - 2}_7_7" if i >= 2 else None),
                emit=lambda sy, i=i, j=j: sy.dma_start(
                    out=gt[j].rearrange("p (dc t) -> p dc t", dc=8),
                    in_=AP(g_src.tensor, box["ofs"] + i * 512,
                           [[T, 128], [128 * T, 8], [1, 512]]))))
            for oc in range(8):
                for dc in range(8):
                    w = []
                    if dc == 0:
                        if oc == 0:
                            w.append("wo")
                            w.append(f"g{i}")
                        if i >= 1 or oc >= 2:
                            pe = i * 8 + oc - 2
                            w.append(f"ev{pe // 8}_{pe % 8}")
                    ops.append(dict(eng="t", name=f"mm{i}_{oc}_{dc}", waits=w,
                        emit=lambda tn, j=j, oc=oc, dc=dc, b=(i * 8 + oc) % 2:
                        tn.matmul(pj[b][:],
                                  wo[:, dc * 1024 + oc * 128:dc * 1024 + (oc + 1) * 128],
                                  gt[j][:, dc * 512:(dc + 1) * 512],
                                  start=(dc == 0), stop=(dc == 7))))
                ops.append(dict(eng="s", name=f"ev{i}_{oc}",
                    waits=W(f"mm{i}_{oc}_7",
                            f"apa{i - 2}_{oc}" if i >= 2 else None),
                    emit=lambda sc, j=j, oc=oc, b=(i * 8 + oc) % 2:
                    sc.activation(ao[j][:, oc * 512:(oc + 1) * 512], pj[b][:],
                                  AF.Identity, bias=bo[:, oc:oc + 1], scale=1.0)))
                ops.append(dict(eng="v", name=f"cb{i}_{oc}", waits=[f"ev{i}_{oc}"],
                    emit=lambda v, j=j, oc=oc:
                    v.tensor_copy(aob[j][:, oc * 512:(oc + 1) * 512],
                                  ao[j][:, oc * 512:(oc + 1) * 512])))
                ops.append(dict(eng="s", name=f"sq{i}_{oc}", waits=[f"ev{i}_{oc}"],
                    emit=lambda sc, j=j, oc=oc:
                    sc.square(sqb[j][:, oc * 512:(oc + 1) * 512],
                              ao[j][:, oc * 512:(oc + 1) * 512])))
            for oc in range(8):
                ops.append(dict(eng="t", name=f"sm1{i}_{oc}",
                    waits=W(f"cb{i}_{oc}",
                            f"mu{i - 2}" if (oc == 0 and i >= 2) else None),
                    emit=lambda tn, j=j, oc=oc: tn.matmul(
                        stP[0][:], k.ones_bf[:], aob[j][:, oc * 512:(oc + 1) * 512],
                        start=(oc == 0), stop=(oc == 7), skip_group_check=True)))
                ops.append(dict(eng="t", name=f"sm2{i}_{oc}",
                    waits=W(f"sq{i}_{oc}",
                            f"va{i - 2}" if (oc == 0 and i >= 2) else None),
                    emit=lambda tn, j=j, oc=oc: tn.matmul(
                        stP[1][:], k.ones_bf[:], sqb[j][:, oc * 512:(oc + 1) * 512],
                        start=(oc == 0), stop=(oc == 7), skip_group_check=True)))
            ops.append(dict(eng="v", name=f"mu{i}", waits=[f"sm1{i}_7"],
                emit=lambda v: v.tensor_scalar(murow[:], stP[0][0:1, :], 1.0 / D,
                                               None, ALU.mult)))
            ops.append(dict(eng="v", name=f"mu2{i}", waits=[f"mu{i}"],
                emit=lambda v: v.tensor_mul(mu2row[:], murow[:], murow[:])))
            ops.append(dict(eng="v", name=f"va{i}", waits=[f"sm2{i}_7", f"mu2{i}"],
                emit=lambda v: v.scalar_tensor_tensor(varrow[:], stP[1][0:1, :],
                    1.0 / D, mu2row[:], ALU.mult, ALU.subtract)))
            ops.append(dict(eng="s", name=f"ln{i}", waits=[f"va{i}"],
                emit=lambda sc: sc.activation(lnr[:], varrow[:], AF.Ln,
                                              bias=k.eps_t[0:1, 0:1], scale=1.0)))
            ops.append(dict(eng="s", name=f"ex{i}", waits=[f"ln{i}"],
                emit=lambda sc: sc.activation(rsr[:], lnr[:], AF.Exp, bias=0.0,
                                              scale=-0.5)))
            ops.append(dict(eng="t", name=f"mb{i}",
                waits=W(f"mu{i}", f"ap{i - 1}_7" if i >= 1 else None),
                emit=lambda tn: tn.matmul(bcP[0][:], k.onesf[:], murow[:],
                                          start=True, stop=True)))
            ops.append(dict(eng="t", name=f"rb{i}", waits=[f"ex{i}"],
                emit=lambda tn: tn.matmul(bcP[1][:], k.onesf[:], rsr[:],
                                          start=True, stop=True)))
            for oc in range(8):
                g = i * 8 + oc
                ops.append(dict(eng="v", name=f"apa{i}_{oc}",
                    waits=W(f"mb{i}" if oc == 0 else None,
                            f"va{i}" if oc == 0 else None,
                            f"apb{i}_{oc - 1}" if oc >= 1 else
                            (f"apb{i - 1}_7" if i >= 1 else None)),
                    emit=lambda v, j=j, oc=oc: v.tensor_sub(
                        t1[:], ao[j][:, oc * 512:(oc + 1) * 512], bcP[0][:])))
                ops.append(dict(eng="v", name=f"apb{i}_{oc}",
                    waits=W(f"apa{i}_{oc}", f"rb{i}" if oc == 0 else None,
                            f"bst{g - 2}" if g >= 2 else None),
                    emit=lambda v, g=g: v.tensor_mul(brt[g % 2][:], t1[:],
                                                     bcP[1][:])))
                ops.append(dict(eng="v", name=f"ap{i}_{oc}", dma=False, inc=1,
                    waits=[], emit=lambda v: v.engine_nop()))
                ops.append(dict(eng="sy", name=f"bst{g}", waits=[f"apb{i}_{oc}"],
                    emit=lambda sy, i=i, oc=oc, g=g: sy.dma_start(
                        out=k.br_buf[oc * 128:(oc + 1) * 128,
                                     i * 512:(i + 1) * 512],
                        in_=brt[g % 2][:])))
        self.run_ops(ops, tag)
        ctx.close()
        self.barrier()
        self._ln_pass(x_src, TOWN, out_bf=xln_out, tag=f"{tag}b",
                      br_src=self.br_buf, x_out=x_out)

    def p5_ffn0(self):
        k = self
        blk = self.block
        ctx = contextlib.ExitStack()
        wT = self.sbuf("p5_wT", [128, 6 * 8 * 1024], BF16, ctx)
        lbT = self.sbuf("p5_lbT", [128, 6 * 1024], BF16, ctx)
        laT = self.sbuf("p5_laT", [128, 6 * 8 * 32], BF16, ctx)
        qb = self.sbuf("p5_qb", [128, 1024], BF16, ctx)
        scs = self.sbuf("p5_scs", [128, 8 * 64], F32, ctx)
        wtmp = self.sbuf("p5_wtmp", [128, 1024], F32, ctx)
        wod = self.sbuf("p5_wod", [128, 1024], BF16, ctx)
        lbsc = self.sbuf("p5_lbsc", [128, 8 * 32], BF16, ctx)
        bi_sb = self.sbuf("p5_bi", [128, 48], F32, ctx)
        lg_sb = self.sbuf("p5_lg", [128, 24], F32, ctx)
        lbe_sb = self.sbuf("p5_lbe", [128, 24], F32, ctx)
        hb = self.sbuf("p5_hb", [128, 8 * 512], BF16, ctx)
        hx = self.sbuf("p5_hx", [128, 8 * 512], BF16, ctx)
        hf = [self.sbuf(f"p5_hf{j}", [128, 8 * 512], F32, ctx) for j in range(2)]
        y1 = self.sbuf("p5_y1", [128, 8 * 512], F32, ctx)
        sqb = self.sbuf("p5_sqb", [128, 2 * 512], BF16, ctx)
        usb = self.sbuf("p5_usb", [128, 512], BF16, ctx)
        t1 = self.sbuf("p5_t1", [128, 512], F32, ctx)
        brt = self.sbuf("p5_brt", [128, 512], BF16, ctx)
        murow = self.sbuf("p5_mur", [1, 512], F32, ctx)
        rowA = self.sbuf("p5_rwa", [1, 512], F32, ctx)
        rowB = self.sbuf("p5_rwb", [1, 512], F32, ctx)
        rsr = self.sbuf("p5_rsr", [1, 512], F32, ctx)
        pA = [self.psum(f"p5_pA{j}", [128, 512], F32, ctx) for j in range(2)]
        pU = self.psum("p5_pU", [128, 512], F32, ctx)
        pT = self.psum("p5_pT", [128, 128], BF16, ctx)
        stP = [self.psum(f"p5_st{j}", [128, 512], F32, ctx) for j in range(2)]
        bcP = [self.psum(f"p5_bc{j}", [128, 512], F32, ctx) for j in range(2)]

        ops = []
        W = lambda *names: [w for w in names if w is not None]

        ops.append(dict(eng="gp", name="z0", dma=False,
                        emit=lambda gp: gp.memset(lbT[:], 0.0)))
        ops.append(dict(eng="gp", name="z1", dma=False,
                        emit=lambda gp: gp.memset(usb[:], 0.0)))
        for i in range(6):
            ops.append(dict(eng="gp", name=f"ws{i}",
                waits=W(f"dq{i - 1}_7" if i >= 1 else None),
                emit=lambda gp, i=i: gp.dma_start(
                    out=scs.rearrange("p (oc g) -> p oc g", oc=8),
                    in_=k.f_sc[i].rearrange("(oc p) g -> p oc g", p=128))))
            for oc in range(8):
                ops.append(dict(eng="gp", name=f"wq{i}_{oc}",
                    waits=W(f"dqa{i}_{oc - 1}" if oc >= 1 else
                            (f"dqa{i - 1}_7" if i >= 1 else None),
                            f"lap{i - 1}_7" if (i >= 1 and oc == 0) else None),
                    emit=lambda gp, i=i, oc=oc: gp.dma_start(
                        out=qb[:],
                        in_=k.f_qw[i].rearrange("(oc p) d -> oc p d", p=128)[oc])))
                ops.append(dict(eng="v", name=f"dqa{i}_{oc}",
                    waits=W(f"wq{i}_{oc}", f"ws{i}" if oc == 0 else None,
                            f"dq{i}_{oc - 1}" if oc >= 1 else
                            (f"dq{i - 1}_7" if i >= 1 else None)),
                    emit=lambda v: v.tensor_scalar(
                        wtmp[:], qb[:], 1.0 / 7.5, -1.0, ALU.mult, ALU.add)))
                ops.append(dict(eng="v", name=f"dq{i}_{oc}",
                    waits=W(f"dqa{i}_{oc}",
                            f"tp{i}_{oc - 1}_7" if oc >= 1 else
                            (f"tp{i - 1}_7_7" if i >= 1 else None)),
                    emit=lambda v, oc=oc: v.tensor_tensor(
                        wod[:], wtmp[:],
                        AP(scs, oc * 64, [[8 * 64, 128], [1, 64], [0, 16]]),
                        ALU.mult)))
                for dc in range(8):
                    ops.append(dict(eng="t", name=f"tp{i}_{oc}_{dc}",
                        waits=W(f"dq{i}_{oc}" if dc == 0 else None,
                                f"tr{i}_{oc}_{dc - 1}" if dc >= 1 else
                                (f"tr{i}_{oc - 1}_7" if oc >= 1 else
                                 (f"tr{i - 1}_7_7" if i >= 1 else None))),
                        emit=lambda tn, dc=dc: tn.transpose(
                            pT[:], wod[:, dc * 128:(dc + 1) * 128], k.ident[:])))
                    ops.append(dict(eng="v", name=f"tr{i}_{oc}_{dc}",
                        waits=[f"tp{i}_{oc}_{dc}"],
                        emit=lambda v, i=i, oc=oc, dc=dc: v.tensor_copy(
                            wT[:, i * 8192 + dc * 1024 + oc * 128:
                               i * 8192 + dc * 1024 + (oc + 1) * 128], pT[:])))
            ops.append(dict(eng="gp", name=f"wla{i}",
                waits=[f"dqa{i}_7"],
                emit=lambda gp, i=i: gp.dma_start(out=qb[0:32, :], in_=k.f_la[i])))
            for dc in range(8):
                ops.append(dict(eng="t", name=f"lap{i}_{dc}",
                    waits=W(f"wla{i}" if dc == 0 else None,
                            f"lat{i}_{dc - 1}" if dc >= 1 else
                            (f"lat{i - 1}_7" if i >= 1 else None),
                            f"tp{i}_7_7" if dc == 0 else None),
                    emit=lambda tn, dc=dc: tn.transpose(
                        pT[:, 0:32], qb[0:32, dc * 128:(dc + 1) * 128],
                        k.ident[0:32, 0:32])))
                ops.append(dict(eng="v", name=f"lat{i}_{dc}",
                    waits=[f"lap{i}_{dc}"],
                    emit=lambda v, i=i, dc=dc: v.tensor_copy(
                        laT[:, i * 256 + dc * 32:i * 256 + (dc + 1) * 32],
                        pT[:, 0:32])))
            ops.append(dict(eng="gp", name=f"wlb{i}",
                waits=W(f"lbt{i - 1}_7" if i >= 1 else None),
                emit=lambda gp, i=i: gp.dma_start(
                    out=lbsc.rearrange("p (oc r) -> p oc r", oc=8),
                    in_=k.f_lb[i].rearrange("(oc p) r -> p oc r", p=128))))
            for oc in range(8):
                ops.append(dict(eng="t", name=f"lbp{i}_{oc}",
                    waits=W(f"wlb{i}" if oc == 0 else None,
                            f"lbt{i}_{oc - 1}" if oc >= 1 else
                            (f"lbt{i - 1}_7" if i >= 1 else None),
                            f"lap{i}_7" if oc == 0 else None),
                    emit=lambda tn, oc=oc: tn.transpose(
                        pT[0:32, :], lbsc[:, oc * 32:(oc + 1) * 32], k.ident[:])))
                ops.append(dict(eng="v", name=f"lbt{i}_{oc}",
                    waits=W(f"lbp{i}_{oc}", "z0" if i == 0 and oc == 0 else None),
                    emit=lambda v, i=i, oc=oc: v.tensor_copy(
                        lbT[0:32, i * 1024 + oc * 128:i * 1024 + (oc + 1) * 128],
                        pT[0:32, 0:128])))
        for i6 in range(6):
            for oc in range(8):
                ops.append(dict(eng="sy", name=f"bi{i6}_{oc}",
                    emit=lambda sy, i6=i6, oc=oc: sy.dma_start(
                        out=bi_sb[:, i6 * 8 + oc:i6 * 8 + oc + 1],
                        in_=k.f_bi[i6].rearrange("(a b) -> a b", b=1)[
                            128 * oc:128 * (oc + 1), :])))
        for b3 in range(3):
            for oc in range(8):
                ops.append(dict(eng="sy", name=f"lg{b3}_{oc}",
                    emit=lambda sy, b3=b3, oc=oc: sy.dma_start(
                        out=lg_sb[:, b3 * 8 + oc:b3 * 8 + oc + 1],
                        in_=k.f_lg[b3].rearrange("(a b) -> a b", b=1)[
                            128 * oc:128 * (oc + 1), :])))
                ops.append(dict(eng="sy", name=f"lbe{b3}_{oc}",
                    emit=lambda sy, b3=b3, oc=oc: sy.dma_start(
                        out=lbe_sb[:, b3 * 8 + oc:b3 * 8 + oc + 1],
                        in_=k.f_lbe[b3].rearrange("(a b) -> a b", b=1)[
                            128 * oc:128 * (oc + 1), :])))

        def emit_qlora(i, b3, qi, src, dst, extra_in_wait, jj=None):
            if jj is None:
                jj = i % 2
            tag = f"i{i}b{b3}q{qi % 2}"
            for dc in range(8):
                ops.append(dict(eng="t", name=f"u{tag}_{dc}",
                    waits=W(extra_in_wait if dc == 0 else None,
                            f"ue{tag}_prevfree" if False else None),
                    emit=lambda tn, qi=qi, dc=dc, src=src: tn.matmul(
                        pU[0:32, :],
                        laT[:, qi * 256 + dc * 32:qi * 256 + (dc + 1) * 32],
                        src[:, dc * 512:(dc + 1) * 512],
                        start=(dc == 0), stop=(dc == 7))))
            ops.append(dict(eng="v", name=f"ue{tag}",
                waits=W(f"u{tag}_7", "z1" if (i == 0 and b3 == 0 and qi == 0)
                        else None),
                emit=lambda v: v.tensor_copy(usb[0:32, :], pU[0:32, :])))
            for oc in range(8):
                g = oc % 2
                for dc in range(8):
                    ops.append(dict(eng="t", name=f"m{tag}_{oc}_{dc}",
                        waits=W((f"ev{tag}_{oc - 2}" if oc >= 2 else None)
                                if dc == 0 else None),
                        emit=lambda tn, qi=qi, oc=oc, dc=dc, g=g, src=src:
                        tn.matmul(pA[g][:],
                                  wT[:, qi * 8192 + dc * 1024 + oc * 128:
                                     qi * 8192 + dc * 1024 + (oc + 1) * 128],
                                  src[:, dc * 512:(dc + 1) * 512],
                                  start=(dc == 0), stop=False)))
                ops.append(dict(eng="t", name=f"ml{tag}_{oc}",
                    waits=W(f"ue{tag}" if oc == 0 else None),
                    emit=lambda tn, qi=qi, oc=oc, g=g: tn.matmul(
                        pA[g][:],
                        lbT[:, qi * 1024 + oc * 128:qi * 1024 + (oc + 1) * 128],
                        usb[:], start=False, stop=True)))
                if dst == "y1":
                    ops.append(dict(eng="s", name=f"ev{tag}_{oc}",
                        waits=[f"ml{tag}_{oc}"],
                        emit=lambda sc, qi=qi, oc=oc, g=g: sc.activation(
                            y1[:, oc * 512:(oc + 1) * 512], pA[g][:], AF.Identity,
                            bias=bi_sb[:, qi * 8 + oc:qi * 8 + oc + 1], scale=1.0)))
                else:
                    ops.append(dict(eng="v", name=f"ev{tag}_{oc}",
                        waits=[f"ml{tag}_{oc}"],
                        emit=lambda v, qi=qi, oc=oc, g=g, jj=jj:
                        v.scalar_tensor_tensor(
                            hf[1 - jj][:, oc * 512:(oc + 1) * 512], pA[g][:],
                            bi_sb[:, qi * 8 + oc:qi * 8 + oc + 1],
                            hf[jj][:, oc * 512:(oc + 1) * 512], ALU.add, ALU.add)))
            return tag

        for i in range(NO512):
            jj = i % 2
            for dc in range(8):
                ops.append(dict(eng="sy", name=f"x{i}_{dc}",
                    waits=W((f"xmld{i - 1}" if i >= 1 else "tr5_7_7")
                            if dc == 0 else None),
                    emit=lambda sy, i=i, dc=dc: sy.dma_start(
                        out=hb[:, dc * 512:(dc + 1) * 512],
                        in_=k.xln2_buf[i * 512:(i + 1) * 512,
                                       dc * 128:(dc + 1) * 128], transpose=True)))
            ops.append(dict(eng="v", name=f"hcast{i}",
                waits=W(f"x{i}_7", f"evi{i - 1}b2done" if i >= 1 else None,
                        f"pbb{i - 1}_7" if i >= 1 else None,
                        f"psq{i - 1}_7" if i >= 1 else None,
                        f"psm1{i - 1}_7" if i >= 1 else None),
                emit=lambda v, jj=jj: v.tensor_copy(hf[jj][:], hb[:])))
            for b3 in range(3):
                qi0, qi1 = 2 * b3, 2 * b3 + 1
                tagp = f"i{i}b{b3}"
                src_a = hb if b3 == 0 else hx
                in_wait = f"hcast{i}" if b3 == 0 else f"hbc{i}_{b3 - 1}"
                tA = emit_qlora(i, b3, qi0, src_a, "y1", in_wait)
                for oc in range(8):
                    ops.append(dict(eng="s", name=f"sq{tagp}_{oc}",
                        waits=W(f"ev{tA}_{oc}",
                                f"sm2{tagp}_{oc - 2}" if oc >= 2 else None),
                        emit=lambda sc, oc=oc: sc.square(
                            sqb[:, (oc % 2) * 512:(oc % 2 + 1) * 512],
                            y1[:, oc * 512:(oc + 1) * 512])))
                    ops.append(dict(eng="t", name=f"sm1{tagp}_{oc}",
                        waits=[f"ev{tA}_{oc}"],
                        emit=lambda tn, oc=oc: tn.matmul(
                            stP[0][:], k.ones_f[:],
                            y1[:, oc * 512:(oc + 1) * 512],
                            start=(oc == 0), stop=(oc == 7),
                            skip_group_check=True)))
                    ops.append(dict(eng="t", name=f"sm2{tagp}_{oc}",
                        waits=[f"sq{tagp}_{oc}"],
                        emit=lambda tn, oc=oc: tn.matmul(
                            stP[1][:], k.ones_bf[:],
                            sqb[:, (oc % 2) * 512:(oc % 2 + 1) * 512],
                            start=(oc == 0), stop=(oc == 7),
                            skip_group_check=True)))
                if getattr(self, "dbg_p5", False) and i == 0 and b3 == 0:
                    dbg5 = self.nc.dram_tensor("p5dbg", [3, 128, 4096], F32).ap()
                    self._p5dbg = dbg5
                    ops.append(dict(eng="sy", name="dbgy1", waits=[f"ev{tA}_7"],
                        emit=lambda sy: sy.dma_start(out=dbg5[0], in_=y1[:])))
                    ops.append(dict(eng="sy", name="dbgu",
                        waits=[f"ue{tA}" if False else f"uei0b0q0"],
                        emit=lambda sy: sy.dma_start(
                            out=dbg5[1, :, 0:256].bitcast(BF16),
                            in_=usb[:])))
                    ops.append(dict(eng="sy", name="dbghb", waits=[f"hcast0"],
                        emit=lambda sy: sy.dma_start(
                            out=dbg5[2, :, 0:2048].bitcast(BF16), in_=hb[:])))
                    dbgw = self.nc.dram_tensor("p5dbgw", [128, 8192], BF16).ap()
                    self._p5dbgw = dbgw
                    ops.append(dict(eng="sy", name="dbgwt3", waits=["tr3_7_7"],
                        emit=lambda sy: sy.dma_start(
                            out=dbg5c[1, :, 0:4096].bitcast(BF16),
                            in_=wT[:, 3 * 8192:3 * 8192 + 8192])))
                    ops.append(dict(eng="sy", name="dbgwt", waits=["tr0_7_7"],
                        emit=lambda sy: sy.dma_start(
                            out=dbgw[:, :], in_=wT[:, 0:8192])))
                    dbg5b = self.nc.dram_tensor("p5dbg2", [2, 128, 4096], F32).ap()
                    self._p5dbg2 = dbg5b
                    ops.append(dict(eng="sy", name="dbggel", waits=["gelxi0b0"],
                        emit=lambda sy: sy.dma_start(
                            out=dbg5b[0, :, 0:2048].bitcast(BF16), in_=hb[:])))
                    ops.append(dict(eng="sy", name="dbghf", waits=["evi0b0done"],
                        emit=lambda sy: sy.dma_start(
                            out=dbg5b[1, :, :], in_=hf[1][:])))
                    dbg5c = self.nc.dram_tensor("p5dbg3", [2, 128, 4096], F32).ap()
                    self._p5dbg3 = dbg5c
                    ops.append(dict(eng="gp", name="zzc", dma=False,
                        emit=lambda gp: gp.memset(hx[0:1, 0:1], 0.0)))
                    dbg5d = self.nc.dram_tensor("p5dbg4", [2, 128, 4096], F32).ap()
                    self._p5dbg4 = dbg5d
                    ops.append(dict(eng="sy", name="dbghx", waits=["hbc0_0"],
                        emit=lambda sy: sy.dma_start(
                            out=dbg5d[0, :, 0:2048].bitcast(BF16), in_=hx[:])))
                    ops.append(dict(eng="sy", name="dbgy1b1", waits=["evi0b1q0_7"],
                        emit=lambda sy: sy.dma_start(
                            out=dbg5d[1, :, :], in_=y1[:])))
                    ops.append(dict(eng="sy", name="dbghf1b", waits=["evi0b1done"],
                        emit=lambda sy: sy.dma_start(
                            out=dbg5c[0, :, :], in_=hf[0][:])))
                ops.append(dict(eng="v", name=f"mu{tagp}", waits=[f"sm1{tagp}_7"],
                    emit=lambda v: v.tensor_scalar(murow[:], stP[0][0:1, :],
                                                   1.0 / D, None, ALU.mult)))
                ops.append(dict(eng="v", name=f"mu2{tagp}", waits=[f"mu{tagp}"],
                    emit=lambda v: v.tensor_mul(rowA[:], murow[:], murow[:])))
                ops.append(dict(eng="v", name=f"va{tagp}",
                    waits=[f"sm2{tagp}_7", f"mu2{tagp}"],
                    emit=lambda v: v.scalar_tensor_tensor(rowB[:], stP[1][0:1, :],
                        1.0 / D, rowA[:], ALU.mult, ALU.subtract)))
                ops.append(dict(eng="s", name=f"lnv{tagp}", waits=[f"va{tagp}"],
                    emit=lambda sc: sc.activation(rowA[:], rowB[:], AF.Ln,
                                                  bias=k.eps_t[0:1, 0:1],
                                                  scale=1.0)))
                ops.append(dict(eng="s", name=f"exv{tagp}", waits=[f"lnv{tagp}"],
                    emit=lambda sc: sc.activation(rsr[:], rowA[:], AF.Exp,
                                                  bias=0.0, scale=-0.5)))
                ops.append(dict(eng="t", name=f"mb{tagp}", waits=[f"mu{tagp}"],
                    emit=lambda tn: tn.matmul(bcP[0][:], k.onesf[:], murow[:],
                                              start=True, stop=True)))
                ops.append(dict(eng="t", name=f"rb{tagp}", waits=[f"exv{tagp}"],
                    emit=lambda tn: tn.matmul(bcP[1][:], k.onesf[:], rsr[:],
                                              start=True, stop=True)))
                for oc in range(8):
                    ops.append(dict(eng="v", name=f"ga{tagp}_{oc}",
                        waits=W(f"mb{tagp}" if oc == 0 else None,
                                f"rb{tagp}" if oc == 0 else None,
                                f"gel{tagp}_{oc - 1}" if oc >= 1 else None),
                        emit=lambda v, oc=oc: v.tensor_sub(
                            t1[:], y1[:, oc * 512:(oc + 1) * 512], bcP[0][:])))
                    ops.append(dict(eng="v", name=f"gb{tagp}_{oc}",
                        waits=[f"ga{tagp}_{oc}"],
                        emit=lambda v: v.tensor_mul(t1[:], t1[:], bcP[1][:])))
                    ops.append(dict(eng="s", name=f"gel{tagp}_{oc}",
                        waits=W(f"gb{tagp}_{oc}",
                                f"m{tA}_7_7" if oc == 0 else None),
                        emit=lambda sc, b3=b3, oc=oc: sc.activation(
                            hb[:, oc * 512:(oc + 1) * 512], t1[:], AF.Gelu,
                            bias=lbe_sb[:, b3 * 8 + oc:b3 * 8 + oc + 1],
                            scale=lg_sb[:, b3 * 8 + oc:b3 * 8 + oc + 1])))
                ops.append(dict(eng="s", name=f"gelx{tagp}", inc=1,
                    waits=[f"gel{tagp}_7"], emit=lambda sc: sc.nop()))
                tB = emit_qlora(i, b3, qi1, hb, "res", f"gelx{tagp}", jj=jj)
                ops.append(dict(eng="v", name=f"evi{i}b{b3}done", inc=1, dma=False,
                    waits=[f"ev{tB}_7"], emit=lambda v: v.engine_nop()))
                jj = 1 - jj
                if b3 < 2:
                    dbgw2 = ["dbghx", "dbgy1b1", "dbghf1b"] if (
                        getattr(self, "dbg_p5", False) and i == 0 and b3 == 1) else []
                    if getattr(self, "dbg_p5", False) and i == 0 and b3 == 1:
                        pass
                    ops.append(dict(eng="v", name=f"hbc{i}_{b3}",
                        waits=W(f"evi{i}b{b3}done",
                                f"m{f'i{i}b{b3}q0'}_7_7" if b3 >= 1 else None,
                                f"u{f'i{i}b{b3}q0'}_7" if b3 >= 1 else None,
                                *dbgw2),
                        emit=lambda v, jj=jj: v.tensor_copy(hx[:], hf[jj][:])))
            ops.append(dict(eng="t", name=f"xmld{i}", inc=1, dma=False,
                waits=[f"evi{i}b2done"], emit=lambda tn: tn.nop()))
            for oc in range(8):
                ops.append(dict(eng="s", name=f"psq{i}_{oc}",
                    waits=W(f"evi{i}b2done",
                            f"psm2{i}_{oc - 2}" if oc >= 2 else None),
                    emit=lambda sc, jj=jj, oc=oc: sc.square(
                        sqb[:, (oc % 2) * 512:(oc % 2 + 1) * 512],
                        hf[jj][:, oc * 512:(oc + 1) * 512])))
                ops.append(dict(eng="t", name=f"psm1{i}_{oc}",
                    waits=W(f"evi{i}b2done" if oc == 0 else None),
                    emit=lambda tn, jj=jj, oc=oc: tn.matmul(
                        stP[0][:], k.ones_f[:],
                        hf[jj][:, oc * 512:(oc + 1) * 512],
                        start=(oc == 0), stop=(oc == 7), skip_group_check=True)))
                ops.append(dict(eng="t", name=f"psm2{i}_{oc}",
                    waits=[f"psq{i}_{oc}"],
                    emit=lambda tn, oc=oc: tn.matmul(
                        stP[1][:], k.ones_bf[:],
                        sqb[:, (oc % 2) * 512:(oc % 2 + 1) * 512],
                        start=(oc == 0), stop=(oc == 7), skip_group_check=True)))
            ops.append(dict(eng="v", name=f"pmu{i}", waits=[f"psm1{i}_7"],
                emit=lambda v: v.tensor_scalar(murow[:], stP[0][0:1, :], 1.0 / D,
                                               None, ALU.mult)))
            ops.append(dict(eng="v", name=f"pmu2{i}", waits=[f"pmu{i}"],
                emit=lambda v: v.tensor_mul(rowA[:], murow[:], murow[:])))
            ops.append(dict(eng="v", name=f"pva{i}", waits=[f"psm2{i}_7", f"pmu2{i}"],
                emit=lambda v: v.scalar_tensor_tensor(rowB[:], stP[1][0:1, :],
                    1.0 / D, rowA[:], ALU.mult, ALU.subtract)))
            ops.append(dict(eng="s", name=f"pln{i}", waits=[f"pva{i}"],
                emit=lambda sc: sc.activation(rowA[:], rowB[:], AF.Ln,
                                              bias=k.eps_t[0:1, 0:1], scale=1.0)))
            ops.append(dict(eng="s", name=f"pex{i}", waits=[f"pln{i}"],
                emit=lambda sc: sc.activation(rsr[:], rowA[:], AF.Exp, bias=0.0,
                                              scale=-0.5)))
            ops.append(dict(eng="t", name=f"pmb{i}", waits=[f"pmu{i}"],
                emit=lambda tn: tn.matmul(bcP[0][:], k.onesf[:], murow[:],
                                          start=True, stop=True)))
            ops.append(dict(eng="t", name=f"prb{i}", waits=[f"pex{i}"],
                emit=lambda tn: tn.matmul(bcP[1][:], k.onesf[:], rsr[:],
                                          start=True, stop=True)))
            for oc in range(8):
                no = i * 8 + oc
                ops.append(dict(eng="v", name=f"pba{i}_{oc}",
                    waits=W(f"pmb{i}" if oc == 0 else None,
                            f"prb{i}" if oc == 0 else None,
                            f"pbb{i}_{oc - 1}" if oc >= 1 else None),
                    emit=lambda v, jj=jj, oc=oc: v.tensor_sub(
                        t1[:], hf[jj][:, oc * 512:(oc + 1) * 512], bcP[0][:])))
                ops.append(dict(eng="v", name=f"pbb{i}_{oc}",
                    waits=W(f"pba{i}_{oc}",
                            f"bst{no - 1}" if no >= 1 else None),
                    emit=lambda v: v.tensor_mul(brt[:], t1[:], bcP[1][:])))
                ops.append(dict(eng="sy", name=f"bst{no}",
                    waits=[f"pbb{i}_{oc}"],
                    emit=lambda sy, i=i, oc=oc: sy.dma_start(
                        out=k.br_buf[oc * 128:(oc + 1) * 128,
                                     i * 512:(i + 1) * 512], in_=brt[:])))
        self.run_ops(ops, "p5")
        ctx.close()
        self.barrier()
        self._res_pass(self.x1_buf, self.br_buf, self.x2_buf, self.cc2_in, "p5b")
        self.s_cc2 = self.sem("cc2")
        s_cc2 = self.s_cc2
        k2 = self

        @blk.gpsimd
        def _(gp):
            gp.wait_ge(self._res_done_sem, self._res_done_tgt)
            gp.collective_compute(
                "AllGather", ALU.bypass,
                replica_groups=[[0, 1], [2, 3], [4, 5], [6, 7]],
                ins=[k2.cc2_in.opt()], outs=[k2.cc2_out.opt()],
            ).then_inc(s_cc2, 1)

    def _res_pass(self, x_src, br_src, x_out, xbf_out, tag):
        k = self
        blk = self.block
        ctx = contextlib.ExitStack()
        nt = TOWN // 128
        xin = [self.sbuf(f"{tag}_xin{j}", [128, D], F32, ctx) for j in range(2)]
        brt = [self.sbuf(f"{tag}_brt{j}", [128, D], BF16, ctx) for j in range(2)]
        xrs = [self.sbuf(f"{tag}_xrs{j}", [128, D], F32, ctx) for j in range(2)]
        xbf = [self.sbuf(f"{tag}_xbf{j}", [128, D], BF16, ctx) for j in range(2)]
        s_ld = self.sem(f"{tag}ld")
        s_ad = self.sem(f"{tag}ad")
        s_sd = self.sem(f"{tag}sd")

        has_bf = xbf_out is not None
        spt = 32 if has_bf else 16
        apt = 2 if has_bf else 1

        @blk.sync
        def _(sy):
            for i in range(nt):
                j = i % 2
                if i >= 2:
                    sy.wait_ge(s_ad, apt * (i - 1))
                sy.dma_start(out=xin[j][:], in_=x_src[i * 128:(i + 1) * 128, :]
                             ).then_inc(s_ld, 16)
                sy.dma_start(out=brt[j][:], in_=br_src[:, i * 128:(i + 1) * 128],
                             transpose=True).then_inc(s_ld, 16)
                if i > 0:
                    sy.wait_ge(s_ad, apt * i)
                    sy.dma_start(out=x_out[(i - 1) * 128:i * 128, :],
                                 in_=xrs[(i - 1) % 2][:]).then_inc(s_sd, 16)
                    if has_bf:
                        sy.dma_start(out=xbf_out[(i - 1) * 128:i * 128, :],
                                     in_=xbf[(i - 1) % 2][:]).then_inc(s_sd, 16)
            sy.wait_ge(s_ad, apt * nt)
            sy.dma_start(out=x_out[(nt - 1) * 128:nt * 128, :],
                         in_=xrs[(nt - 1) % 2][:]).then_inc(s_sd, 16)
            if has_bf:
                sy.dma_start(out=xbf_out[(nt - 1) * 128:nt * 128, :],
                             in_=xbf[(nt - 1) % 2][:]).then_inc(s_sd, 16)
            sy.wait_ge(s_sd, spt * nt)

        @blk.vector
        def _(v):
            for i in range(nt):
                j = i % 2
                v.wait_ge(s_ld, 32 * (i + 1))
                if i >= 2:
                    v.wait_ge(s_sd, spt * (i - 1))
                v.tensor_add(xrs[j][:], xin[j][:], brt[j][:]).then_inc(s_ad, 1)
                if has_bf:
                    v.wait_ge(s_ad, 2 * i + 1)
                    v.tensor_copy(xbf[j][:], xrs[j][:]).then_inc(s_ad, 1)
        self._res_done_sem = s_sd
        self._res_done_tgt = spt * nt
        ctx.close()

    def p6_proj1(self):
        k = self
        blk = self.block
        ctx0 = contextlib.ExitStack()
        self._p6_gate(ctx0)
        self._ln_pass(self.cc2_out, T, out_bf=self.xln3_buf, tag="p6a",
                      src_dt=BF16)
        self.barrier()
        ctx0.close()
        ctx = contextlib.ExitStack()
        wsrc = self.sbuf("p6_wsrc", [128, 4 * 1024], BF16, ctx)
        wqT = self.sbuf("p6_wqT", [128, 8 * 512], BF16, ctx)
        wkT = self.sbuf("p6_wkT", [128, 8 * 512], BF16, ctx)
        wvT = self.sbuf("p6_wvT", [128, 8 * 512], BF16, ctx)
        xf = self.sbuf("p6_xf", [128, 8 * 512], BF16, ctx)
        zmin = self.sbuf("p6_zmin", [128, 512], F32, ctx)
        ex = self.sbuf("p6_ex", [128, 512], F32, ctx)
        fmo = [self.sbuf(f"p6_fm{j}", [128, 512], BF16, ctx) for j in range(2)]
        kpl = self.sbuf("p6_kpl", [128, 512], BF16, ctx)
        ktr = [self.sbuf(f"p6_ktr{j}", [128, 128], BF16, ctx) for j in range(2)]
        pP = [self.psum(f"p6_p{j}", [128, 512], F32, ctx) for j in range(2)]
        pT = self.psum("p6_pT", [128, 128], BF16, ctx)
        ops = []
        W = lambda *names: [w for w in names if w is not None]
        for wi, (wsb, wnm) in enumerate([(wqT, "w_q1"), (wkT, "w_k1"),
                                         (wvT, "w_v1")]):
            ops.append(dict(eng="gp", name=f"wl{wi}",
                waits=W(f"wt{wi - 1}_3_7" if wi >= 1 else None),
                emit=lambda gp, wnm=wnm: gp.dma_start(
                    out=wsrc.rearrange("p (rc d) -> p rc d", rc=4),
                    in_=getattr(k, wnm).rearrange("(rc p) d -> p rc d", p=128))))
            for rc in range(4):
                for dc in range(8):
                    ops.append(dict(eng="t", name=f"wp{wi}_{rc}_{dc}",
                        waits=W(f"wl{wi}" if rc == 0 and dc == 0 else None,
                                f"wt{wi}_{rc}_{dc - 1}" if dc >= 1 else
                                (f"wt{wi}_{rc - 1}_7" if rc >= 1 else
                                 (f"wt{wi - 1}_3_7" if wi >= 1 else None))),
                        emit=lambda tn, rc=rc, dc=dc: tn.transpose(
                            pT[:], wsrc[:, rc * 1024 + dc * 128:
                                        rc * 1024 + (dc + 1) * 128], k.ident[:])))
                    ops.append(dict(eng="v", name=f"wt{wi}_{rc}_{dc}",
                        waits=[f"wp{wi}_{rc}_{dc}"],
                        emit=lambda v, wsb=wsb, rc=rc, dc=dc: v.tensor_copy(
                            wsb[:, dc * 512 + rc * 128:dc * 512 + (rc + 1) * 128],
                            pT[:])))
        nq = 0
        prev_kk = [None]
        prev_fe = [None]
        cons = {}
        for i in range(NT512):
            for dc in range(8):
                ops.append(dict(eng="sy", name=f"x{i}_{dc}",
                    waits=W((f"vst{i - 1}_3" if i >= 1 else "wt2_3_7")
                            if dc == 0 else None),
                    emit=lambda sy, i=i, dc=dc: sy.dma_start(
                        out=xf[:, dc * 512:(dc + 1) * 512],
                        in_=k.xln3_buf[i * 512:(i + 1) * 512,
                                       dc * 128:(dc + 1) * 128], transpose=True)))
            for which, wsb, buf in [("q", wqT, k.qp_buf), ("kk", wkT, k.kp_buf)]:
                for rc in range(4):
                    tgq = f"{which}{i}_{rc}"
                    g = nq % 2
                    for dc in range(8):
                        ops.append(dict(eng="t", name=f"mm{tgq}_{dc}",
                            waits=W((f"x{i}_7" if rc == 0 and which == "q" else None)
                                    if dc == 0 else None,
                                    (f"fm{nq - 2}" if nq >= 2 else None)
                                    if dc == 0 else None),
                            emit=lambda tn, wsb=wsb, rc=rc, dc=dc, g=g: tn.matmul(
                                pP[g][:],
                                wsb[:, dc * 512 + rc * 128:dc * 512 + (rc + 1) * 128],
                                xf[:, dc * 512:(dc + 1) * 512],
                                start=(dc == 0), stop=(dc == 7))))
                    ops.append(dict(eng="v", name=f"fa{nq}",
                        waits=W(f"mm{tgq}_7", prev_fe[0]),
                        emit=lambda v, g=g: v.tensor_scalar(
                            zmin[:], pP[g][:], 0.0, None, ALU.min)))
                    ops.append(dict(eng="s", name=f"fe{nq}", waits=[f"fa{nq}"],
                        emit=lambda sc: sc.activation(ex[:], zmin[:], AF.Exp,
                                                      bias=0.0, scale=1.0)))
                    prev_fe[0] = f"fe{nq}"
                    ops.append(dict(eng="v", name=f"fm{nq}",
                        waits=W(f"fe{nq}", cons.get(nq - 2)),
                        emit=lambda v, g=g, jf=nq % 2: v.scalar_tensor_tensor(
                            fmo[jf][:], pP[g][:], 0.0, ex[:], ALU.max, ALU.add)))
                    for hh in range(2):
                        ops.append(dict(eng="sy", name=f"fst{nq}_{hh}",
                            waits=W(f"fm{nq}" if hh == 0 else None),
                            emit=lambda sy, buf=buf, rc=rc, hh=hh, i=i, jf=nq % 2:
                            sy.dma_start(
                                out=buf[2 * rc + hh, :, i * 512:(i + 1) * 512],
                                in_=fmo[jf][64 * hh:64 * (hh + 1), :])))
                    cons[nq] = f"fst{nq}_1"
                    if which == "kk":
                        pk = prev_kk[0]
                        ops.append(dict(eng="v", name=f"kc{nq}",
                            waits=W(f"fm{nq}",
                                    f"ktp{pk}_3" if pk is not None else None),
                            emit=lambda v, jf=nq % 2: v.tensor_copy(
                                kpl[:], fmo[jf][:])))
                        for tt in range(4):
                            ops.append(dict(eng="t", name=f"ktp{nq}_{tt}",
                                waits=W(f"kc{nq}" if tt == 0 else None,
                                        f"kte{nq}_{tt - 1}" if tt >= 1 else
                                        (f"kte{pk}_3" if pk is not None else None)),
                                emit=lambda tn, tt=tt: tn.transpose(
                                    pT[:], kpl[:, tt * 128:(tt + 1) * 128],
                                    k.ident[:])))
                            ops.append(dict(eng="v", name=f"kte{nq}_{tt}",
                                waits=W(f"ktp{nq}_{tt}",
                                        f"kst{nq}_{tt - 1}_1" if tt >= 1 else
                                        (f"kst{pk}_3_1" if pk is not None else None)),
                                emit=lambda v, jt=tt % 2: v.tensor_copy(
                                    ktr[jt][:], pT[:])))
                            for hh in range(2):
                                ops.append(dict(eng="sy", name=f"kst{nq}_{tt}_{hh}",
                                    waits=W(f"kte{nq}_{tt}" if hh == 0 else None),
                                    emit=lambda sy, rc=rc, tt=tt, hh=hh, i=i,
                                           jt=tt % 2: sy.dma_start(
                                        out=k.kpt_buf[2 * rc + hh,
                                                      i * 512 + tt * 128:
                                                      i * 512 + (tt + 1) * 128, :],
                                        in_=ktr[jt][:, 64 * hh:64 * (hh + 1)])))
                        prev_kk[0] = nq
                    nq += 1
            for st in range(4):
                g = nq % 2
                for dc in range(8):
                    ops.append(dict(eng="t", name=f"vm{i}_{st}_{dc}",
                        waits=W((f"fm{nq - 2}" if nq >= 2 else None)
                                if dc == 0 else None),
                        emit=lambda tn, st=st, dc=dc, g=g: tn.matmul(
                            pP[g][:], xf[:, dc * 512 + st * 128:dc * 512 + (st + 1) * 128],
                            wvT[:, dc * 512:(dc + 1) * 512],
                            start=(dc == 0), stop=(dc == 7))))
                ops.append(dict(eng="v", name=f"fm{nq}",
                    waits=W(f"vm{i}_{st}_7", cons.get(nq - 2)),
                    emit=lambda v, g=g, jf=nq % 2: v.tensor_copy(fmo[jf][:],
                                                                 pP[g][:])))
                ops.append(dict(eng="sy", name=f"vst{i}_{st}",
                    waits=[f"fm{nq}"],
                    emit=lambda sy, i=i, st=st, jf=nq % 2: sy.dma_start(
                        out=k.v1_buf[i * 512 + st * 128:
                                     i * 512 + (st + 1) * 128, :],
                        in_=fmo[jf][:])))
                cons[nq] = f"vst{i}_{st}"
                nq += 1
        self.run_ops(ops, "p6")
        ctx.close()

    def _p6_gate(self, ctx):
        blk = self.block
        s_cc2 = self.s_cc2

        @blk.sync
        def _(sy):
            sy.wait_ge(s_cc2, 1)

    def p7_attn1(self):
        k = self
        blk = self.block
        ctx = contextlib.ExitStack()
        NCH = T // 128
        qpr = [self.sbuf(f"p7_qp{j}", [128, 8 * 128], BF16, ctx) for j in range(2)]
        kpr = [self.sbuf(f"p7_kp{j}", [128, 8 * 128], BF16, ctx) for j in range(2)]
        ktr = [self.sbuf(f"p7_kt{j}", [128, 8 * 128], BF16, ctx) for j in range(2)]
        vr = [self.sbuf(f"p7_vr{j}", [128, 8 * 128], BF16, ctx) for j in range(2)]
        Ssb = self.sbuf("p7_S", [128, 8 * 128], BF16, ctx)
        ksc = self.sbuf("p7_ksc", [128, 8], BF16, ctx)
        ksr = self.sbuf("p7_ksr", [128, 8 * 128], BF16, ctx)
        Pl = [self.sbuf(f"p7_pl{j}", [128, 128], BF16, ctx) for j in range(2)]
        dtmp = [self.sbuf(f"p7_dt{j}", [64, 128], F32, ctx) for j in range(2)]
        dinv = [self.sbuf(f"p7_di{j}", [64, 128], F32, ctx) for j in range(2)]
        ysb = [self.sbuf(f"p7_y{j}", [128, 4 * 128], BF16, ctx) for j in range(2)]
        scP = self.psum("p7_sc", [128, 2 * 128], F32, ctx)
        avP = self.psum("p7_av", [128, 8 * 128], F32, ctx)
        dnP = self.psum("p7_dn", [128, 8 * 128], F32, ctx)
        SP = self.psum("p7_SP", [128, 512], F32, ctx)
        kP = self.psum("p7_kP", [128, 8], F32, ctx)
        ops = []
        W = lambda *names: [w for w in names if w is not None]
        for j in range(2):
            for t_ in (qpr, kpr, ktr, vr):
                ops.append(dict(eng="gp", name=f"z{id(t_) % 997}_{j}", dma=False,
                    emit=lambda gp, t_=t_, j=j: gp.memset(t_[j][:], 0.0)))
        ops.append(dict(eng="gp", name="zS", dma=False,
                        emit=lambda gp: gp.memset(Ssb[:], 0.0)))
        ops.append(dict(eng="gp", name="zksr", dma=False,
                        emit=lambda gp: gp.memset(ksr[:], 0.0)))

        def loads(c, j):
            c0 = c * 128
            lw = W(f"ks{c - 2}_7" if c >= 2 else "zS",
                   "zksr" if c < 2 else None)
            ops.append(dict(eng="sy", name=f"lq{c}", waits=lw,
                emit=lambda sy, c0=c0, j=j: sy.dma_start(
                    out=AP(qpr[j], 0, [[1024, 64], [128, 8], [1, 128]]),
                    in_=AP(k.qp_buf.tensor, c0, [[T, 64], [64 * T, 8], [1, 128]]))))
            ops.append(dict(eng="sy", name=f"lk{c}", waits=[],
                emit=lambda sy, c0=c0, j=j: sy.dma_start(
                    out=AP(kpr[j], 0, [[1024, 64], [128, 8], [1, 128]]),
                    in_=AP(k.kp_buf.tensor, c0, [[T, 64], [64 * T, 8], [1, 128]]))))
            ops.append(dict(eng="sy", name=f"lt{c}", waits=[],
                emit=lambda sy, c0=c0, j=j: sy.dma_start(
                    out=AP(ktr[j], 0, [[1024, 128], [128, 8], [1, 64]]),
                    in_=AP(k.kpt_buf.tensor, c0 * 64,
                           [[64, 128], [T * 64, 8], [1, 64]]))))
            ops.append(dict(eng="sy", name=f"lv{c}", waits=[],
                emit=lambda sy, c0=c0, j=j: sy.dma_start(
                    out=AP(vr[j], 0, [[1024, 128], [128, 8], [1, 64]]),
                    in_=AP(k.v1_buf.tensor, c0 * 512,
                           [[512, 128], [64, 8], [1, 64]]))))

        loads(0, 0)
        loads(1, 1)
        for c in range(NCH):
            j = c % 2
            if c + 2 < NCH:
                loads(c + 2, j)
            for h in range(8):
                g = h % 2
                plprev = (f"pl{c}_{h - 2}" if h >= 2 else
                          (f"pl{c - 1}_{h + 6}" if c >= 1 else None))
                ops.append(dict(eng="t", name=f"sc{c}_{h}",
                    waits=W(f"lq{c}" if h == 0 else None,
                            f"lk{c}" if h == 0 else None, plprev),
                    emit=lambda tn, j=j, h=h, g=g: tn.matmul(
                        scP[:, g * 128:(g + 1) * 128],
                        kpr[j][:, h * 128:(h + 1) * 128],
                        qpr[j][:, h * 128:(h + 1) * 128],
                        start=True, stop=True, skip_group_check=True)))
                dnprev = (f"dn{c}_{h - 2}_0" if h >= 2 else
                          (f"dn{c - 1}_{h + 6}_0" if c >= 1 else None))
                ops.append(dict(eng="v", name=f"pl{c}_{h}",
                    waits=W(f"sc{c}_{h}", dnprev),
                    emit=lambda v, h=h, g=g, jp=h % 2: v.tensor_tensor(
                        Pl[jp][:], scP[:, g * 128:(g + 1) * 128],
                        k.causal01[:], ALU.mult)))
                ops.append(dict(eng="t", name=f"av{c}_{h}_0",
                    waits=W(f"pl{c}_{h}", f"lv{c}" if h == 0 else None,
                            f"yt{c - 1}_{h}" if c >= 1 else None),
                    emit=lambda tn, j=j, h=h, jp=h % 2: tn.matmul(
                        avP[:, h * 128:(h + 1) * 128],
                        vr[j][:, h * 128:(h + 1) * 128], Pl[jp][:],
                        start=True, stop=False, skip_group_check=True)))
                ops.append(dict(eng="t", name=f"av{c}_{h}_1",
                    waits=W(f"se{c - 1}" if c >= 1 else "zS" if h == 0 else None),
                    emit=lambda tn, j=j, h=h: tn.matmul(
                        avP[:, h * 128:(h + 1) * 128],
                        Ssb[:, h * 128:(h + 1) * 128],
                        qpr[j][:, h * 128:(h + 1) * 128],
                        start=False, stop=True, skip_group_check=True)))
                ops.append(dict(eng="t", name=f"dn{c}_{h}_0",
                    waits=W(f"pl{c}_{h}", f"dnv{c - 1}_{h}" if c >= 1 else None),
                    emit=lambda tn, h=h, jp=h % 2: tn.matmul(
                        dnP[:, h * 128:(h + 1) * 128], k.ones_bf[:], Pl[jp][:],
                        start=True, stop=False, skip_group_check=True)))
                ops.append(dict(eng="t", name=f"dn{c}_{h}_1",
                    waits=W(f"ke{c - 1}" if (c >= 1 and h == 0) else None,
                            f"kr{c - 1}" if (c >= 1 and h == 0) else None),
                    emit=lambda tn, j=j, h=h: tn.matmul(
                        dnP[:, h * 128:(h + 1) * 128],
                        ksr[:, h * 128:(h + 1) * 128],
                        qpr[j][:, h * 128:(h + 1) * 128],
                        start=False, stop=True, skip_group_check=True)))
                ops.append(dict(eng="t", name=f"st{c}_{h}",
                    waits=W(f"lt{c}" if h == 0 else None,
                            f"se{c - 1}" if (c >= 1 and h == 0) else None),
                    emit=lambda tn, j=j, h=h, st=(c == 0 and h == 0),
                           sp=(c == NCH - 1):
                    tn.matmul(
                        SP[:, h * 64:(h + 1) * 64],
                        ktr[j][:, h * 128:(h + 1) * 128],
                        vr[j][:, h * 128:h * 128 + 64],
                        start=st, stop=sp, skip_group_check=True)))
                ops.append(dict(eng="t", name=f"ks{c}_{h}",
                    waits=W(f"ke{c - 1}" if (c >= 1 and h == 0) else None),
                    emit=lambda tn, j=j, h=h, st=(c == 0 and h == 0),
                           sp=(c == NCH - 1):
                    tn.matmul(
                        kP[:, h:h + 1], ktr[j][:, h * 128:(h + 1) * 128],
                        k.ones_bf[:, 0:1],
                        start=st, stop=sp, skip_group_check=True)))
                drprev = (f"dnr{c}_{h - 2}" if h >= 2 else
                          (f"dnr{c - 1}_{h + 6}" if c >= 1 else None))
                ops.append(dict(eng="v", name=f"dnv{c}_{h}",
                    waits=W(f"dn{c}_{h}_1", drprev),
                    emit=lambda v, h=h, jp=h % 2: v.tensor_scalar(
                        dtmp[jp][:], dnP[0:64, h * 128:(h + 1) * 128], 1e-6,
                        None, ALU.add)))
                ytprev = (f"yt{c}_{h - 2}" if h >= 2 else
                          (f"yt{c - 1}_{h + 6}" if c >= 1 else None))
                ops.append(dict(eng="v", name=f"dnr{c}_{h}",
                    waits=W(f"dnv{c}_{h}", ytprev),
                    emit=lambda v, h=h, jp=h % 2: v.reciprocal(
                        dinv[jp][:], dtmp[jp][:])))
                ops.append(dict(eng="v", name=f"dni{c}_{h}", inc=1, dma=False,
                    waits=[f"dnr{c}_{h}"], emit=lambda v: v.engine_nop()))
                ops.append(dict(eng="v", name=f"yt{c}_{h}",
                    waits=W(f"dnr{c}_{h}", f"av{c}_{h}_1",
                            f"yst{c - 2}" if c >= 2 else None),
                    emit=lambda v, h=h, jp=h % 2, jy=c % 2: v.tensor_mul(
                        ysb[jy][64 * (h % 2):64 * (h % 2) + 64,
                                (h // 2) * 128:(h // 2 + 1) * 128],
                        avP[0:64, h * 128:(h + 1) * 128], dinv[jp][:])))
            if getattr(self, "dbg_p7", False) and c in (0, 1):
                if c == 0:
                    dbg7 = self.nc.dram_tensor("p7dbg", [6, 128, 1024], F32).ap()
                    self._p7dbg = dbg7
                else:
                    dbg7 = self._p7dbg
                base = 3 * c
                ops.append(dict(eng="sy", name=f"dbgpl{c}", waits=[f"pl{c}_0"],
                    emit=lambda sy, base=base: sy.dma_start(
                        out=dbg7[base + 0, :, 0:64].bitcast(BF16),
                        in_=Pl[0][:])))
                ops.append(dict(eng="sy", name=f"dbgdn{c}", waits=[f"dnr{c}_0"],
                    emit=lambda sy, base=base: sy.dma_start(
                        out=dbg7[base + 1, 0:64, 0:128], in_=dinv[0][:])))
                ops.append(dict(eng="sy", name=f"dbgy{c}",
                    waits=[f"yt{c}_7", f"yt{c}_6"],
                    emit=lambda sy, base=base, jy=c % 2: sy.dma_start(
                        out=dbg7[base + 2, :, 0:256].bitcast(BF16),
                        in_=ysb[jy][:])))
                ops.append(dict(eng="v", name=f"dbgg{c}", inc=1, dma=False,
                    waits=[f"dbgpl{c}", f"dbgdn{c}", f"dbgy{c}"],
                    emit=lambda v: v.engine_nop()))
            ops.append(dict(eng="v", name=f"se{c}",
                waits=W(f"st{c}_7", f"av{c}_7_1",
                        "dbgS" if (getattr(self, "dbg_p7", False) and c == 1)
                        else None,
                        "dbgS1" if (getattr(self, "dbg_p7", False) and c == 2)
                        else None,
                        "dbgK1" if (getattr(self, "dbg_p7", False) and c == 2)
                        else None),
                emit=lambda v: v.tensor_copy(
                    AP(Ssb, 0, [[1024, 128], [128, 8], [1, 64]]),
                    AP(SP, 0, [[512, 128], [64, 8], [1, 64]]))))
            if getattr(self, "dbg_p7", False) and c == 0:
                ops.append(dict(eng="sy", name="dbgS", waits=["se0"],
                    emit=lambda sy: sy.dma_start(
                        out=self._p7dbg[1, 64:128, 0:512].bitcast(BF16),
                        in_=Ssb[0:64, :])))
            if getattr(self, "dbg_p7", False) and c == 1:
                ops.append(dict(eng="sy", name="dbgS1", waits=["se1"],
                    emit=lambda sy: sy.dma_start(
                        out=self._p7dbg[4, 0:128, 0:512].bitcast(BF16),
                        in_=Ssb[:, :])))
                ops.append(dict(eng="sy", name="dbgK1", waits=["kr1"],
                    emit=lambda sy: sy.dma_start(
                        out=self._p7dbg[4, 0:128, 512:1024].bitcast(BF16)[:, 0:8],
                        in_=ksc[:, :])))
            ops.append(dict(eng="v", name=f"ke{c}",
                waits=W(f"ks{c}_7", f"dn{c}_7_1"),
                emit=lambda v: v.tensor_copy(ksc[:], kP[:])))
            ops.append(dict(eng="v", name=f"kr{c}", waits=[f"ke{c}"],
                emit=lambda v: v.tensor_copy(
                    AP(ksr, 0, [[1024, 128], [128, 8], [1, 128]]),
                    AP(ksc, 0, [[8, 128], [1, 8], [0, 128]]))))
            ops.append(dict(eng="sy", name=f"yst{c}",
                waits=W(f"yt{c}_7", f"yt{c}_6",
                        f"dbgg{c}" if (getattr(self, "dbg_p7", False) and
                                      c in (0, 1)) else None),
                emit=lambda sy, c=c, jy=c % 2: sy.dma_start(
                    out=k.cc3_in.rearrange("(rc p) t -> p rc t", p=128)[
                        :, :, c * 128:(c + 1) * 128],
                    in_=ysb[jy].rearrange("p (rc t) -> p rc t", rc=4))))
        sems7, cnt7, _ = self.run_ops(ops, "p7")
        self._p7_sy_sem = sems7["sy"]
        self._p7_sy_tgt = cnt7["sy"]
        self.s_cc3 = self.sem("cc3")
        s_cc3 = self.s_cc3
        k2 = self
        nst = NCH

        @blk.sync
        def _(sy):
            pass

        @blk.gpsimd
        def _(gp):
            gp.wait_ge(self._p7_sy_sem, self._p7_sy_tgt)
            gp.collective_compute(
                "AllGather", ALU.bypass,
                replica_groups=[[0, 1], [2, 3], [4, 5], [6, 7]],
                ins=[k2.cc3_in.opt()], outs=[k2.cc3_out.opt()],
            ).then_inc(s_cc3, 1)
        ctx.close()

    def p8_wp_res(self):
        self._proj_ln_res(self.w_p1, self.b_p1, self.cc3_out, self.s_cc3,
                          self.x2_buf, self.x3_buf, self.xln4_buf, "p8")



def _stage_inputs(inputs, core):
    b, half = core // 2, core % 2
    hsl0 = slice((core % 2) * H0C * HD0, ((core % 2) + 1) * H0C * HD0)
    hsl1 = slice((core % 2) * H1C * HD1, ((core % 2) + 1) * H1C * HD1)
    x = np.ascontiguousarray(inputs["L0_attn_x"][b])
    d = {
        "x_full": x,
        "x_own": np.ascontiguousarray(x[half * TOWN:(half + 1) * TOWN]),
        "w_q0": np.ascontiguousarray(inputs["L0_attn_Wq"][:, hsl0]),
        "w_k0": np.ascontiguousarray(inputs["L0_attn_Wk"][:, hsl0]),
        "w_v0": np.ascontiguousarray(inputs["L0_attn_Wv"][:, hsl0]),
        "w_o0": inputs["L0_attn_Wo"],
        "b_o0": inputs["L0_attn_bo"],
        "theta": inputs["L0_attn_theta"],
        "gn_gamma": np.ascontiguousarray(inputs["L0_attn_gamma"][hsl0]),
        "gn_beta": np.ascontiguousarray(inputs["L0_attn_beta"][hsl0]),
        "f_qw": inputs["L0_ffn_qweights"],
        "f_sc": inputs["L0_ffn_scales"],
        "f_bi": inputs["L0_ffn_biases"],
        "f_la": inputs["L0_ffn_lora_a"],
        "f_lb": inputs["L0_ffn_lora_b"],
        "f_lg": inputs["L0_ffn_ln_gamma"],
        "f_lbe": inputs["L0_ffn_ln_beta"],
        "w_q1": np.ascontiguousarray(inputs["L1_attn_Wq"][hsl1, :]),
        "w_k1": np.ascontiguousarray(inputs["L1_attn_Wk"][hsl1, :]),
        "w_v1": np.ascontiguousarray(inputs["L1_attn_Wv"][hsl1, :]),
        "w_p1": np.ascontiguousarray(inputs["L1_attn_Wp"].T),
        "b_p1": inputs["L1_attn_bp"],
        "m_gw": inputs["L1_ffn_gate_w"],
        "m_w1": np.concatenate([inputs["L1_ffn_w1"], inputs["L1_ffn_sw1"]], 0),
        "m_w2": np.concatenate([inputs["L1_ffn_w2"], inputs["L1_ffn_sw2"]], 0),
        "m_w3": np.concatenate([inputs["L1_ffn_w3"], inputs["L1_ffn_sw3"]], 0),
    }
    return d


def kernel(**inputs):
    kb = KB()
    nc = kb.build()
    in_maps = [_stage_inputs(inputs, c) for c in range(8)]
    res = run_bass_kernel_spmd(nc, in_maps, core_ids=list(range(8)))
    out = np.empty((B, T, D), np.float32)
    for c in range(8):
        b, half = c // 2, c % 2
        out[b, half * TOWN:(half + 1) * TOWN] = res.results[c]["out"]
    return out
